# version 09
# baseline (speedup 1.0000x reference)
import contextlib
import numpy as np
import ml_dtypes
import concourse.bass as bass
import concourse.mybir as mybir
from concourse.bass_utils import run_bass_kernel_spmd

F32 = mybir.dt.float32
F32R = mybir.dt.float32r
BF16 = mybir.dt.bfloat16
AF = mybir.ActivationFunctionType
OP = mybir.AluOpType
AP = bass.AP
AX = mybir.AxisListType.X

B_, SEQ, D = 4, 2048, 4096
T = 1024
H, PD, NS, G, KC, CS = 128, 64, 128, 8, 4, 256
NCHUNK = T // CS
INTER = H * PD
CONVC = INTER + 2 * G * NS
PROJ = INTER + CONVC + H
FCH = PROJ // 128
NXBC, NGATE = 80, 64
RANK, NEXP = 32, 8
GS = INTER // G
ENGINES = ("tensor", "vector", "scalar", "gpsimd", "sync")


class Sem:
    def __init__(self, handle):
        self.h = handle
        self.n = 0

    def inc(self, amt=1):
        self.n += amt
        return self.n

    @property
    def now(self):
        return self.n


class KB:

    def __init__(self):
        self.nc = bass.Bass("TRN2", target_bir_lowering=False)
        self.ops = {e: [] for e in ENGINES}
        self.dr = {}
        self.chain = {}
        self.guard = None
        self.guard_pending = set()

    def set_chain_sems(self, s_dve, s_act, s_pool):
        self.chain = {"vector": s_dve, "scalar": s_act, "gpsimd": s_pool}

    def scope_barrier(self, sems):
        cur = [(s, s.n) for s in sems if s.n > 0]
        if self.guard:
            old = dict((id(s), (s, v)) for s, v in self.guard)
            for s, v in cur:
                old[id(s)] = (s, v)
            cur = list(old.values())
        self.guard = cur
        self.guard_pending = set(ENGINES)

    def op(self, eng, fn, waits=(), inc=None, inc_amt=1, chain=True):
        allw = [(s, v) for (s, v) in waits if v > 0]
        if self.guard and eng in self.guard_pending:
            self.guard_pending.discard(eng)
            allw = list(self.guard) + allw
        ret = None
        cs = self.chain.get(eng) if chain else None
        if cs is not None and inc is None:
            if cs.n > 0:
                allw.append((cs, cs.n))
            ret = cs.inc(1)
            inc_h, inc_amt = cs.h, 1
        elif inc is not None:
            if cs is not None and cs.n > 0:
                allw.append((cs, cs.n))
            inc.inc(inc_amt)
            inc_h = inc.h
            ret = inc.now
        else:
            inc_h = None
        allw = tuple((s.h, v) for (s, v) in allw)

        def run(e, fn=fn, waits=allw, inc_h=inc_h, inc_amt=inc_amt):
            for hh, v in waits:
                e.wait_ge(hh, v)
            r = fn(e)
            if inc_h is not None:
                r.then_inc(inc_h, inc_amt)
        self.ops[eng].append(run)
        return ret

    def wait(self, eng, sem, val):
        if val > 0:
            self.ops[eng].append(lambda e, hh=sem.h, v=val: e.wait_ge(hh, v))

    def dma(self, eng, out_ap, in_ap, sem, waits=(), **kw):
        return self.op(eng, lambda e, o=out_ap, i=in_ap, kw=kw:
                       e.dma_start(out=o, in_=i, **kw),
                       waits=waits, inc=sem, inc_amt=16, chain=False)

    def din(self, name, shape, dt=F32):
        ap = self.nc.dram_tensor(name, shape, dt, kind="ExternalInput").ap()
        self.dr[name] = ap
        return ap

    def dout(self, name, shape, dt=F32):
        ap = self.nc.dram_tensor(name, shape, dt, kind="ExternalOutput").ap()
        self.dr[name] = ap
        return ap

    def dint(self, name, shape, dt):
        ap = self.nc.dram_tensor(name, shape, dt).ap()
        self.dr[name] = ap
        return ap


def _ap(handle, offset, dims):
    return AP(handle, offset, dims)


def build_program(debug=()):
    kb = KB()
    nc = kb.nc
    debug = set(debug)
    dbg = {}

    x_in = kb.din("x", [T, D])
    w1 = kb.din("w1", [FCH, 128, 32 * 128])
    hw2 = kb.din("hw2", [32, 128, 32 * 128])
    how = kb.din("how", [32, 128, NGATE * 128])
    affn = kb.din("affn", [32, 128, RANK])
    w8 = kb.din("w8", [32, 128, NEXP])
    bmat = kb.din("bmat", [RANK, D])
    limes = kb.din("limes", [NEXP, RANK])
    bffn = kb.din("bffn", [128, 32])
    convw = kb.din("convw", [128, NXBC * KC])
    convb = kb.din("convb", [128, NXBC])
    dtbias = kb.din("dtb", [128, 1])
    alog = kb.din("alog", [128, 1])
    dgrp = kb.din("dgrp", [128, NGATE])
    nwgrp = kb.din("nwgrp", [128, NGATE])
    idn_d = kb.din("idn", [128, 128])
    maskp_d = kb.din("maskp", [128, 2 * CS])
    ish0 = kb.din("ish0", [128, 1])
    ish1 = kb.din("ish1", [128, 1])

    out_d = kb.dout("out", [D, T], F32)

    def dtap(name, shape, dt=F32):
        if name in debug:
            dbg[name] = kb.dout("dbg_" + name, shape, dt)
            return dbg[name]
        return None

    xrawsp = kb.dint("xrawsp", [D, T], BF16)
    xbcsp = kb.dint("xbcsp", [NXBC, 128, T], BF16)
    gatesp = kb.dint("gatesp", [NGATE, 128, T], BF16)
    xtoksp = kb.dint("xtoksp", [NCHUNK, 2, 128, INTER], BF16)
    hdsp = kb.dint("hdsp", [8, D, 128], BF16)
    yrawsp = kb.dint("yrawsp", [NGATE, 128, T], F32)
    ygsp = kb.dint("ygsp", [NGATE, 128, T], BF16)
    cc1_in = kb.dint("cc1_in", [1, 16], F32)
    cc1_out = kb.dint("cc1_out", [1, 16], F32)
    cc2a_in = kb.dint("cc2a_in", [128, NXBC * 3], F32)
    cc2a_out = kb.dint("cc2a_out", [128, NXBC * 3], F32)
    cc2b_in = kb.dint("cc2b_in", [128, INTER], F32)
    cc2b_out = kb.dint("cc2b_out", [128, INTER], F32)

    es = contextlib.ExitStack()

    def sb(name, pdim, fdim, dt):
        return es.enter_context(nc.sbuf_tensor(name, [pdim, fdim], dt))

    def sem(name):
        return Sem(es.enter_context(nc.semaphore(name)))

    with es:
        banks = [es.enter_context(nc.psum_tensor(f"bank{i}", [128, 512], F32))
                 for i in range(8)]
        banksb = [b.ap().bitcast(BF16) for b in banks]

        s_ld = sem("s_ld")
        s_gd = sem("s_gd")
        s_pe = sem("s_pe")
        s_dve = sem("s_dve")
        s_act = sem("s_act")
        s_pool = sem("s_pool")
        s_sp = sem("s_sp")
        s_cc = sem("s_cc")
        s_cc2a = sem("s_cc2a")
        s_cc2b = sem("s_cc2b")
        s_xt2 = [sem("s_xt0"), sem("s_xt1")]
        s_xrs2 = [sem("s_xrs0"), sem("s_xrs1")]
        s_w2 = [sem("s_w0"), sem("s_w1")]
        s_hdsp2 = [sem("s_hdsp0"), sem("s_hdsp1")]
        s_spst2 = [sem("s_spst0"), sem("s_spst1")]
        s_cdf = sem("s_cdf")
        s_hsb2 = [sem("s_hsb0"), sem("s_hsb1")]
        s_bfm = sem("s_bfm")
        s_xtsp = sem("s_xtsp")
        s_slsp = sem("s_slsp")
        s_x42 = [sem("s_x40"), sem("s_x41")]
        s_bc = sem("s_bc")
        s_slbr = sem("s_slbr")
        s_hs4r = sem("s_hs4r")
        s_gt4r = sem("s_gt4r")
        s_fcb2 = [sem("s_fcb0"), sem("s_fcb1")]
        s_ygsp = sem("s_ygsp")
        s_yg5 = sem("s_yg5")
        s_xrw = sem("s_xrw")
        s_out = sem("s_out")
        kb.set_chain_sems(s_dve, s_act, s_pool)
        ALLS = [s_ld, s_gd, s_pe, s_dve, s_act, s_pool, s_sp,
                s_xt2[0], s_xt2[1], s_xrs2[0], s_xrs2[1], s_w2[0], s_w2[1],
                s_hdsp2[0], s_hdsp2[1], s_spst2[0], s_spst2[1], s_cdf,
                s_hsb2[0], s_hsb2[1], s_bfm, s_xtsp, s_slsp, s_x42[0], s_x42[1],
                s_bc, s_slbr, s_hs4r, s_gt4r, s_fcb2[0], s_fcb2[1], s_ygsp]

        IDN = sb("IDN", 128, 128, F32)
        IDNB = sb("IDNB", 128, 128, BF16)
        MASKP = sb("MASKP", 128, 2 * CS, BF16)
        CONVW = sb("CONVW", 128, NXBC * KC, F32)
        CONVB = sb("CONVB", 128, NXBC, F32)
        DTB = sb("DTB", 128, 1, F32)
        ANEG = sb("ANEG", 128, 1, F32)
        DGRP = sb("DGRP", 128, NGATE, F32)
        NWGRP = sb("NWGRP", 128, NGATE, F32)
        BFFN = sb("BFFN", 128, 32, F32)
        ONESB = sb("ONESB", 4, 128, BF16)
        ONESF = sb("ONESF", 1, 128, F32)
        ISH0 = sb("ISH0", 128, 1, F32)
        ISH1 = sb("ISH1", 128, 1, F32)
        BMATB = sb("BMATB", RANK, D, BF16)
        LIMESB = sb("LIMESB", NEXP, RANK, BF16)
        STT = sb("STT", 128, 8, F32)
        SCR2 = sb("SCR2", 128, T, F32)
        ACC = sb("ACC", 128, T, F32)
        ONES128 = sb("ONES128", 128, CS, F32)
        CR3 = sb("CR3", 128, NXBC * 3, F32)
        HALOS = sb("HALOS", 128, NXBC * 3, F32)
        XBCFIX = sb("XBCFIX", 128, NXBC * 3, BF16)
        ZPMT = sb("ZPMT", RANK, T, BF16)
        scope_dt = contextlib.ExitStack()
        _pdt = lambda name, p, f, dt: scope_dt.enter_context(
            nc.sbuf_tensor(name, [p, f], dt))
        DT = _pdt("DT", 128, T, F32)
        CUM = _pdt("CUM", 128, T, F32)
        LCT = _pdt("LCT", 128, 8 * 128, F32)
        WDTT = _pdt("WDTT", 128, 8 * 128, BF16)
        Q1 = _pdt("Q1", 128, T, BF16)
        Q2 = _pdt("Q2", 128, T, BF16)
        R2 = _pdt("R2", 128, T, BF16)
        ECUM = _pdt("ECUM", 128, T, BF16)
        PBF = _pdt("PBF", 1, 4 * 128, F32)

        kb.dma("sync", IDN[:, :], idn_d, s_ld)
        kb.dma("sync", CONVW[:, :], convw, s_ld)
        kb.dma("sync", CONVB[:, :], convb, s_ld)
        kb.dma("sync", DTB[:, :], dtbias, s_ld)
        kb.dma("sync", DGRP[:, :], dgrp, s_ld)
        kb.dma("sync", NWGRP[:, :], nwgrp, s_ld)
        kb.dma("sync", BFFN[:, :], bffn, s_ld)
        kb.dma("sync", ISH0[:, :], ish0, s_ld)
        kb.dma("sync", ISH1[:, :], ish1, s_ld)
        kb.dma("gpsimd", MASKP[:, :], maskp_d, s_gd)
        v_alog = kb.dma("sync", ANEG[:, :], alog, s_ld)
        kb.dma("gpsimd", BMATB[:, :], bmat, s_gd)
        kb.dma("gpsimd", LIMESB[:, :], limes, s_gd)
        v_idnb = kb.dma("gpsimd", IDNB[:, :], idn_d, s_gd)
        v_consts = s_ld.now

        kb.op("vector", lambda e: e.memset(ONESB[:, :], 1.0))
        v_ones = kb.op("vector", lambda e: e.memset(ONESF[:, :], 1.0), inc=s_dve)
        kb.op("scalar", lambda e: e.activation(ANEG[:, :], ANEG[:, :], AF.Exp),
              waits=[(s_ld, v_consts)])
        v_aneg = kb.op("scalar", lambda e: e.mul(ANEG[:, :], ANEG[:, :], -1.0),
                       inc=s_act)

        scope_xn = contextlib.ExitStack()
        XN = scope_xn.enter_context(nc.sbuf_tensor("XN", [128, 32 * T], BF16))
        scope_p0 = contextlib.ExitStack()
        XT = [scope_p0.enter_context(nc.sbuf_tensor(f"XT{i}", [128, D], F32))
              for i in range(2)]
        XTN = [scope_p0.enter_context(nc.sbuf_tensor(f"XTN{i}", [128, D], BF16))
               for i in range(2)]
        SQS = scope_p0.enter_context(nc.sbuf_tensor("SQS", [128, D], F32))

        XRB = [scope_p0.enter_context(nc.sbuf_tensor(f"XRB{i}", [128, 512], BF16))
               for i in range(2)]
        xr_bank_free = [0, 0]
        xt_free_v = [0, 0]
        xt_free_a = [0, 0]
        xt_free_pe = [0, 0]
        xtn_free = [0, 0]
        sqs_free = 0
        bank_free = {6: 0, 7: 0}
        v_xn = 0

        for i in range(8):
            b = i % 2
            v_x = kb.dma("sync", XT[b][:, :], x_in[i * 128:(i + 1) * 128, :], s_xt2[b],
                         waits=[(s_dve, xt_free_v[b]), (s_act, xt_free_a[b]),
                                (s_pe, xt_free_pe[b])])
            v_ms = kb.op("vector", lambda e, b=b: e.reduce_sum(STT[:, 0:1], XT[b][:, :], AX),
                         waits=[(s_xt2[b], v_x)], inc=s_dve)
            v_sq = kb.op("scalar", lambda e, b=b: e.activation(SQS[:, :], XT[b][:, :],
                                                               AF.Square),
                         waits=[(s_xt2[b], v_x), (s_dve, sqs_free)], inc=s_act)
            xt_free_a[b] = v_sq
            v_ss = kb.op("vector", lambda e: e.reduce_sum(STT[:, 1:2], SQS[:, :], AX),
                         waits=[(s_act, v_sq)], inc=s_dve)
            sqs_free = v_ss
            kb.op("vector", lambda e: e.tensor_scalar_mul(STT[:, 2:3], STT[:, 0:1], 1.0 / D))
            kb.op("vector", lambda e: e.tensor_mul(STT[:, 3:4], STT[:, 2:3], STT[:, 2:3]))
            kb.op("vector", lambda e: e.tensor_scalar(STT[:, 4:5], STT[:, 1:2], 1.0 / D,
                                                      1e-5, OP.mult, OP.add))
            v_var = kb.op("vector", lambda e: e.tensor_sub(STT[:, 4:5], STT[:, 4:5],
                                                           STT[:, 3:4]), inc=s_dve)
            v_sd = kb.op("scalar", lambda e: e.activation(STT[:, 5:6], STT[:, 4:5], AF.Sqrt),
                         waits=[(s_dve, v_var)], inc=s_act)
            kb.wait("vector", s_act, v_sd)
            kb.op("vector", lambda e: e.reciprocal(STT[:, 6:7], STT[:, 5:6]))
            v_nrm = kb.op("vector",
                          lambda e, b=b: e.tensor_scalar(XTN[b][:, :], XT[b][:, :],
                                                         STT[:, 2:3], STT[:, 6:7],
                                                         OP.subtract, OP.mult),
                          waits=[(s_pe, xtn_free[b])], inc=s_dve)
            xt_free_v[b] = v_nrm
            for gg in range(4):
                pb = 6 + (gg % 2)
                bk = banksb[pb]
                for k in range(8):
                    dc = gg * 8 + k
                    w = []
                    if k == 0:
                        w = [(s_dve, v_nrm), (s_gd, v_idnb), (s_dve, bank_free[pb])]
                    v_tp = kb.op("tensor",
                                 lambda e, bk=bk, k=k, dc=dc, b=b, st=(k == 0), sp=(k == 7):
                                 e.matmul(bk[:, k * 128:(k + 1) * 128],
                                          XTN[b][:, dc * 128:(dc + 1) * 128],
                                          IDNB[:, :], is_transpose=True,
                                          start=st, stop=sp),
                                 waits=w, inc=(s_pe if k == 7 else None))
                if gg == 3:
                    xtn_free[b] = v_tp
                xn_out = _ap(XN, gg * 8 * T + i * 128, [[32 * T, 128], [T, 8], [1, 128]])
                bank_free[pb] = kb.op("vector",
                                      lambda e, bk=bk, xo=xn_out:
                                      e.tensor_copy(xo, bk[:, :]),
                                      waits=[(s_pe, v_tp)], inc=s_dve)
                v_xn = bank_free[pb]
            for gg in range(8):
                pb4 = 4 + (gg % 2)
                for k in range(4):
                    dc = gg * 4 + k
                    w = [(s_xt2[b], v_x)]
                    if k == 0:
                        w.append((s_xrs2[pb4 - 4], xr_bank_free[pb4 - 4]))
                    v_tp4 = kb.op("tensor",
                                  lambda e, pb4=pb4, k=k, dc=dc, b=b,
                                  st=(k == 0), sp=(k == 3):
                                  e.matmul(banks[pb4][:, k * 128:(k + 1) * 128],
                                           XT[b][:, dc * 128:(dc + 1) * 128],
                                           IDN[:, :], is_transpose=True,
                                           start=st, stop=sp),
                                  waits=w, inc=(s_pe if k == 3 else None))
                xrb = gg % 2
                v_xre = kb.op("vector",
                              lambda e, pb4=pb4, xrb=xrb:
                              e.tensor_copy(XRB[xrb][:, :], banks[pb4][:, 0:512]),
                              waits=[(s_pe, v_tp4)])
                xr_bank_free[pb4 - 4] = kb.dma(
                    "sync",
                    _ap(xrawsp.tensor, gg * 4 * 128 * T + i * 128,
                        [[T, 128], [128 * T, 4], [1, 128]]),
                    XRB[xrb][:, :], s_xrs2[xrb], waits=[(s_dve, v_xre)])
            xt_free_pe[b] = v_tp4

        p0_dve_end = s_dve.now
        tp_xn = dtap("xn", [128, 32 * T], BF16)
        if tp_xn is not None:
            kb.dma("sync", tp_xn, _ap(XN, 0, [[32 * T, 128], [1, 32 * T]]), s_sp,
                   waits=[(s_dve, v_xn)])
        scope_p0.close()
        kb.scope_barrier([s_dve, s_act, s_pe, s_ld])

        scope_p1 = contextlib.ExitStack()
        _p1 = lambda name, p, f, dt: scope_p1.enter_context(
            nc.sbuf_tensor(name, [p, f], dt))
        AZ = _p1("AZ", 128, 32 * RANK, BF16)
        W8B = _p1("W8B", 128, 32 * NEXP, BF16)
        ZT = _p1("ZT", RANK, T, F32)
        ZTB = _p1("ZTB", RANK, T, BF16)
        H8T = _p1("H8T", NEXP, T, F32)
        DST = _p1("DST", NEXP, T // 2, F32)
        SCL = _p1("SCL", 1, 16, F32)
        GSCL = _p1("GSCL", 1, 16, F32)

        v_az = kb.dma("gpsimd", AZ[:, :], _ap(affn.tensor, 0, [[RANK, 128], [128 * RANK, 32], [1, RANK]]),
                      s_gd)
        v_w8 = kb.dma("gpsimd", W8B[:, :], _ap(w8.tensor, 0, [[NEXP, 128], [128 * NEXP, 32], [1, NEXP]]),
                      s_gd)

        for tt in range(2):
            for dc in range(32):
                kb.op("tensor",
                      lambda e, tt=tt, dc=dc:
                      e.matmul(banks[tt][0:RANK, :],
                               AZ[:, dc * RANK:(dc + 1) * RANK],
                               XN[:, dc * T + tt * 512: dc * T + (tt + 1) * 512],
                               start=(dc == 0), stop=(dc == 31)),
                      waits=([(s_gd, v_w8), (s_dve, v_xn)] if dc == 0 else ()),
                      inc=(s_pe if dc == 31 else None))
            v_z = s_pe.now
            kb.op("vector", lambda e, tt=tt: e.tensor_copy(ZT[:, tt * 512:(tt + 1) * 512],
                                                           banks[tt][0:RANK, :]),
                  waits=[(s_pe, v_z)])
            v_ztb = kb.op("vector", lambda e, tt=tt: e.tensor_copy(ZTB[:, tt * 512:(tt + 1) * 512],
                                                                   banks[tt][0:RANK, :]),
                          inc=s_dve)
        v_zt = v_ztb

        for tt in range(2):
            for dc in range(32):
                kb.op("tensor",
                      lambda e, tt=tt, dc=dc:
                      e.matmul(banks[2 + tt][0:NEXP, :],
                               W8B[:, dc * NEXP:(dc + 1) * NEXP],
                               XN[:, dc * T + tt * 512: dc * T + (tt + 1) * 512],
                               start=(dc == 0), stop=(dc == 31)),
                      waits=([(s_gd, v_w8)] if dc == 0 else ()),
                      inc=(s_pe if dc == 31 else None))
            v_h8 = s_pe.now
            kb.op("vector",
                  lambda e, tt=tt: e.tensor_scalar(H8T[:, tt * 512:(tt + 1) * 512],
                                                   banks[2 + tt][0:NEXP, :],
                                                   BFFN[0:NEXP, 0:1], None, OP.add),
                  waits=[(s_pe, v_h8), (s_ld, v_consts)], inc=s_dve)
        v_h8t = s_dve.now

        LIMT = _p1("LIMT", RANK, NEXP, BF16)
        kb.op("tensor", lambda e: e.matmul(banksb[4][0:RANK, 0:NEXP], LIMESB[:, :],
                                           IDNB[0:NEXP, 0:NEXP], is_transpose=True,
                                           start=True, stop=True),
              waits=[(s_gd, v_idnb), (s_dve, p0_dve_end)], inc=s_pe)
        v_limt_pe = s_pe.now
        v_limt = kb.op("vector", lambda e: e.tensor_copy(LIMT[:, :], banksb[4][0:RANK, 0:NEXP]),
                       waits=[(s_pe, v_limt_pe)], inc=s_dve)
        zanch = _ap(ZTB, 1, [[T, RANK], [2, T // 2]])
        kb.op("tensor", lambda e: e.matmul(banks[5][0:NEXP, :], LIMT[:, :], zanch,
                                           start=True, stop=True),
              waits=[(s_dve, max(v_limt, v_zt))], inc=s_pe)
        v_ds_pe = s_pe.now
        v_ds = kb.op("vector", lambda e: e.tensor_copy(DST[:, :], banks[5][0:NEXP, :]),
                     waits=[(s_pe, v_ds_pe)], inc=s_dve)

        h8anch = _ap(H8T, 1, [[T, NEXP], [2, T // 2]])
        kb.wait("vector", s_dve, v_h8t)
        kb.op("vector", lambda e: e.reduce_max(STT[0:NEXP, 0:1], h8anch, AX,
                                               apply_absolute_value=True))
        v_lm = kb.op("vector", lambda e: e.reduce_max(STT[0:NEXP, 1:2], DST[:, :], AX,
                                                      apply_absolute_value=True),
                     inc=s_dve)
        SCRO = _p1("SCRO", 1, 16, F32)
        v_t = kb.dma("gpsimd", SCRO[0:1, :],
                     _ap(STT, 0, [[8, 8], [1, 2]]), s_gd,
                     waits=[(s_dve, v_lm)])
        kb.wait("vector", s_gd, v_t)
        kb.op("vector", lambda e: e.memset(SCL[:, :], 0.0))
        kb.op("vector", lambda e: e.reduce_max(SCL[0:1, 0:1],
                                               _ap(SCRO, 0, [[16, 1], [2, 8]]), AX))
        v_scl = kb.op("vector", lambda e: e.reduce_max(SCL[0:1, 1:2],
                                                       _ap(SCRO, 1, [[16, 1], [2, 8]]), AX),
                      inc=s_dve)
        v_c1w = kb.dma("gpsimd", cc1_in, SCL[0:1, :], s_gd, waits=[(s_dve, v_scl)])
        kb.op("gpsimd", lambda e: e.collective_compute(
            "AllReduce", OP.max, replica_groups=[list(range(8))],
            ins=[cc1_in.opt()], outs=[cc1_out.opt()]),
            waits=[(s_gd, v_c1w)], inc=s_cc)
        v_c1 = s_cc.now
        kb.wait("gpsimd", s_cc, v_c1)
        v_gscl = kb.dma("gpsimd", GSCL[0:1, :], cc1_out, s_gd)

        tp_sc = dtap("scales", [1, 16])
        if tp_sc is not None:
            kb.dma("sync", tp_sc, GSCL[0:1, :], s_sp, waits=[(s_gd, v_gscl)])

        kb.wait("vector", s_gd, v_gscl)
        kb.op("vector", lambda e: e.tensor_scalar_max(GSCL[0:1, 0:2], GSCL[0:1, 0:2], 1e-6))
        kb.op("vector", lambda e: e.reciprocal(GSCL[0:1, 2:4], GSCL[0:1, 0:2]))
        v_rr = kb.op("vector", lambda e: e.tensor_scalar_mul(GSCL[0:1, 2:4], GSCL[0:1, 2:4],
                                                             0.5), inc=s_dve)
        kb.op("tensor", lambda e: e.matmul(banks[0][0:NEXP, 0:2],
                                           ONESF[0:1, 0:NEXP], GSCL[0:1, 2:4],
                                           start=True, stop=True),
              waits=[(s_dve, max(v_rr, v_ones))], inc=s_pe)
        v_rb_pe = s_pe.now
        RSC = _p1("RSC", NEXP, 2, F32)
        v_rsc = kb.op("vector", lambda e: e.tensor_copy(RSC[:, :], banks[0][0:NEXP, 0:2]),
                      waits=[(s_pe, v_rb_pe)], inc=s_dve)

        LGT = _p1("LGT", NEXP, T // 2, F32)
        kb.op("vector", lambda e: e.tensor_scalar_mul(LGT[:, :], h8anch, RSC[:, 0:1]))
        v_lg = kb.op("vector", lambda e: e.scalar_tensor_tensor(LGT[:, :], DST[:, :],
                                                                RSC[:, 1:2], LGT[:, :],
                                                                OP.mult, OP.add),
                     inc=s_dve)
        for a in range(4):
            kb.op("tensor", lambda e, a=a: e.matmul(banks[1][:, a * 8:(a + 1) * 8],
                                                    LGT[:, a * 128:(a + 1) * 128],
                                                    IDN[0:NEXP, 0:NEXP], is_transpose=True,
                                                    start=(a == 0), stop=(a == 3)),
                  waits=([(s_dve, v_lg), (s_ld, v_consts)] if a == 0 else ()),
                  inc=(s_pe if a == 3 else None))
        v_lgt = s_pe.now

        PRB = _p1("PRB", 128, 4 * NEXP, F32)
        WTK = _p1("WTK", 128, 4 * NEXP, F32)
        GST = _p1("GST", 128, 12, F32)
        kb.wait("vector", s_pe, v_lgt)
        kb.wait("scalar", s_pe, v_lgt)
        for a in range(4):
            pa = banks[1][:, a * 8:(a + 1) * 8]
            prb = PRB[:, a * 8:(a + 1) * 8]
            wtk = WTK[:, a * 8:(a + 1) * 8]
            v_mx = kb.op("vector", lambda e, pa=pa: e.reduce_max(GST[:, 0:1], pa, AX,
                                                                 negate=True), inc=s_dve)
            v_ex = kb.op("scalar", lambda e, pa=pa, prb=prb:
                         e.activation(prb, pa, AF.Exp, bias=GST[:, 0:1],
                                      accum_out=GST[:, 1:2]),
                         waits=[(s_dve, v_mx)], inc=s_act)
            kb.wait("vector", s_act, v_ex)
            kb.op("vector", lambda e: e.reciprocal(GST[:, 2:3], GST[:, 1:2]))
            kb.op("vector", lambda e, prb=prb: e.tensor_scalar_mul(prb, prb, GST[:, 2:3]))
            kb.op("vector", lambda e, prb=prb: e.reduce_max(GST[:, 3:4], prb, AX))
            kb.op("vector", lambda e, prb=prb: e.tensor_scalar(GST[:, 4:12][:, 0:8], prb,
                                                               GST[:, 3:4], None, OP.is_ge))
            kb.op("vector", lambda e: e.tensor_scalar(SCR2[:, 0:8], GST[:, 4:12], 1.0, 0.0,
                                                      OP.mult, OP.add,
                                                      accum_out=GST[:, 2:3]))
            kb.op("vector", lambda e, prb=prb:
                  e.scalar_tensor_tensor(SCR2[:, 8:16], GST[:, 4:12], -1e30, prb,
                                         OP.mult, OP.add))
            kb.op("vector", lambda e: e.reduce_max(GST[:, 5:6], SCR2[:, 8:16], AX))
            kb.op("vector", lambda e: e.tensor_scalar(GST[:, 6:7], GST[:, 2:3], 1.5, None,
                                                      OP.is_ge))
            kb.op("vector", lambda e: e.tensor_sub(GST[:, 7:8], GST[:, 3:4], GST[:, 5:6]))
            kb.op("vector", lambda e: e.tensor_mul(GST[:, 7:8], GST[:, 7:8], GST[:, 6:7]))
            kb.op("vector", lambda e: e.tensor_add(GST[:, 8:9], GST[:, 5:6], GST[:, 7:8]))
            v_thr = kb.op("vector", lambda e: e.tensor_scalar_mul(GST[:, 9:10], GST[:, 8:9],
                                                                  -2.0), inc=s_dve)
            v_msk = kb.op("scalar", lambda e, prb=prb:
                          e.activation(SCR2[:, 16:24], prb, AF.Sigmoid, scale=2.0,
                                       bias=GST[:, 9:10]),
                          waits=[(s_dve, v_thr)], inc=s_act)
            kb.wait("vector", s_act, v_msk)
            kb.op("vector", lambda e, prb=prb, wtk=wtk: e.tensor_mul(wtk, prb, SCR2[:, 16:24]))
            kb.op("vector", lambda e, wtk=wtk: e.tensor_scalar(SCR2[:, 24:32], wtk, 1.0, 0.0,
                                                               OP.mult, OP.add,
                                                               accum_out=GST[:, 10:11]))
            kb.op("vector", lambda e: e.tensor_scalar_add(GST[:, 10:11], GST[:, 10:11], 1e-9))
            kb.op("vector", lambda e: e.reciprocal(GST[:, 11:12], GST[:, 10:11]))
            v_wtk = kb.op("vector", lambda e, wtk=wtk:
                          e.tensor_scalar_mul(wtk, wtk, GST[:, 11:12]), inc=s_dve)

        tp_w = dtap("weights", [128, 32])
        if tp_w is not None:
            kb.dma("sync", tp_w, WTK[:, :], s_sp, waits=[(s_dve, v_wtk)])

        WTE = _p1("WTE", NEXP, T // 2, BF16)
        WFULL = _p1("WFULL", NEXP, T, BF16)
        for a in range(4):
            kb.op("tensor", lambda e, a=a: e.matmul(banks[2][0:NEXP, a * 128:(a + 1) * 128],
                                                    WTK[:, a * 8:(a + 1) * 8],
                                                    IDN[:, :], is_transpose=True,
                                                    start=(a == 0), stop=(a == 3)),
                  waits=([(s_dve, v_wtk)] if a == 0 else ()),
                  inc=(s_pe if a == 3 else None))
        v_wte_pe = s_pe.now
        v_wte = kb.op("vector", lambda e: e.tensor_copy(WTE[:, :], banks[2][0:NEXP, 0:512]),
                      waits=[(s_pe, v_wte_pe)], inc=s_dve)
        v_wf = kb.op("vector", lambda e: e.tensor_copy(
            WFULL[:, :], _ap(WTE, 0, [[T // 2, NEXP], [1, T // 2], [0, 2]])),
            inc=s_dve)

        for tt in range(2):
            pmb = banks[3 + tt]
            kb.op("tensor", lambda e, tt=tt, pmb=pmb: e.matmul(pmb[0:RANK, :], LIMESB[:, :],
                                                               WFULL[:, tt * 512:(tt + 1) * 512],
                                                               start=True, stop=True),
                  waits=([(s_dve, v_wf)] if tt == 0 else ()), inc=s_pe)
            v_pm = s_pe.now
            kb.op("vector", lambda e, tt=tt, pmb=pmb:
                  e.scalar_tensor_tensor(ZPMT[:, tt * 512:(tt + 1) * 512],
                                         pmb[0:RANK, :], 0.5,
                                         ZT[:, tt * 512:(tt + 1) * 512],
                                         OP.mult, OP.mult),
                  waits=[(s_pe, v_pm)], inc=s_dve)
        v_zpm = s_dve.now

        tp_z = dtap("zT", [RANK, T])
        if tp_z is not None:
            kb.dma("sync", tp_z, ZT[:, :], s_sp, waits=[(s_dve, v_zpm)])
        tp_zpm = dtap("zpmT", [RANK, T], BF16)
        if tp_zpm is not None:
            kb.dma("sync", tp_zpm, ZPMT[:, :], s_sp, waits=[(s_dve, v_zpm)])
        tp_h8 = dtap("H8T", [NEXP, T])
        if tp_h8 is not None:
            kb.dma("sync", tp_h8, H8T[:, :], s_sp, waits=[(s_dve, v_h8t)])

        scope_p1.close()
        kb.scope_barrier([s_dve, s_act, s_pe, s_gd, s_ld])

        scope_p2 = contextlib.ExitStack()
        W2T = [scope_p2.enter_context(nc.sbuf_tensor(f"W2T{i}", [128, 32 * 128], BF16))
               for i in range(2)]
        HDS = [scope_p2.enter_context(nc.sbuf_tensor(f"HDS{i}", [128, T], BF16))
               for i in range(2)]
        w2_free = [0, 0]
        hds_free = [0, 0]
        p2bank_free = [0, 0, 0, 0]
        v_hd_spills = 0
        for dmc in range(32):
            wb = dmc % 2
            v_w = kb.dma("gpsimd", W2T[wb][:, :],
                         _ap(hw2.tensor, dmc * 128 * 4096, [[4096, 128], [1, 4096]]),
                         s_w2[wb], waits=[(s_pe, w2_free[wb])])
            pb = (dmc % 2) * 2
            for dc in range(32):
                w = []
                if dc == 0:
                    w = [(s_w2[wb], v_w), (s_dve, max(p2bank_free[pb], p2bank_free[pb + 1],
                                                      v_xn))]
                for tt in range(2):
                    kb.op("tensor",
                          lambda e, wb=wb, dc=dc, tt=tt, pb=pb:
                          e.matmul(banks[pb + tt][:, :],
                                   W2T[wb][:, dc * 128:(dc + 1) * 128],
                                   XN[:, dc * T + tt * 512: dc * T + (tt + 1) * 512],
                                   start=(dc == 0), stop=False),
                          waits=(w if tt == 0 else ()))
            for tt in range(2):
                v_mm = kb.op("tensor",
                             lambda e, dmc=dmc, tt=tt, pb=pb:
                             e.matmul(banks[pb + tt][:, :],
                                      BMATB[:, dmc * 128:(dmc + 1) * 128],
                                      ZPMT[:, tt * 512:(tt + 1) * 512],
                                      start=False, stop=True),
                             waits=([(s_dve, v_zpm)] if dmc == 0 and tt == 0 else ()),
                             inc=s_pe)
            w2_free[wb] = v_mm
            hb = dmc % 2
            for tt in range(2):
                p2bank_free[pb + tt] = kb.op(
                    "vector",
                    lambda e, dmc=dmc, tt=tt, pb=pb, hb=hb:
                    e.tensor_scalar(HDS[hb][:, tt * 512:(tt + 1) * 512],
                                    banks[pb + tt][:, :],
                                    BFFN[:, dmc:dmc + 1], None, OP.add),
                    waits=[(s_pe, v_mm), (s_hdsp2[hb], hds_free[hb])])
            hd_out = _ap(hdsp.tensor, dmc * 128 * 128,
                         [[128, 128], [D * 128, 8], [1, 128]])
            hds_free[hb] = kb.dma("sync", hd_out, HDS[hb][:, :], s_hdsp2[hb],
                                  waits=[(s_dve, p2bank_free[pb + 1])])
            v_hd_spills = hds_free[hb]

        tp_hd = dtap("hd", [8 * D * 128], BF16)
        if tp_hd is not None:
            kb.dma("sync", tp_hd,
                   _ap(hdsp.tensor, 0, [[8 * D * 128, 1], [1, 8 * D * 128]]), s_sp,
                   waits=[(s_hdsp2[0], hds_free[0]), (s_hdsp2[1], hds_free[1])])

        scope_p2.close()
        kb.scope_barrier([s_dve, s_act, s_pe, s_gd, s_sp])

        scope_p3a = contextlib.ExitStack()
        _p3 = lambda name, p, f, dt: scope_p3a.enter_context(
            nc.sbuf_tensor(name, [p, f], dt))
        W1T = [_p3(f"W1T{i}", 128, 32 * 128, BF16) for i in range(2)]
        CIN = [_p3(f"CIN{i}", 128, 1027, BF16) for i in range(2)]
        SPST = [_p3(f"SPST{i}", 128, T, BF16) for i in range(2)]
        DTU = _p3("DTU", 128, T, F32)
        DA = _p3("DA", 128, T, F32)
        LNDT = _p3("LNDT", 128, T, F32)
        LC = _p3("LC", 128, T, F32)

        kb.op("vector", lambda e: e.memset(ONES128[:, :], 1.0))
        kb.op("vector", lambda e: e.memset(CIN[0][:, 0:3], 0.0))
        v_cin0 = kb.op("vector", lambda e: e.memset(CIN[1][:, 0:3], 0.0))

        w1_free = [0, 0]
        spst_free = [0, 0]
        p3bank_free = [None, None, None, None]
        cin_free = [0, 0]
        fc_order = [144] + list(range(64, 144)) + list(range(0, 64))
        v_dt_pipe = 0
        v_lct = 0
        v_ecum = 0
        v_halos = 0
        v_xbc_spill = 0
        v_gate_spill = 0
        v_q_done = 0

        for idx, fc in enumerate(fc_order):
            wb = idx % 2
            v_w = kb.dma("gpsimd", W1T[wb][:, :],
                         _ap(w1.tensor, fc * 128 * 4096, [[4096, 128], [1, 4096]]),
                         s_w2[wb], waits=[(s_pe, w1_free[wb])])
            pb = (idx % 2) * 2
            for dc in range(32):
                w = []
                if dc == 0:
                    w = [(s_w2[wb], v_w)]
                    for bf in (p3bank_free[pb], p3bank_free[pb + 1]):
                        if bf is not None:
                            w.append(bf)
                for tt in range(2):
                    v_mm = kb.op("tensor",
                                 lambda e, wb=wb, dc=dc, tt=tt, pb=pb:
                                 e.matmul(banks[pb + tt][:, :],
                                          W1T[wb][:, dc * 128:(dc + 1) * 128],
                                          XN[:, dc * T + tt * 512: dc * T + (tt + 1) * 512],
                                          start=(dc == 0), stop=(dc == 31)),
                                 waits=(w if tt == 0 else ()),
                                 inc=(s_pe if dc == 31 and tt == 1 else None))
            w1_free[wb] = v_mm

            if fc == 144:
                for tt in range(2):
                    v_u = kb.op("vector", lambda e, tt=tt, pb=pb:
                                e.tensor_scalar(DTU[:, tt * 512:(tt + 1) * 512],
                                                banks[pb + tt][:, :], DTB[:, 0:1], None,
                                                OP.add),
                                waits=[(s_pe, v_mm), (s_ld, v_consts)])
                p3bank_free[pb] = p3bank_free[pb + 1] = (s_dve, v_u)
                v_abs = kb.op("scalar", lambda e: e.activation(SCR2[:, :], DTU[:, :],
                                                               AF.Abs),
                              waits=[(s_dve, v_u)])
                v_e = kb.op("scalar", lambda e: e.activation(SCR2[:, :], SCR2[:, :],
                                                             AF.Exp, scale=-1.0))
                v_l = kb.op("scalar", lambda e: e.activation(SCR2[:, :], SCR2[:, :],
                                                             AF.Ln, bias=1.0))
                v_r = kb.op("vector", lambda e: e.tensor_scalar_max(DA[:, :], DTU[:, :],
                                                                    0.0))
                kb.wait("vector", s_act, v_l)
                v_dt = kb.op("vector", lambda e: e.tensor_add(DT[:, :], DA[:, :],
                                                              SCR2[:, :]))
                v_da = kb.op("vector", lambda e: e.tensor_scalar_mul(DA[:, :], DT[:, :],
                                                                     ANEG[:, 0:1]),
                             waits=[(s_act, v_aneg)])
                for c in range(NCHUNK):
                    v_cum = kb.op("vector", lambda e, c=c:
                                  e.tensor_tensor_scan(CUM[:, c * CS:(c + 1) * CS],
                                                       ONES128[:, :],
                                                       DA[:, c * CS:(c + 1) * CS],
                                                       0.0, OP.mult, OP.add))
                kb.wait("scalar", s_dve, v_dt)
                v_lndt = kb.op("scalar", lambda e: e.activation(LNDT[:, :], DT[:, :],
                                                                AF.Ln))
                v_ecum = kb.op("scalar", lambda e, v=v_cum: e.activation(
                    ECUM[:, :], CUM[:, :], AF.Exp), waits=[(s_dve, v_cum)])
                kb.wait("vector", s_act, v_lndt)
                v_lc = kb.op("vector", lambda e: e.tensor_sub(LC[:, :], LNDT[:, :],
                                                              CUM[:, :]))
                for g8 in range(8):
                    tb = 6 + (g8 % 2)
                    kb.op("tensor", lambda e, g8=g8, tb=tb:
                          e.matmul(banks[tb][:, 0:128], LC[:, g8 * 128:(g8 + 1) * 128],
                                   IDN[:, :], is_transpose=True, start=True, stop=True),
                          waits=[(s_dve, v_lc if g8 < 2 else 0),
                                 (s_dve, bank_free[tb])],
                          inc=s_pe)
                    v_tpe = s_pe.now
                    bank_free[tb] = kb.op("vector", lambda e, g8=g8, tb=tb:
                                          e.tensor_copy(LCT[:, g8 * 128:(g8 + 1) * 128],
                                                        banks[tb][:, 0:128]),
                                          waits=[(s_pe, v_tpe)])
                v_lct = bank_free[tb]
                kb.op("vector", lambda e: e.tensor_copy(Q1[:, :], CUM[:, :]))
                kb.op("vector", lambda e: e.tensor_tensor(SCR2[:, :], CUM[:, :], Q1[:, :],
                                                          OP.subtract))
                kb.op("vector", lambda e: e.tensor_copy(Q2[:, :], SCR2[:, :]))
                kb.op("vector", lambda e: e.tensor_tensor(SCR2[:, :], SCR2[:, :], Q2[:, :],
                                                          OP.subtract))
                v_q_done = kb.op("vector", lambda e: e.tensor_copy(R2[:, :], SCR2[:, :]))
                v_dt_pipe = v_q_done
            elif fc >= 64:
                cc = fc - 64
                cb = idx % 2
                for tt in range(2):
                    v_cc = kb.op("scalar", lambda e, tt=tt, pb=pb, cb=cb:
                                 e.copy(CIN[cb][:, 3 + tt * 512: 3 + (tt + 1) * 512],
                                        banks[pb + tt][:, :]),
                                 waits=([(s_pe, v_mm), (s_dve, cin_free[cb])]
                                        if tt == 0 else ()))
                p3bank_free[pb] = p3bank_free[pb + 1] = (s_act, v_cc)
                v_t0 = kb.op("vector", lambda e, cb=cb, cc=cc:
                             e.tensor_scalar_mul(ACC[:, :], CIN[cb][:, 0:T],
                                                 CONVW[:, cc * KC:cc * KC + 1]),
                             waits=[(s_act, v_cc)])
                for k in range(1, 4):
                    v_tk = kb.op("vector", lambda e, cb=cb, cc=cc, k=k:
                                 e.scalar_tensor_tensor(ACC[:, :], CIN[cb][:, k:k + T],
                                                        CONVW[:, cc * KC + k:cc * KC + k + 1],
                                                        ACC[:, :], OP.mult, OP.add))
                cin_free[cb] = v_tk
                kb.op("vector", lambda e, cc=cc:
                      e.tensor_copy(CR3[:, cc * 3:(cc + 1) * 3], ACC[:, 0:3]))
                v_halo = kb.op("vector", lambda e, cc=cc, cb=cb:
                               e.tensor_copy(HALOS[:, cc * 3:(cc + 1) * 3],
                                             CIN[cb][:, 1024:1027]))
                if cc == NXBC - 1:
                    v_halos = v_halo
                sbuf_i = idx % 2
                v_silu = kb.op("scalar", lambda e, cc=cc, sbuf_i=sbuf_i:
                               e.activation(SPST[sbuf_i][:, :], ACC[:, :], AF.Silu,
                                            bias=CONVB[:, cc:cc + 1]),
                               waits=[(s_dve, v_halo),
                                      (s_spst2[sbuf_i], spst_free[sbuf_i])])
                spst_free[sbuf_i] = kb.dma("sync",
                                           _ap(xbcsp.tensor, cc * 128 * T,
                                               [[T, 128], [1, T]]),
                                           SPST[sbuf_i][:, :], s_spst2[sbuf_i],
                                           waits=[(s_act, v_silu)])
                v_xbc_spill = spst_free[sbuf_i]
            else:
                sbuf_i = idx % 2
                for tt in range(2):
                    v_gn = kb.op("scalar", lambda e, tt=tt, pb=pb, sbuf_i=sbuf_i:
                                 e.copy(SPST[sbuf_i][:, tt * 512:(tt + 1) * 512],
                                        banks[pb + tt][:, :]),
                                 waits=([(s_pe, v_mm),
                                         (s_spst2[sbuf_i], spst_free[sbuf_i])]
                                        if tt == 0 else ()))
                p3bank_free[pb] = p3bank_free[pb + 1] = (s_act, v_gn)
                spst_free[sbuf_i] = kb.dma("sync",
                                           _ap(gatesp.tensor, fc * 128 * T,
                                               [[T, 128], [1, T]]),
                                           SPST[sbuf_i][:, :], s_spst2[sbuf_i],
                                           waits=[(s_act, v_gn)])
                v_gate_spill = spst_free[sbuf_i]

        tp_xbc = dtap("xbcsp", [NXBC * 128 * T], BF16)
        if tp_xbc is not None:
            kb.dma("sync", tp_xbc,
                   _ap(xbcsp.tensor, 0, [[NXBC * 128 * T, 1], [1, NXBC * 128 * T]]),
                   s_sp, waits=[(s_spst2[0], spst_free[0]), (s_spst2[1], spst_free[1])])
        tp_dt = dtap("dt", [128, T])
        if tp_dt is not None:
            kb.dma("sync", tp_dt, DT[:, :], s_sp, waits=[(s_dve, v_dt_pipe)])
        tp_cum = dtap("cum", [128, T])
        if tp_cum is not None:
            kb.dma("sync", tp_cum, CUM[:, :], s_sp, waits=[(s_dve, v_dt_pipe)])

        v_hs = kb.op("vector", lambda e: e.tensor_scalar_mul(HALOS[:, :], HALOS[:, :],
                                                             ISH0[:, 0:1]),
                     waits=[(s_dve, v_halos), (s_ld, v_consts)])
        v_c2aw = kb.dma("gpsimd", cc2a_in, HALOS[:, :], s_gd, waits=[(s_dve, v_hs)])
        kb.op("gpsimd", lambda e: e.collective_compute(
            "AllReduce", OP.add, replica_groups=[[0, 1], [2, 3], [4, 5], [6, 7]],
            ins=[cc2a_in.opt()], outs=[cc2a_out.opt()]),
            waits=[(s_gd, v_c2aw)], inc=s_cc2a, chain=False)
        kb.wait("gpsimd", s_cc2a, 1)
        v_hb = kb.dma("gpsimd", HALOS[:, :], cc2a_out, s_gd)
        v_f = kb.op("vector", lambda e: e.tensor_scalar_mul(HALOS[:, :], HALOS[:, :],
                                                            ISH1[:, 0:1]),
                    waits=[(s_gd, v_hb)])
        for t in range(3):
            for k in range(0, 3 - t):
                kb.op("vector", lambda e, t=t, k=k: e.tensor_tensor(
                    _ap(SCR2, 0, [[T, 128], [3, NXBC]]),
                    _ap(HALOS, t + k, [[NXBC * 3, 128], [3, NXBC]]),
                    _ap(CONVW, k, [[NXBC * KC, 128], [KC, NXBC]]),
                    OP.mult))
                kb.op("vector", lambda e, t=t: e.tensor_tensor(
                    _ap(CR3, t, [[NXBC * 3, 128], [3, NXBC]]),
                    _ap(CR3, t, [[NXBC * 3, 128], [3, NXBC]]),
                    _ap(SCR2, 0, [[T, 128], [3, NXBC]]),
                    OP.add))
        v_fb = kb.op("vector", lambda e: e.tensor_tensor(
            CR3[:, :], CR3[:, :],
            _ap(CONVB, 0, [[NXBC, 128], [1, NXBC], [0, 3]]), OP.add))
        v_fix = kb.op("scalar", lambda e: e.activation(XBCFIX[:, :], CR3[:, :], AF.Silu),
                      waits=[(s_dve, v_fb)])

        tp_fix = dtap("xbcfix", [128, NXBC * 3], BF16)
        if tp_fix is not None:
            kb.dma("sync", tp_fix, XBCFIX[:, :], s_sp, waits=[(s_act, v_fix)])

        scope_p3a.close()
        scope_xn.close()
        kb.scope_barrier([s_dve, s_act, s_pe, s_gd, s_sp, s_cc])

        slsp = kb.dint("slsp", [NCHUNK, 128, INTER], BF16)
        scope_p3b = contextlib.ExitStack()
        _pb = lambda name, p, f, dt: scope_p3b.enter_context(
            nc.sbuf_tensor(name, [p, f], dt))
        S = _pb("S", 128, INTER, F32)
        HSB = [_pb(f"HSB{i}", 128, 2048, BF16) for i in range(2)]
        X_TOK = [_pb(f"XTOK{i}", 128, INTER, BF16) for i in range(2)]
        XW = [_pb(f"XW{i}", 128, INTER, BF16) for i in range(2)]
        BFM3 = _pb("BFM3", 128, 2048, BF16)
        B_TOK = [_pb(f"BTOK{i}", 128, 1024, BF16) for i in range(2)]
        SB16 = _pb("SB16", 128, INTER, BF16)
        CDF = _pb("CDF", 1, 4 * 128, F32)
        CDECB = _pb("CDECB", 128, 128, F32)

        v_szero = kb.op("vector", lambda e: e.memset(S[:, :], 0.0))
        hsb_free = [(None, 0), (None, 0)]
        state_bank_free = [None] * 8
        v_cdf = 0
        v_xtok_spill = {}
        v_sl_spill = {}
        v_states_last = 0
        v_xtspill_last = 0
        v_slspill_last = 0

        for c in range(NCHUNK):
            v_g = kb.dma("sync", _ap(CDF, c * 128, [[512, 1], [1, 128]]),
                         _ap(CUM, c * CS + (CS - 1), [[T, 128], [1, 1]]), s_cdf,
                         waits=[(s_dve, v_dt_pipe)])
            v_cdf = kb.op("scalar", lambda e, c=c: e.activation(
                _ap(CDF, c * 128, [[512, 1], [1, 128]]),
                _ap(CDF, c * 128, [[512, 1], [1, 128]]), AF.Exp),
                waits=[(s_cdf, v_g)])

            v_dl = kb.op("scalar", lambda e, c=c: e.activation(
                SCR2[:, 0:CS], CUM[:, c * CS:(c + 1) * CS], AF.Exp, scale=-1.0,
                bias=CUM[:, c * CS + CS - 1:c * CS + CS]),
                waits=[(s_dve, v_dt_pipe)])
            v_w = kb.op("vector", lambda e, c=c: e.tensor_mul(
                SCR2[:, CS:2 * CS], SCR2[:, 0:CS], DT[:, c * CS:(c + 1) * CS]),
                waits=[(s_act, v_dl)])
            for jt in range(2):
                tb = 6 + jt
                v_tp = kb.op("tensor", lambda e, jt=jt, tb=tb:
                             e.matmul(banks[tb][:, 0:128],
                                      SCR2[:, CS + jt * 128:CS + (jt + 1) * 128],
                                      IDN[:, :], is_transpose=True, start=True,
                                      stop=True),
                             waits=[(s_dve, max(v_w, bank_free[tb]))], inc=s_pe)
                bank_free[tb] = kb.op("vector", lambda e, c=c, jt=jt, tb=tb:
                                      e.tensor_copy(
                                          WDTT[:, (2 * c + jt) * 128:(2 * c + jt + 1) * 128],
                                          banks[tb][:, 0:128]),
                                      waits=[(s_pe, v_tp)])

            for blk in range(8):
                hb = blk % 2
                v_h = kb.dma("sync", HSB[hb][:, :],
                             _ap(xbcsp.tensor, blk * 8 * 128 * T + c * CS,
                                 [[T, 128], [128 * T, 8], [1, CS]]), s_hsb2[hb],
                             waits=[hsb_free[hb]] if hsb_free[hb][0] else [])
                v_patch = 0
                if c == 0:
                    v_patch = kb.op("vector", lambda e, blk=blk:
                                    e.tensor_copy(
                                        _ap(HSB[blk % 2], 0, [[2048, 128], [CS, 8], [1, 3]]),
                                        _ap(XBCFIX, blk * 24, [[240, 128], [3, 8], [1, 3]])),
                                    waits=[(s_hsb2[hb], v_h), (s_act, v_fix)])
                for sub in range(8):
                    cc = blk * 8 + sub
                    for jt in range(2):
                        tb = 6 + jt
                        w = [(s_hsb2[hb], v_h), (s_dve, max(bank_free[tb], v_patch))] \
                            if sub == 0 else [(s_dve, bank_free[tb])]
                        v_tp = kb.op("tensor", lambda e, hb=hb, sub=sub, jt=jt, tb=tb:
                                     e.matmul(banksb[tb][:, 0:128],
                                              HSB[hb][:, sub * CS + jt * 128:
                                                      sub * CS + (jt + 1) * 128],
                                              IDNB[:, :], is_transpose=True,
                                              start=True, stop=True),
                                     waits=w, inc=s_pe)
                        w_ev = [(s_pe, v_tp)]
                        if blk == 0 and sub == 0 and v_xtspill_last:
                            w_ev.append((s_xtsp, v_xtspill_last))
                        bank_free[tb] = kb.op(
                            "vector", lambda e, cc=cc, jt=jt, tb=tb:
                            e.tensor_copy(X_TOK[jt][:, cc * 128:(cc + 1) * 128],
                                          banksb[tb][:, 0:128]),
                            waits=w_ev)
                hsb_free[hb] = (s_dve, bank_free[7])
            v_xtok = bank_free[7]
            v_xw = 0
            for jt in range(2):
                v_xw = kb.op("vector", lambda e, c=c, jt=jt: e.tensor_tensor(
                    XW[jt][:, :], X_TOK[jt][:, :],
                    _ap(WDTT, (2 * c + jt) * 128, [[1024, 128], [1, 128], [0, PD]]),
                    OP.mult),
                    waits=([(s_pe, v_states_last)] if jt == 0 else ()))
            for jt in range(2):
                v_xtok_spill[(c, jt)] = kb.dma(
                    "sync", _ap(xtoksp.tensor, (c * 2 + jt) * 128 * INTER,
                                [[INTER, 128], [1, INTER]]),
                    X_TOK[jt][:, :], s_xtsp, waits=[(s_dve, v_xw)])
            v_xtspill_last = v_xtok_spill[(c, 1)]

            if c > 0:
                v_sb = kb.op("vector", lambda e: e.tensor_copy(SB16[:, :], S[:, :]),
                             waits=([(s_slsp, v_slspill_last)] if v_slspill_last else ()))
                v_sl_spill[c] = kb.dma("sync",
                                       _ap(slsp.tensor, c * 128 * INTER,
                                           [[INTER, 128], [1, INTER]]),
                                       SB16[:, :], s_slsp, waits=[(s_dve, v_sb)])
                v_slspill_last = v_sl_spill[c]

            v_b = kb.dma("sync", BFM3[:, :],
                         _ap(xbcsp.tensor, 64 * 128 * T + c * CS,
                             [[T, 128], [128 * T, 8], [1, CS]]), s_bfm,
                         waits=[(s_dve, v_xw)])
            v_bpatch = 0
            if c == 0:
                v_bpatch = kb.op("vector", lambda e: e.tensor_copy(
                    _ap(BFM3, 0, [[2048, 128], [CS, 8], [1, 3]]),
                    _ap(XBCFIX, 64 * 3, [[240, 128], [3, 8], [1, 3]])),
                    waits=[(s_bfm, v_b), (s_act, v_fix)])
            for g in range(G):
                for jt in range(2):
                    tb = 6 + jt
                    w = [(s_bfm, v_b), (s_dve, max(bank_free[tb], v_bpatch))] \
                        if g == 0 else [(s_dve, bank_free[tb])]
                    v_tp = kb.op("tensor", lambda e, g=g, jt=jt, tb=tb:
                                 e.matmul(banksb[tb][:, 0:128],
                                          BFM3[:, g * CS + jt * 128:g * CS + (jt + 1) * 128],
                                          IDNB[:, :], is_transpose=True,
                                          start=True, stop=True),
                                 waits=w, inc=s_pe)
                    bank_free[tb] = kb.op(
                        "vector", lambda e, g=g, jt=jt, tb=tb:
                        e.tensor_copy(B_TOK[jt][:, g * 128:(g + 1) * 128],
                                      banksb[tb][:, 0:128]),
                        waits=[(s_pe, v_tp)])
            v_btok = bank_free[7]

            v_cb = kb.op("tensor", lambda e, c=c: e.matmul(
                banks[5][:, 0:128], ONESF[0:1, :],
                _ap(CDF, c * 128, [[512, 1], [1, 128]]),
                start=True, stop=True),
                waits=[(s_act, v_cdf), (s_dve, max(v_ones, state_bank_free[5][1]
                                                   if state_bank_free[5] else 0))],
                inc=s_pe)
            v_cdecb = kb.op("vector", lambda e: e.tensor_copy(CDECB[:, :],
                                                              banks[5][:, 0:128]),
                            waits=[(s_pe, v_cb)])
            state_bank_free[5] = (s_dve, v_cdecb)

            for g in range(G):
                bka, bkb = (0, 1) if g % 2 == 0 else (2, 3)
                for jt in range(2):
                    for hh in range(16):
                        h = g * 16 + hh
                        bk = bka if hh < 8 else bkb
                        reg = (hh % 8) * 64
                        w = []
                        if jt == 0 and hh == 0:
                            w = [(s_dve, v_btok)]
                            for bkx in (bka, bkb):
                                if state_bank_free[bkx]:
                                    w.append(state_bank_free[bkx])
                        v_smm = kb.op("tensor", lambda e, g=g, jt=jt, h=h, bk=bk, reg=reg,
                                      hh=hh:
                                      e.matmul(banks[bk][:, reg:reg + PD],
                                               B_TOK[jt][:, g * 128:(g + 1) * 128],
                                               XW[jt][:, h * PD:(h + 1) * PD],
                                               start=(jt == 0 and hh % 8 == 0),
                                               stop=(jt == 1 and hh % 8 == 7)),
                                      waits=w,
                                      inc=(s_pe if jt == 1 and hh == 15 else None))
                for hh in range(16):
                    h = g * 16 + hh
                    bk = bka if hh < 8 else bkb
                    reg = (hh % 8) * 64
                    w = [(s_pe, v_smm), (s_dve, v_cdecb)] if hh == 0 else []
                    v_su = kb.op("vector", lambda e, h=h, bk=bk, reg=reg:
                                 e.scalar_tensor_tensor(S[:, h * PD:(h + 1) * PD],
                                                        S[:, h * PD:(h + 1) * PD],
                                                        CDECB[:, h:h + 1],
                                                        banks[bk][:, reg:reg + PD],
                                                        OP.mult, OP.add),
                                 waits=w)
                state_bank_free[bka] = state_bank_free[bkb] = (s_dve, v_su)
                v_states_last = v_smm
            kb.scope_barrier(ALLS)

        v_ss = kb.op("vector", lambda e: e.tensor_scalar_mul(S[:, :], S[:, :],
                                                             ISH0[:, 0:1]))
        v_c2bw = kb.dma("gpsimd", cc2b_in, S[:, :], s_gd, waits=[(s_dve, v_ss)])
        kb.op("gpsimd", lambda e: e.collective_compute(
            "AllReduce", OP.add, replica_groups=[[0, 1], [2, 3], [4, 5], [6, 7]],
            ins=[cc2b_in.opt()], outs=[cc2b_out.opt()]),
            waits=[(s_gd, v_c2bw)], inc=s_cc2b, chain=False)
        v_c2b = 1

        kb.op("vector", lambda e: e.memset(PBF[0:1, 0:128], 1.0))
        for c in range(1, NCHUNK):
            kb.op("vector", lambda e, c=c: e.tensor_mul(
                PBF[0:1, c * 128:(c + 1) * 128],
                PBF[0:1, (c - 1) * 128:c * 128],
                CDF[0:1, (c - 1) * 128:c * 128]),
                waits=([(s_act, v_cdf)] if c == 1 else []))
        v_pbf = kb.op("vector", lambda e: e.tensor_scalar_mul(PBF[0:1, :], PBF[0:1, :],
                                                              ISH1[0:1, 0:1]))
        scope_p3b.close()
        kb.scope_barrier([s_dve, s_act, s_pe, s_gd, s_sp, s_ld, s_cc])

        scope_p4 = contextlib.ExitStack()
        _p4 = lambda name, p, f, dt: scope_p4.enter_context(
            nc.sbuf_tensor(name, [p, f], dt))
        XT4 = [_p4(f"XT4_{i}", 128, INTER, BF16) for i in range(2)]
        CB4 = _p4("CB4", 128, 2048, BF16)
        BB4 = _p4("BB4", 128, 2048, BF16)
        SLB = _p4("SLB", 128, INTER, BF16)
        S4B = _p4("S4B", 128, INTER, BF16)
        S4S = _p4("S4S", 128, INTER, BF16)
        PBBE = _p4("PBBE", 128, 128, F32)
        GM = _p4("GM", 128, G * 512, BF16)
        FCB3 = [_p4(f"FCB3_{i}", 3, 2048, BF16) for i in range(2)]
        EROWS = [_p4(f"EROWS{i}", 1, 2048, BF16) for i in range(2)]
        EB = [_p4(f"EB{i}", 128, 512, BF16) for i in range(3)]
        SC4 = [_p4(f"SC4_{i}", 128, 512, BF16) for i in range(4)]
        CX4 = [_p4(f"CX4_{i}", 128, 256, BF16) for i in range(4)]
        HS4 = _p4("HS4", 128, 2048, BF16)
        GT4 = _p4("GT4", 128, 2048, BF16)
        SILG = _p4("SILG", 128, 2048, BF16)
        GG4 = _p4("GG4", 128, 2048, F32)
        GSQ4 = [_p4(f"GSQ4_{i}", 128, 256, BF16) for i in range(2)]
        RSR = _p4("RSR", 1, 256, F32)
        RSBE = _p4("RSBE", 128, 256, F32)
        YGST = [_p4(f"YGST{i}", 128, 2048, BF16) for i in range(2)]
        ONECOL = _p4("ONECOL", 128, 1, BF16)
        CROW = _p4("CROW", 1, 4, F32)

        v_onecol = kb.op("vector", lambda e: e.memset(ONECOL[:, :], 1.0))
        v_crow = kb.op("vector", lambda e: e.memset(CROW[0:1, 0:1], 1e-5))
        kb.wait("gpsimd", s_cc2b, 1)
        v_s4 = kb.dma("gpsimd", S4B[:, :], cc2b_out, s_gd)
        tp_s4 = dtap("s4", [128, INTER])
        if tp_s4 is not None:
            kb.dma("sync", tp_s4, cc2b_out, s_sp, waits=[(s_gd, v_s4)])

        if "sc0" in debug:
            dbg["sc0"] = kb.dout("dbg_sc0", [128, 512], BF16)
            dbg["e0"] = kb.dout("dbg_e0", [128, 512], BF16)
            dbg["g0"] = kb.dout("dbg_g0", [128, 512], BF16)
        yraw_tap_on = "yraw" in debug
        yraw_guard = 0
        djbank_exp = [(None, 0), (None, 0)]
        eb_free = [(None, 0)] * 3
        sc_free = [(None, 0)] * 4
        cx_free = [(None, 0)] * 4
        ybank_evac = [(None, 0), (None, 0)]
        ybank_ready = [0, 0]
        fcb_guard = [(None, 0), (None, 0)]
        ebcast_guard = (None, 0)
        gsq_guard = [(None, 0), (None, 0)]
        gm_dve = 0
        pbb_guard = 0
        silg_guard = 0
        hs4_guard = 0
        pool_gg_last = 0
        var_guard = 0
        rsb_guard = 0
        ygst_guard = [(None, 0), (None, 0)]
        v_ebc = 0
        v_yg_spill = 0
        re_guard = []

        for c in range(NCHUNK):
            w_re = re_guard
            v_x0 = kb.dma("sync", XT4[0][:, :],
                          _ap(xtoksp.tensor, (c * 2 + 0) * 128 * INTER,
                              [[INTER, 128], [1, INTER]]), s_x42[0], waits=w_re)
            v_x1 = kb.dma("sync", XT4[1][:, :],
                          _ap(xtoksp.tensor, (c * 2 + 1) * 128 * INTER,
                              [[INTER, 128], [1, INTER]]), s_x42[1], waits=w_re)
            v_bb = kb.dma("sync", BB4[:, :],
                          _ap(xbcsp.tensor, 64 * 128 * T + c * CS,
                              [[T, 128], [128 * T, 8], [1, CS]]), s_bc, waits=w_re)
            v_cb4 = kb.dma("sync", CB4[:, :],
                           _ap(xbcsp.tensor, 72 * 128 * T + c * CS,
                               [[T, 128], [128 * T, 8], [1, CS]]), s_bc, waits=w_re)
            v_slb = 0
            if c > 0:
                v_slb = kb.dma("sync", SLB[:, :],
                               _ap(slsp.tensor, c * 128 * INTER,
                                   [[INTER, 128], [1, INTER]]), s_slbr, waits=w_re)
            v_bpat = v_cpat = 0
            if c == 0:
                v_bpat = kb.op("vector", lambda e: e.tensor_copy(
                    _ap(BB4, 0, [[2048, 128], [CS, 8], [1, 3]]),
                    _ap(XBCFIX, 64 * 3, [[240, 128], [3, 8], [1, 3]])),
                    waits=[(s_bc, v_cb4), (s_act, v_fix)])
                v_cpat = kb.op("vector", lambda e: e.tensor_copy(
                    _ap(CB4, 0, [[2048, 128], [CS, 8], [1, 3]]),
                    _ap(XBCFIX, 72 * 3, [[240, 128], [3, 8], [1, 3]])),
                    waits=[(s_bc, v_cb4), (s_act, v_fix)])

            for g in range(G):
                gb = 6 + (g % 2)
                for jt in range(2):
                    w = []
                    if jt == 0:
                        w = [(s_bc, v_cb4), (s_dve, max(v_bpat, v_cpat))]
                        if g >= 2:
                            w.append((s_dve, gm_dve))
                        if c > 0 and g < 2:
                            w = w + re_guard
                    v_gmm = kb.op("tensor", lambda e, g=g, jt=jt, gb=gb:
                                  e.matmul(banks[gb][:, jt * CS:(jt + 1) * CS],
                                           BB4[:, g * CS + jt * 128:g * CS + (jt + 1) * 128],
                                           CB4[:, g * CS:(g + 1) * CS],
                                           start=(jt == 0), stop=(jt == 1)),
                                  waits=w, inc=(s_pe if jt == 1 else None))
                gm_dve = kb.op("vector", lambda e, g=g, gb=gb:
                               e.tensor_copy(GM[:, g * 512:(g + 1) * 512],
                                             banks[gb][:, :]),
                               waits=[(s_pe, v_gmm)])
            v_gm = gm_dve

            v_pbb = kb.op("tensor", lambda e, c=c: e.matmul(
                banks[5][:, 0:128], ONESF[0:1, :],
                _ap(PBF, c * 128, [[512, 1], [1, 128]]),
                start=True, stop=True),
                waits=[(s_dve, max(v_pbf, pbb_guard))], inc=s_pe)
            pbb_guard = kb.op("vector", lambda e: e.tensor_copy(PBBE[:, :],
                                                                banks[5][:, 0:128]),
                              waits=[(s_pe, v_pbb)])
            v_s4s = kb.op("vector", lambda e: e.tensor_tensor(
                S4S[:, :], S4B[:, :],
                _ap(PBBE, 0, [[128, 128], [1, 128], [0, PD]]), OP.mult),
                waits=[(s_dve, pbb_guard), (s_gd, v_s4)] + re_guard)

            for g in range(G):
                v_gt = kb.dma("sync", GT4[:, :],
                              _ap(gatesp.tensor, g * 8 * 128 * T + c * CS,
                                  [[T, 128], [128 * T, 8], [1, CS]]), s_gt4r,
                              waits=[(s_act, silg_guard)] if c + g > 0 else [])
                v_hs4 = kb.dma("sync", HS4[:, :],
                               _ap(xbcsp.tensor, g * 8 * 128 * T + c * CS,
                                   [[T, 128], [128 * T, 8], [1, CS]]), s_hs4r,
                               waits=[(s_dve, hs4_guard)] if c + g > 0 else [])
                v_hpat = 0
                if c == 0:
                    v_hpat = kb.op("vector", lambda e, g=g: e.tensor_copy(
                        _ap(HS4, 0, [[2048, 128], [CS, 8], [1, 3]]),
                        _ap(XBCFIX, g * 24, [[240, 128], [3, 8], [1, 3]])),
                        waits=[(s_hs4r, v_hs4), (s_act, v_fix)])
                silg_guard = kb.op("scalar", lambda e: e.activation(
                    SILG[:, :], GT4[:, :], AF.Silu),
                    waits=[(s_gt4r, v_gt), (s_pool, pool_gg_last)])

                for hb2 in range(2):
                    hb = g * 2 + hb2
                    fb = hb % 2
                    wfl = [fcb_guard[fb]] if fcb_guard[fb][0] else []
                    v_f1 = kb.dma("sync", _ap(FCB3[fb], 0, [[2048, 1], [1, 2048]]),
                                  _ap(Q1, hb * 8 * T + c * CS, [[T, 8], [1, CS]]),
                                  s_fcb2[fb], waits=wfl)
                    v_f2 = kb.dma("sync", _ap(FCB3[fb], 2048, [[2048, 1], [1, 2048]]),
                                  _ap(Q2, hb * 8 * T + c * CS, [[T, 8], [1, CS]]),
                                  s_fcb2[fb], waits=wfl)
                    v_f3 = kb.dma("sync", _ap(FCB3[fb], 4096, [[2048, 1], [1, 2048]]),
                                  _ap(R2, hb * 8 * T + c * CS, [[T, 8], [1, CS]]),
                                  s_fcb2[fb], waits=wfl)
                    v_f4 = kb.dma("sync", EROWS[fb][0:1, :],
                                  _ap(ECUM, hb * 8 * T + c * CS, [[T, 8], [1, CS]]),
                                  s_fcb2[fb], waits=wfl)
                    v_fl = v_f4
                    last_dmm = 0
                    for hh in range(8):
                        h = hb * 8 + hh
                        dbk = h % 2
                        ebuf = h % 3
                        sbuf4 = h % 4
                        cbuf = h % 4
                        w = [(s_fcb2[fb], v_fl)]
                        if djbank_exp[dbk][0]:
                            w.append(djbank_exp[dbk])
                        kb.op("tensor", lambda e, dbk=dbk:
                              e.matmul(banks[dbk][:, :], IDNB[:, :], MASKP[:, :],
                                       start=True, stop=False),
                              waits=w)
                        last_dmm = kb.op("tensor", lambda e, dbk=dbk, hh=hh, fb=fb:
                                         e.matmul(banks[dbk][:, :], ONESB[0:3, :],
                                                  _ap(FCB3[fb], hh * CS,
                                                      [[2048, 3], [0, 2], [1, CS]]),
                                                  start=False, stop=True),
                                         waits=[], inc=s_pe)
                        w_e = [(s_pe, last_dmm)]
                        if eb_free[ebuf][0]:
                            w_e.append(eb_free[ebuf])
                        for jt in range(2):
                            v_exp = kb.op("scalar", lambda e, dbk=dbk, ebuf=ebuf, jt=jt,
                                          c=c, h=h:
                                          e.activation(EB[ebuf][:, jt * CS:(jt + 1) * CS],
                                                       banks[dbk][:, jt * CS:(jt + 1) * CS],
                                                       AF.Exp,
                                                       bias=LCT[:, (2 * c + jt) * 128 + h:
                                                                (2 * c + jt) * 128 + h + 1]),
                                          waits=(w_e if jt == 0 else []))
                        djbank_exp[dbk] = (s_act, v_exp)
                        w_s = [(s_act, v_exp), (s_dve, v_gm)]
                        if sc_free[sbuf4][0]:
                            w_s.append(sc_free[sbuf4])
                        eng_s = "vector" if h % 2 == 0 else "gpsimd"
                        v_sc = kb.op(eng_s, lambda e, ebuf=ebuf, sbuf4=sbuf4, g=g:
                                     e.tensor_tensor(SC4[sbuf4][:, :], EB[ebuf][:, :],
                                                     GM[:, g * 512:(g + 1) * 512],
                                                     OP.mult),
                                     waits=w_s)
                        sc_ready = (s_dve if h % 2 == 0 else s_pool, v_sc)
                        if c == 0 and h == 0 and "sc0" in debug:
                            kb.dma("sync", dbg["sc0"], SC4[sbuf4][:, :], s_sp,
                                   waits=[sc_ready])
                            kb.dma("sync", dbg["e0"], EB[ebuf][:, :], s_sp,
                                   waits=[sc_ready])
                            kb.dma("sync", dbg["g0"], GM[:, 0:512], s_sp,
                                   waits=[(s_dve, v_gm)])
                        eb_free[ebuf] = sc_ready
                        if hh % 2 == 0:
                            w_eb = [(s_fcb2[fb], v_fl)]
                            if ebcast_guard[0]:
                                w_eb.append(ebcast_guard)
                            v_ebc = kb.op("tensor", lambda e, hh=hh, fb=fb:
                                          e.matmul(banks[2][:, :], ONESB[0:1, :],
                                                   EROWS[fb][0:1, hh * CS:(hh + 2) * CS],
                                                   start=True, stop=True),
                                          waits=w_eb, inc=s_pe)
                        w_c = [(s_pe, v_ebc), (s_bc, v_cb4)]
                        if cx_free[cbuf][0]:
                            w_c.append(cx_free[cbuf])
                        if c == 0:
                            w_c.append((s_dve, v_cpat))
                        v_cx = kb.op("vector", lambda e, cbuf=cbuf, g=g, hh=hh:
                                     e.tensor_tensor(CX4[cbuf][:, :],
                                                     CB4[:, g * CS:(g + 1) * CS],
                                                     banks[2][:, (hh % 2) * CS:
                                                              (hh % 2 + 1) * CS],
                                                     OP.mult),
                                     waits=w_c)
                        if hh % 2 == 1:
                            ebcast_guard = (s_dve, v_cx)
                        ybk = 3 + (h // 4) % 2
                        preg = (h % 2) * 64
                        creg = ((h % 4) // 2) * CS
                        w_y = [sc_ready, (s_x42[0], v_x0), (s_x42[1], v_x1)]
                        if h % 4 < 2 and ybank_evac[ybk - 3][0]:
                            w_y.append(ybank_evac[ybk - 3])
                        kb.op("tensor", lambda e, h=h, ybk=ybk, preg=preg, creg=creg,
                              sbuf4=sbuf4:
                              e.matmul(banks[ybk][preg:preg + PD, creg:creg + CS],
                                       XT4[0][:, h * PD:(h + 1) * PD],
                                       SC4[sbuf4][:, 0:CS],
                                       start=(h % 4 < 2), stop=False,
                                       skip_group_check=True),
                              waits=w_y)
                        kb.op("tensor", lambda e, h=h, ybk=ybk, preg=preg, creg=creg,
                              sbuf4=sbuf4:
                              e.matmul(banks[ybk][preg:preg + PD, creg:creg + CS],
                                       XT4[1][:, h * PD:(h + 1) * PD],
                                       SC4[sbuf4][:, CS:2 * CS],
                                       start=False, stop=False,
                                       skip_group_check=True))
                        w_yi = [(s_dve, v_cx)]
                        if c > 0:
                            kb.op("tensor", lambda e, h=h, ybk=ybk, preg=preg,
                                  creg=creg, cbuf=cbuf:
                                  e.matmul(banks[ybk][preg:preg + PD, creg:creg + CS],
                                           SLB[:, h * PD:(h + 1) * PD],
                                           CX4[cbuf][:, :],
                                           start=False, stop=False,
                                           skip_group_check=True),
                                  waits=[(s_slbr, v_slb), (s_dve, v_cx)])
                        v_ymm = kb.op("tensor", lambda e, h=h, ybk=ybk, preg=preg,
                                      creg=creg, cbuf=cbuf:
                                      e.matmul(banks[ybk][preg:preg + PD, creg:creg + CS],
                                               S4S[:, h * PD:(h + 1) * PD],
                                               CX4[cbuf][:, :],
                                               start=False, stop=(h % 4 >= 2),
                                               skip_group_check=True),
                                      waits=w_yi + [(s_dve, v_s4s)],
                                      inc=(s_pe if h % 4 == 3 else None))
                        if h % 4 == 3:
                            for _i in range(4):
                                sc_free[_i] = (s_pe, v_ymm)
                                cx_free[_i] = (s_pe, v_ymm)
                            ybank_ready[ybk - 3] = v_ymm
                    fcb_guard[fb] = (s_pe, s_pe.n)
                    for cc8 in range(hb2 * 4, hb2 * 4 + 4):
                        cc = g * 8 + cc8
                        ybk = 3 + ((cc % 4) // 2)
                        creg = (cc % 2) * CS
                        w_e2 = [(s_pe, ybank_ready[ybk - 3]), (s_hs4r, v_hs4),
                                (s_pool, pool_gg_last)]
                        if c == 0:
                            w_e2.append((s_dve, v_hpat))
                        if yraw_tap_on and yraw_guard:
                            w_e2 = w_e2 + [(s_sp, yraw_guard)]
                        v_gpre = kb.op("vector",
                                       lambda e, cc=cc, cc8=cc8, ybk=ybk, creg=creg:
                                       e.scalar_tensor_tensor(
                                           SCR2[:, 0:CS],
                                           HS4[:, cc8 * CS:(cc8 + 1) * CS],
                                           DGRP[:, cc:cc + 1],
                                           banks[ybk][:, creg:creg + CS],
                                           OP.mult, OP.add),
                                       waits=w_e2)
                        if yraw_tap_on:
                            yraw_guard = kb.dma("sync",
                                                _ap(yrawsp.tensor,
                                                    cc * 128 * T + c * CS,
                                                    [[T, 128], [1, CS]]),
                                                SCR2[:, 0:CS], s_sp,
                                                waits=[(s_dve, v_gpre)])
                        v_gg = kb.op("gpsimd", lambda e, cc8=cc8:
                                     e.tensor_tensor(GG4[:, cc8 * CS:(cc8 + 1) * CS],
                                                     SCR2[:, 0:CS],
                                                     SILG[:, cc8 * CS:(cc8 + 1) * CS],
                                                     OP.mult),
                                     waits=[(s_dve, v_gpre), (s_act, silg_guard)])
                        pool_gg_last = v_gg
                        gq = cc8 % 2
                        w_sq = [(s_pool, v_gg)]
                        if gsq_guard[gq][0]:
                            w_sq.append(gsq_guard[gq])
                        v_gsq = kb.op("scalar", lambda e, cc8=cc8, gq=gq:
                                      e.activation(GSQ4[gq][:, :],
                                                   GG4[:, cc8 * CS:(cc8 + 1) * CS],
                                                   AF.Square),
                                      waits=w_sq)
                        w_v = [(s_act, v_gsq)]
                        if cc8 == 0:
                            w_v.append((s_dve, max(v_onecol, var_guard, v_gm)))
                        v_var = kb.op("tensor", lambda e, cc8=cc8, gq=gq:
                                      e.matmul(banks[6][0:1, 0:CS], ONECOL[:, :],
                                               GSQ4[gq][:, :],
                                               start=(cc8 == 0), stop=(cc8 == 7)),
                                      waits=w_v, inc=s_pe)
                        gsq_guard[gq] = (s_pe, v_var)
                    ybank_evac[0] = ybank_evac[1] = (s_dve, v_gpre)
                    kb.scope_barrier(ALLS)
                v_rs1 = kb.op("scalar", lambda e: e.activation(
                    RSR[0:1, :], banks[6][0:1, 0:CS], AF.Sqrt, scale=1.0 / GS,
                    bias=CROW[0:1, 0:1]),
                    waits=[(s_pe, v_var), (s_dve, v_crow)])
                var_guard = kb.op("vector", lambda e: e.reciprocal(RSR[0:1, :],
                                                                   RSR[0:1, :]),
                                  waits=[(s_act, v_rs1)])
                hs4_guard = var_guard
                v_rsb = kb.op("tensor", lambda e: e.matmul(
                    banks[7][:, 0:CS], ONESF[0:1, :], RSR[0:1, :],
                    start=True, stop=True),
                    waits=[(s_dve, max(var_guard, rsb_guard))], inc=s_pe)
                v_rsbe = kb.op("vector", lambda e: e.tensor_copy(RSBE[:, :],
                                                                 banks[7][:, 0:CS]),
                               waits=[(s_pe, v_rsb)])
                rsb_guard = v_rsbe
                rsbe_guard = v_rsbe
                yb2 = g % 2
                w_y7 = [(s_dve, v_rsbe)]
                if ygst_guard[yb2][0]:
                    w_y7.append(ygst_guard[yb2])
                for cc8 in range(8):
                    cc = g * 8 + cc8
                    v_yg = kb.op("vector", lambda e, cc=cc, cc8=cc8, yb2=yb2:
                                 e.scalar_tensor_tensor(YGST[yb2][:, cc8 * CS:(cc8 + 1) * CS],
                                                        GG4[:, cc8 * CS:(cc8 + 1) * CS],
                                                        NWGRP[:, cc:cc + 1],
                                                        RSBE[:, :], OP.mult, OP.mult),
                                 waits=(w_y7 if cc8 == 0 else []))
                v_yg_spill = kb.dma("sync",
                                    _ap(ygsp.tensor, g * 8 * 128 * T + c * CS,
                                        [[T, 128], [128 * T, 8], [1, CS]]),
                                    YGST[yb2][:, :], s_ygsp, waits=[(s_dve, v_yg)])
                ygst_guard[yb2] = (s_ygsp, v_yg_spill)
                kb.scope_barrier(ALLS)
            re_guard = [(s_pe, s_pe.n), (s_dve, s_dve.n), (s_pool, s_pool.n)]

        tp_yraw = dtap("yraw", [NGATE * 128 * T])
        if tp_yraw is not None:
            kb.dma("sync", tp_yraw,
                   _ap(yrawsp.tensor, 0, [[NGATE * 128 * T, 1], [1, NGATE * 128 * T]]),
                   s_sp, waits=[(s_sp, yraw_guard)])
        tp_sl = dtap("sl", [3 * 128 * INTER], BF16)
        if tp_sl is not None:
            kb.dma("sync", tp_sl,
                   _ap(slsp.tensor, 128 * INTER, [[3 * 128 * INTER, 1], [1, 3 * 128 * INTER]]),
                   s_sp, waits=[(s_sp, s_sp.n)])
        tp_yg = dtap("yg", [NGATE * 128 * T], BF16)
        if tp_yg is not None:
            kb.dma("sync", tp_yg,
                   _ap(ygsp.tensor, 0, [[NGATE * 128 * T, 1], [1, NGATE * 128 * T]]),
                   s_sp, waits=[(s_sp, v_yg_spill)])

        scope_p4.close()
        scope_dt.close()
        kb.scope_barrier([s_dve, s_act, s_pe, s_gd, s_sp, s_ld, s_pool])

        scope_p5 = contextlib.ExitStack()
        _p5 = lambda name, p, f, dt: scope_p5.enter_context(
            nc.sbuf_tensor(name, [p, f], dt))
        YG5 = _p5("YG5", 128, NGATE * T, BF16)
        OW5 = [_p5(f"OW5_{i}", 128, NGATE * 128, BF16) for i in range(2)]
        HDF = [_p5(f"HDF{i}", 128, T, BF16) for i in range(2)]
        G2 = _p5("G2", 128, T, F32)
        XRW = [_p5("XRW0", 128, T, BF16)] * 2
        OST = [_p5("OST0", 128, T, F32)] * 2

        v_yg5 = 0
        for gg in range(8):
            v_yg5 = kb.dma("sync", YG5[:, gg * 8 * T:(gg + 1) * 8 * T],
                           _ap(ygsp.tensor, gg * 8 * 128 * T,
                               [[T, 128], [128 * T, 8], [1, T]]), s_yg5)
        ow_free = [0, 0]
        hdf_free = [0, 0]
        g2_guard = 0
        xrw_free = [0, 0]
        ost_free = [0, 0]
        p5bank_free = [0, 0, 0, 0]
        v_out_last = 0
        for dmc in range(32):
            wb = dmc % 2
            v_w = kb.dma("gpsimd", OW5[wb][:, :],
                         _ap(how.tensor, dmc * 128 * NGATE * 128,
                             [[NGATE * 128, 128], [1, NGATE * 128]]), s_w2[wb],
                         waits=[(s_pe, ow_free[wb])])
            v_hdf = kb.dma("sync", HDF[wb][:, :],
                           _ap(hdsp.tensor, dmc * 128 * 128,
                               [[128, 128], [D * 128, 8], [1, 128]]), s_hdsp2[wb],
                           waits=[(s_act, hdf_free[wb])])
            v_xrw = kb.dma("sync", XRW[wb][:, :],
                           _ap(xrawsp.tensor, dmc * 128 * T, [[T, 128], [1, T]]),
                           s_xrw, waits=[(s_dve, xrw_free[wb])])
            pb = (dmc % 2) * 2
            for fc in range(NGATE):
                w = []
                if fc == 0:
                    w = [(s_w2[wb], v_w), (s_yg5, v_yg5),
                         (s_dve, max(p5bank_free[pb], p5bank_free[pb + 1]))]
                for tt in range(2):
                    v_mm = kb.op("tensor",
                                 lambda e, wb=wb, fc=fc, tt=tt, pb=pb:
                                 e.matmul(banks[pb + tt][:, :],
                                          OW5[wb][:, fc * 128:(fc + 1) * 128],
                                          YG5[:, fc * T + tt * 512: fc * T + (tt + 1) * 512],
                                          start=(fc == 0), stop=(fc == NGATE - 1)),
                                 waits=(w if tt == 0 else ()),
                                 inc=(s_pe if fc == NGATE - 1 and tt == 1 else None))
            ow_free[wb] = v_mm
            hdf_free[wb] = kb.op("scalar", lambda e, wb=wb:
                                 e.activation(G2[:, :], HDF[wb][:, :], AF.Silu),
                                 waits=[(s_hdsp2[wb], v_hdf), (s_dve, g2_guard)])
            ob = dmc % 2
            for tt in range(2):
                kb.op("vector", lambda e, tt=tt, pb=pb, ob=ob:
                      e.tensor_tensor(OST[ob][:, tt * 512:(tt + 1) * 512],
                                      banks[pb + tt][:, :], G2[:, tt * 512:(tt + 1) * 512],
                                      OP.mult),
                      waits=([(s_pe, v_mm), (s_act, hdf_free[wb]),
                              (s_out, ost_free[ob])] if tt == 0 else ()))
            p5bank_free[pb] = p5bank_free[pb + 1] = kb.op(
                "vector", lambda e, ob=ob, wb=wb: e.tensor_tensor(
                    OST[ob][:, :], OST[ob][:, :], XRW[wb][:, :], OP.add),
                waits=[(s_xrw, v_xrw)])
            g2_guard = p5bank_free[pb]
            xrw_free[wb] = p5bank_free[pb]
            ost_free[ob] = kb.dma("sync", out_d[dmc * 128:(dmc + 1) * 128, :],
                                  OST[ob][:, :], s_out,
                                  waits=[(s_dve, p5bank_free[pb])])
            v_out_last = ost_free[ob]
        kb.wait("sync", s_out, v_out_last)
        scope_p5.close()

        block = es.enter_context(nc.Block())
        for e in ENGINES:
            fns = kb.ops[e]
            if fns:
                getattr(block, e)(lambda eng, fns=fns: [f(eng) for f in fns])

    return nc, dbg


def _pack_inputs(inputs):
    f32 = np.float32
    ipw = np.ascontiguousarray(inputs["L0_attn_in_proj_w"], dtype=f32)
    ffnw = np.ascontiguousarray(inputs["L0_ffn_W"], dtype=f32)
    outw = np.ascontiguousarray(inputs["L0_attn_out_w"], dtype=f32)
    shared = {
        "w1": np.ascontiguousarray(
            ipw.reshape(FCH, 128, 32, 128).transpose(0, 3, 2, 1)).reshape(FCH, 128, 32 * 128),
        "hw2": np.ascontiguousarray(
            ffnw.reshape(32, 128, 32, 128).transpose(0, 3, 2, 1)).reshape(32, 128, 32 * 128),
        "how": np.ascontiguousarray(
            outw.T.reshape(NGATE, 128, 32, 128).transpose(2, 1, 0, 3)).reshape(
                32, 128, NGATE * 128),
        "affn": np.ascontiguousarray(inputs["L0_ffn_A"], dtype=f32).reshape(32, 128, RANK),
        "w8": np.ascontiguousarray(ffnw[:NEXP].T).reshape(32, 128, NEXP),
        "bmat": np.ascontiguousarray(inputs["L0_ffn_Bmat"], dtype=f32),
        "limes": np.ascontiguousarray(inputs["L0_ffn_LiMEs"], dtype=f32),
        "bffn": np.ascontiguousarray(
            np.asarray(inputs["L0_ffn_b"], dtype=f32).reshape(32, 128).T),
        "convw": np.ascontiguousarray(
            np.asarray(inputs["L0_attn_conv_w"], dtype=f32).reshape(NXBC, 128, KC)
            .transpose(1, 0, 2)).reshape(128, NXBC * KC),
        "convb": np.ascontiguousarray(
            np.asarray(inputs["L0_attn_conv_b"], dtype=f32).reshape(NXBC, 128).T),
        "dtb": np.ascontiguousarray(
            np.asarray(inputs["L0_attn_dt_bias"], dtype=f32).reshape(128, 1)),
        "alog": np.ascontiguousarray(
            np.asarray(inputs["L0_attn_A_log"], dtype=f32).reshape(128, 1)),
        "dgrp": np.ascontiguousarray(
            np.repeat(np.asarray(inputs["L0_attn_D"], dtype=f32).reshape(64, 2).T, 64, axis=0)),
        "nwgrp": np.ascontiguousarray(
            np.asarray(inputs["L0_attn_norm_w"], dtype=f32).reshape(NGATE, 128).T),
        "idn": np.eye(128, dtype=f32),
    }
    j = np.arange(128)[:, None]
    i = np.arange(CS)[None, :]
    m0 = np.where(i >= j, 0.0, -1e6).astype(f32)
    m1 = np.where(i >= j + 128, 0.0, -1e6).astype(f32)
    shared["maskp"] = np.ascontiguousarray(np.concatenate([m0, m1], axis=1))

    x_full = np.asarray(inputs["L0_attn_input_states"], dtype=f32)
    maps = []
    for core in range(8):
        b, h = core // 2, core % 2
        m = dict(shared)
        m["x"] = np.ascontiguousarray(x_full[b, h * T:(h + 1) * T, :])
        m["ish0"] = np.full((128, 1), 1.0 - h, dtype=f32)
        m["ish1"] = np.full((128, 1), float(h), dtype=f32)
        maps.append(m)
    return maps


_NC_CACHE = {}


def _get_nc(debug=()):
    key = tuple(sorted(debug))
    if key not in _NC_CACHE:
        _NC_CACHE[key] = build_program(debug)
    return _NC_CACHE[key]


def kernel(**inputs):
    nc, _ = _get_nc()
    maps = _pack_inputs(inputs)
    res = run_bass_kernel_spmd(nc, maps, core_ids=list(range(8)))
    out = np.empty((B_, SEQ, D), dtype=np.float32)
    for core in range(8):
        b, h = core // 2, core % 2
        out[b, h * T:(h + 1) * T, :] = np.asarray(res.results[core]["out"]).T
    return out


# version 10
# speedup vs baseline: 1.0401x; 1.0401x over previous
import contextlib
import numpy as np
import ml_dtypes
import concourse.bass as bass
import concourse.mybir as mybir
from concourse.bass_utils import run_bass_kernel_spmd

F32 = mybir.dt.float32
F32R = mybir.dt.float32r
BF16 = mybir.dt.bfloat16
AF = mybir.ActivationFunctionType
OP = mybir.AluOpType
AP = bass.AP
AX = mybir.AxisListType.X

B_, SEQ, D = 4, 2048, 4096
T = 1024
H, PD, NS, G, KC, CS = 128, 64, 128, 8, 4, 256
NCHUNK = T // CS
INTER = H * PD
CONVC = INTER + 2 * G * NS
PROJ = INTER + CONVC + H
FCH = PROJ // 128
NXBC, NGATE = 80, 64
RANK, NEXP = 32, 8
GS = INTER // G
ENGINES = ("tensor", "vector", "scalar", "gpsimd", "sync")


class Sem:
    def __init__(self, handle):
        self.h = handle
        self.n = 0

    def inc(self, amt=1):
        self.n += amt
        return self.n

    @property
    def now(self):
        return self.n


class KB:

    def __init__(self):
        self.nc = bass.Bass("TRN2", target_bir_lowering=False)
        self.ops = {e: [] for e in ENGINES}
        self.dr = {}
        self.chain = {}
        self.guard = None
        self.guard_pending = set()

    def set_chain_sems(self, s_dve, s_act, s_pool):
        self.chain = {"vector": s_dve, "scalar": s_act, "gpsimd": s_pool}

    def scope_barrier(self, sems):
        cur = [(s, s.n) for s in sems if s.n > 0]
        if self.guard:
            old = dict((id(s), (s, v)) for s, v in self.guard)
            for s, v in cur:
                old[id(s)] = (s, v)
            cur = list(old.values())
        self.guard = cur
        self.guard_pending = set(ENGINES)

    def op(self, eng, fn, waits=(), inc=None, inc_amt=1, chain=True):
        allw = [(s, v) for (s, v) in waits if v > 0]
        if self.guard and eng in self.guard_pending:
            self.guard_pending.discard(eng)
            allw = list(self.guard) + allw
        ret = None
        cs = self.chain.get(eng) if chain else None
        if cs is not None and inc is None:
            if cs.n > 0:
                allw.append((cs, cs.n))
            ret = cs.inc(1)
            inc_h, inc_amt = cs.h, 1
        elif inc is not None:
            if cs is not None and cs.n > 0:
                allw.append((cs, cs.n))
            inc.inc(inc_amt)
            inc_h = inc.h
            ret = inc.now
        else:
            inc_h = None
        allw = tuple((s.h, v) for (s, v) in allw)

        def run(e, fn=fn, waits=allw, inc_h=inc_h, inc_amt=inc_amt):
            for hh, v in waits:
                e.wait_ge(hh, v)
            r = fn(e)
            if inc_h is not None:
                r.then_inc(inc_h, inc_amt)
        self.ops[eng].append(run)
        return ret

    def wait(self, eng, sem, val):
        if val > 0:
            self.ops[eng].append(lambda e, hh=sem.h, v=val: e.wait_ge(hh, v))

    def dma(self, eng, out_ap, in_ap, sem, waits=(), **kw):
        return self.op(eng, lambda e, o=out_ap, i=in_ap, kw=kw:
                       e.dma_start(out=o, in_=i, **kw),
                       waits=waits, inc=sem, inc_amt=16, chain=False)

    def din(self, name, shape, dt=F32):
        ap = self.nc.dram_tensor(name, shape, dt, kind="ExternalInput").ap()
        self.dr[name] = ap
        return ap

    def dout(self, name, shape, dt=F32):
        ap = self.nc.dram_tensor(name, shape, dt, kind="ExternalOutput").ap()
        self.dr[name] = ap
        return ap

    def dint(self, name, shape, dt):
        ap = self.nc.dram_tensor(name, shape, dt).ap()
        self.dr[name] = ap
        return ap


def _ap(handle, offset, dims):
    return AP(handle, offset, dims)


def build_program(debug=()):
    kb = KB()
    nc = kb.nc
    debug = set(debug)
    dbg = {}

    x_in = kb.din("x", [T, D])
    w1 = kb.din("w1", [FCH, 128, 32 * 128])
    hw2 = kb.din("hw2", [32, 128, 32 * 128])
    how = kb.din("how", [32, 128, NGATE * 128])
    affn = kb.din("affn", [32, 128, RANK])
    w8 = kb.din("w8", [32, 128, NEXP])
    bmat = kb.din("bmat", [RANK, D])
    limes = kb.din("limes", [NEXP, RANK])
    bffn = kb.din("bffn", [128, 32])
    convw = kb.din("convw", [128, NXBC * KC])
    convb = kb.din("convb", [128, NXBC])
    dtbias = kb.din("dtb", [128, 1])
    alog = kb.din("alog", [128, 1])
    dgrp = kb.din("dgrp", [128, NGATE])
    nwgrp = kb.din("nwgrp", [128, NGATE])
    idn_d = kb.din("idn", [128, 128])
    maskp_d = kb.din("maskp", [128, 2 * CS])
    ish0 = kb.din("ish0", [128, 1])
    ish1 = kb.din("ish1", [128, 1])

    out_d = kb.dout("out", [D, T], F32)

    def dtap(name, shape, dt=F32):
        if name in debug:
            dbg[name] = kb.dout("dbg_" + name, shape, dt)
            return dbg[name]
        return None

    xrawsp = kb.dint("xrawsp", [D, T], BF16)
    xbcsp = kb.dint("xbcsp", [NXBC, 128, T], BF16)
    gatesp = kb.dint("gatesp", [NGATE, 128, T], BF16)
    xtoksp = kb.dint("xtoksp", [NCHUNK, 2, 128, INTER], BF16)
    hdsp = kb.dint("hdsp", [8, D, 128], BF16)
    yrawsp = kb.dint("yrawsp", [NGATE, 128, T], F32)
    ygsp = kb.dint("ygsp", [NGATE, 128, T], BF16)
    cc1_in = kb.dint("cc1_in", [1, 16], F32)
    cc1_out = kb.dint("cc1_out", [1, 16], F32)
    cc2a_in = kb.dint("cc2a_in", [128, NXBC * 3], F32)
    cc2a_out = kb.dint("cc2a_out", [128, NXBC * 3], F32)
    cc2b_in = kb.dint("cc2b_in", [128, INTER], F32)
    cc2b_out = kb.dint("cc2b_out", [128, INTER], F32)

    es = contextlib.ExitStack()

    def sb(name, pdim, fdim, dt):
        return es.enter_context(nc.sbuf_tensor(name, [pdim, fdim], dt))

    def sem(name):
        return Sem(es.enter_context(nc.semaphore(name)))

    with es:
        banks = [es.enter_context(nc.psum_tensor(f"bank{i}", [128, 512], F32))
                 for i in range(8)]
        banksb = [b.ap().bitcast(BF16) for b in banks]

        s_ld = sem("s_ld")
        s_gd = sem("s_gd")
        s_pe = sem("s_pe")
        s_dve = sem("s_dve")
        s_act = sem("s_act")
        s_pool = sem("s_pool")
        s_sp = sem("s_sp")
        s_cc = sem("s_cc")
        s_cc2a = sem("s_cc2a")
        s_cc2b = sem("s_cc2b")
        s_xt2 = [sem("s_xt0"), sem("s_xt1")]
        s_xrs2 = [sem("s_xrs0"), sem("s_xrs1")]
        s_w2 = [sem("s_w0"), sem("s_w1")]
        s_hdsp2 = [sem("s_hdsp0"), sem("s_hdsp1")]
        s_spst2 = [sem("s_spst0"), sem("s_spst1")]
        s_cdf = sem("s_cdf")
        s_hsb2 = [sem("s_hsb0"), sem("s_hsb1")]
        s_bfm = sem("s_bfm")
        s_xtsp = sem("s_xtsp")
        s_slsp = sem("s_slsp")
        s_x42 = [sem("s_x40"), sem("s_x41")]
        s_bc = sem("s_bc")
        s_slbr = sem("s_slbr")
        s_hs4r = sem("s_hs4r")
        s_gt4r = sem("s_gt4r")
        s_fcb2 = [sem("s_fcb0"), sem("s_fcb1")]
        s_ygsp = sem("s_ygsp")
        s_yg5 = sem("s_yg5")
        s_xrw = sem("s_xrw")
        s_out = sem("s_out")
        kb.set_chain_sems(s_dve, s_act, s_pool)
        ALLS = [s_ld, s_gd, s_pe, s_dve, s_act, s_pool, s_sp,
                s_xt2[0], s_xt2[1], s_xrs2[0], s_xrs2[1], s_w2[0], s_w2[1],
                s_hdsp2[0], s_hdsp2[1], s_spst2[0], s_spst2[1], s_cdf,
                s_hsb2[0], s_hsb2[1], s_bfm, s_xtsp, s_slsp, s_x42[0], s_x42[1],
                s_bc, s_slbr, s_hs4r, s_gt4r, s_fcb2[0], s_fcb2[1], s_ygsp]

        IDN = sb("IDN", 128, 128, F32)
        IDNB = sb("IDNB", 128, 128, BF16)
        MASKP = sb("MASKP", 128, 2 * CS, BF16)
        CONVW = sb("CONVW", 128, NXBC * KC, F32)
        CONVB = sb("CONVB", 128, NXBC, F32)
        DTB = sb("DTB", 128, 1, F32)
        ANEG = sb("ANEG", 128, 1, F32)
        DGRP = sb("DGRP", 128, NGATE, F32)
        NWGRP = sb("NWGRP", 128, NGATE, F32)
        BFFN = sb("BFFN", 128, 32, F32)
        ONESB = sb("ONESB", 4, 128, BF16)
        ONESF = sb("ONESF", 1, 128, F32)
        ISH0 = sb("ISH0", 128, 1, F32)
        ISH1 = sb("ISH1", 128, 1, F32)
        BMATB = sb("BMATB", RANK, D, BF16)
        LIMESB = sb("LIMESB", NEXP, RANK, BF16)
        STT = sb("STT", 128, 8, F32)
        SCR2 = sb("SCR2", 128, T, F32)
        ACC = sb("ACC", 128, T, F32)
        ONES128 = sb("ONES128", 128, CS, F32)
        CR3 = sb("CR3", 128, NXBC * 3, F32)
        HALOS = sb("HALOS", 128, NXBC * 3, F32)
        XBCFIX = sb("XBCFIX", 128, NXBC * 3, BF16)
        ZPMT = sb("ZPMT", RANK, T, BF16)
        scope_dt = contextlib.ExitStack()
        _pdt = lambda name, p, f, dt: scope_dt.enter_context(
            nc.sbuf_tensor(name, [p, f], dt))
        DT = _pdt("DT", 128, T, F32)
        CUM = _pdt("CUM", 128, T, F32)
        LCT = _pdt("LCT", 128, 8 * 128, F32)
        WDTT = _pdt("WDTT", 128, 8 * 128, BF16)
        Q1 = _pdt("Q1", 128, T, BF16)
        Q2 = _pdt("Q2", 128, T, BF16)
        R2 = _pdt("R2", 128, T, BF16)
        ECUM = _pdt("ECUM", 128, T, BF16)
        PBF = _pdt("PBF", 1, 4 * 128, F32)

        kb.dma("sync", IDN[:, :], idn_d, s_ld)
        kb.dma("sync", CONVW[:, :], convw, s_ld)
        kb.dma("sync", CONVB[:, :], convb, s_ld)
        kb.dma("sync", DTB[:, :], dtbias, s_ld)
        kb.dma("sync", DGRP[:, :], dgrp, s_ld)
        kb.dma("sync", NWGRP[:, :], nwgrp, s_ld)
        kb.dma("sync", BFFN[:, :], bffn, s_ld)
        kb.dma("sync", ISH0[:, :], ish0, s_ld)
        kb.dma("sync", ISH1[:, :], ish1, s_ld)
        kb.dma("gpsimd", MASKP[:, :], maskp_d, s_gd)
        v_alog = kb.dma("sync", ANEG[:, :], alog, s_ld)
        kb.dma("gpsimd", BMATB[:, :], bmat, s_gd)
        kb.dma("gpsimd", LIMESB[:, :], limes, s_gd)
        v_idnb = kb.dma("gpsimd", IDNB[:, :], idn_d, s_gd)
        v_consts = s_ld.now

        kb.op("vector", lambda e: e.memset(ONESB[:, :], 1.0))
        v_ones = kb.op("vector", lambda e: e.memset(ONESF[:, :], 1.0), inc=s_dve)
        kb.op("scalar", lambda e: e.activation(ANEG[:, :], ANEG[:, :], AF.Exp),
              waits=[(s_ld, v_consts)])
        v_aneg = kb.op("scalar", lambda e: e.mul(ANEG[:, :], ANEG[:, :], -1.0),
                       inc=s_act)

        scope_xn = contextlib.ExitStack()
        XN = scope_xn.enter_context(nc.sbuf_tensor("XN", [128, 32 * T], BF16))
        scope_p0 = contextlib.ExitStack()
        XT = [scope_p0.enter_context(nc.sbuf_tensor(f"XT{i}", [128, D], F32))
              for i in range(2)]
        XTN = [scope_p0.enter_context(nc.sbuf_tensor(f"XTN{i}", [128, D], BF16))
               for i in range(2)]
        SQS = scope_p0.enter_context(nc.sbuf_tensor("SQS", [128, D], F32))

        XRB = [scope_p0.enter_context(nc.sbuf_tensor(f"XRB{i}", [128, 512], BF16))
               for i in range(2)]
        xr_bank_free = [0, 0]
        xt_free_v = [0, 0]
        xt_free_a = [0, 0]
        xt_free_pe = [0, 0]
        xtn_free = [0, 0]
        sqs_free = 0
        bank_free = {6: 0, 7: 0}
        v_xn = 0

        for i in range(8):
            b = i % 2
            v_x = kb.dma("sync", XT[b][:, :], x_in[i * 128:(i + 1) * 128, :], s_xt2[b],
                         waits=[(s_dve, xt_free_v[b]), (s_act, xt_free_a[b]),
                                (s_pe, xt_free_pe[b])])
            v_ms = kb.op("vector", lambda e, b=b: e.reduce_sum(STT[:, 0:1], XT[b][:, :], AX),
                         waits=[(s_xt2[b], v_x)], inc=s_dve)
            v_sq = kb.op("scalar", lambda e, b=b: e.activation(SQS[:, :], XT[b][:, :],
                                                               AF.Square),
                         waits=[(s_xt2[b], v_x), (s_dve, sqs_free)], inc=s_act)
            xt_free_a[b] = v_sq
            v_ss = kb.op("vector", lambda e: e.reduce_sum(STT[:, 1:2], SQS[:, :], AX),
                         waits=[(s_act, v_sq)], inc=s_dve)
            sqs_free = v_ss
            kb.op("vector", lambda e: e.tensor_scalar_mul(STT[:, 2:3], STT[:, 0:1], 1.0 / D))
            kb.op("vector", lambda e: e.tensor_mul(STT[:, 3:4], STT[:, 2:3], STT[:, 2:3]))
            kb.op("vector", lambda e: e.tensor_scalar(STT[:, 4:5], STT[:, 1:2], 1.0 / D,
                                                      1e-5, OP.mult, OP.add))
            v_var = kb.op("vector", lambda e: e.tensor_sub(STT[:, 4:5], STT[:, 4:5],
                                                           STT[:, 3:4]), inc=s_dve)
            v_sd = kb.op("scalar", lambda e: e.activation(STT[:, 5:6], STT[:, 4:5], AF.Sqrt),
                         waits=[(s_dve, v_var)], inc=s_act)
            kb.wait("vector", s_act, v_sd)
            kb.op("vector", lambda e: e.reciprocal(STT[:, 6:7], STT[:, 5:6]))
            v_nrm = kb.op("vector",
                          lambda e, b=b: e.tensor_scalar(XTN[b][:, :], XT[b][:, :],
                                                         STT[:, 2:3], STT[:, 6:7],
                                                         OP.subtract, OP.mult),
                          waits=[(s_pe, xtn_free[b])], inc=s_dve)
            xt_free_v[b] = v_nrm
            for gg in range(4):
                pb = 6 + (gg % 2)
                bk = banksb[pb]
                for k in range(8):
                    dc = gg * 8 + k
                    w = []
                    if k == 0:
                        w = [(s_dve, v_nrm), (s_gd, v_idnb), (s_dve, bank_free[pb])]
                    v_tp = kb.op("tensor",
                                 lambda e, bk=bk, k=k, dc=dc, b=b, st=(k == 0), sp=(k == 7):
                                 e.matmul(bk[:, k * 128:(k + 1) * 128],
                                          XTN[b][:, dc * 128:(dc + 1) * 128],
                                          IDNB[:, :], is_transpose=True,
                                          start=st, stop=sp),
                                 waits=w, inc=(s_pe if k == 7 else None))
                if gg == 3:
                    xtn_free[b] = v_tp
                xn_out = _ap(XN, gg * 8 * T + i * 128, [[32 * T, 128], [T, 8], [1, 128]])
                bank_free[pb] = kb.op("vector",
                                      lambda e, bk=bk, xo=xn_out:
                                      e.tensor_copy(xo, bk[:, :]),
                                      waits=[(s_pe, v_tp)], inc=s_dve)
                v_xn = bank_free[pb]
            for gg in range(8):
                pb4 = 4 + (gg % 2)
                for k in range(4):
                    dc = gg * 4 + k
                    w = [(s_xt2[b], v_x)]
                    if k == 0:
                        w.append((s_xrs2[pb4 - 4], xr_bank_free[pb4 - 4]))
                    v_tp4 = kb.op("tensor",
                                  lambda e, pb4=pb4, k=k, dc=dc, b=b,
                                  st=(k == 0), sp=(k == 3):
                                  e.matmul(banks[pb4][:, k * 128:(k + 1) * 128],
                                           XT[b][:, dc * 128:(dc + 1) * 128],
                                           IDN[:, :], is_transpose=True,
                                           start=st, stop=sp),
                                  waits=w, inc=(s_pe if k == 3 else None))
                xrb = gg % 2
                v_xre = kb.op("vector",
                              lambda e, pb4=pb4, xrb=xrb:
                              e.tensor_copy(XRB[xrb][:, :], banks[pb4][:, 0:512]),
                              waits=[(s_pe, v_tp4)])
                xr_bank_free[pb4 - 4] = kb.dma(
                    "sync",
                    _ap(xrawsp.tensor, gg * 4 * 128 * T + i * 128,
                        [[T, 128], [128 * T, 4], [1, 128]]),
                    XRB[xrb][:, :], s_xrs2[xrb], waits=[(s_dve, v_xre)])
            xt_free_pe[b] = v_tp4

        p0_dve_end = s_dve.now
        tp_xn = dtap("xn", [128, 32 * T], BF16)
        if tp_xn is not None:
            kb.dma("sync", tp_xn, _ap(XN, 0, [[32 * T, 128], [1, 32 * T]]), s_sp,
                   waits=[(s_dve, v_xn)])
        scope_p0.close()
        kb.scope_barrier([s_dve, s_act, s_pe, s_ld])

        scope_p1 = contextlib.ExitStack()
        _p1 = lambda name, p, f, dt: scope_p1.enter_context(
            nc.sbuf_tensor(name, [p, f], dt))
        AZ = _p1("AZ", 128, 32 * RANK, BF16)
        W8B = _p1("W8B", 128, 32 * NEXP, BF16)
        ZT = _p1("ZT", RANK, T, F32)
        ZTB = _p1("ZTB", RANK, T, BF16)
        H8T = _p1("H8T", NEXP, T, F32)
        DST = _p1("DST", NEXP, T // 2, F32)
        SCL = _p1("SCL", 1, 16, F32)
        GSCL = _p1("GSCL", 1, 16, F32)

        v_az = kb.dma("gpsimd", AZ[:, :], _ap(affn.tensor, 0, [[RANK, 128], [128 * RANK, 32], [1, RANK]]),
                      s_gd)
        v_w8 = kb.dma("gpsimd", W8B[:, :], _ap(w8.tensor, 0, [[NEXP, 128], [128 * NEXP, 32], [1, NEXP]]),
                      s_gd)

        for tt in range(2):
            for dc in range(32):
                kb.op("tensor",
                      lambda e, tt=tt, dc=dc:
                      e.matmul(banks[tt][0:RANK, :],
                               AZ[:, dc * RANK:(dc + 1) * RANK],
                               XN[:, dc * T + tt * 512: dc * T + (tt + 1) * 512],
                               start=(dc == 0), stop=(dc == 31)),
                      waits=([(s_gd, v_w8), (s_dve, v_xn)] if dc == 0 else ()),
                      inc=(s_pe if dc == 31 else None))
            v_z = s_pe.now
            kb.op("vector", lambda e, tt=tt: e.tensor_copy(ZT[:, tt * 512:(tt + 1) * 512],
                                                           banks[tt][0:RANK, :]),
                  waits=[(s_pe, v_z)])
            v_ztb = kb.op("vector", lambda e, tt=tt: e.tensor_copy(ZTB[:, tt * 512:(tt + 1) * 512],
                                                                   banks[tt][0:RANK, :]),
                          inc=s_dve)
        v_zt = v_ztb

        for tt in range(2):
            for dc in range(32):
                kb.op("tensor",
                      lambda e, tt=tt, dc=dc:
                      e.matmul(banks[2 + tt][0:NEXP, :],
                               W8B[:, dc * NEXP:(dc + 1) * NEXP],
                               XN[:, dc * T + tt * 512: dc * T + (tt + 1) * 512],
                               start=(dc == 0), stop=(dc == 31)),
                      waits=([(s_gd, v_w8)] if dc == 0 else ()),
                      inc=(s_pe if dc == 31 else None))
            v_h8 = s_pe.now
            kb.op("vector",
                  lambda e, tt=tt: e.tensor_scalar(H8T[:, tt * 512:(tt + 1) * 512],
                                                   banks[2 + tt][0:NEXP, :],
                                                   BFFN[0:NEXP, 0:1], None, OP.add),
                  waits=[(s_pe, v_h8), (s_ld, v_consts)], inc=s_dve)
        v_h8t = s_dve.now

        LIMT = _p1("LIMT", RANK, NEXP, BF16)
        kb.op("tensor", lambda e: e.matmul(banksb[4][0:RANK, 0:NEXP], LIMESB[:, :],
                                           IDNB[0:NEXP, 0:NEXP], is_transpose=True,
                                           start=True, stop=True),
              waits=[(s_gd, v_idnb), (s_dve, p0_dve_end)], inc=s_pe)
        v_limt_pe = s_pe.now
        v_limt = kb.op("vector", lambda e: e.tensor_copy(LIMT[:, :], banksb[4][0:RANK, 0:NEXP]),
                       waits=[(s_pe, v_limt_pe)], inc=s_dve)
        zanch = _ap(ZTB, 1, [[T, RANK], [2, T // 2]])
        kb.op("tensor", lambda e: e.matmul(banks[5][0:NEXP, :], LIMT[:, :], zanch,
                                           start=True, stop=True),
              waits=[(s_dve, max(v_limt, v_zt))], inc=s_pe)
        v_ds_pe = s_pe.now
        v_ds = kb.op("vector", lambda e: e.tensor_copy(DST[:, :], banks[5][0:NEXP, :]),
                     waits=[(s_pe, v_ds_pe)], inc=s_dve)

        h8anch = _ap(H8T, 1, [[T, NEXP], [2, T // 2]])
        kb.wait("vector", s_dve, v_h8t)
        kb.op("vector", lambda e: e.reduce_max(STT[0:NEXP, 0:1], h8anch, AX,
                                               apply_absolute_value=True))
        v_lm = kb.op("vector", lambda e: e.reduce_max(STT[0:NEXP, 1:2], DST[:, :], AX,
                                                      apply_absolute_value=True),
                     inc=s_dve)
        SCRO = _p1("SCRO", 1, 16, F32)
        v_t = kb.dma("gpsimd", SCRO[0:1, :],
                     _ap(STT, 0, [[8, 8], [1, 2]]), s_gd,
                     waits=[(s_dve, v_lm)])
        kb.wait("vector", s_gd, v_t)
        kb.op("vector", lambda e: e.memset(SCL[:, :], 0.0))
        kb.op("vector", lambda e: e.reduce_max(SCL[0:1, 0:1],
                                               _ap(SCRO, 0, [[16, 1], [2, 8]]), AX))
        v_scl = kb.op("vector", lambda e: e.reduce_max(SCL[0:1, 1:2],
                                                       _ap(SCRO, 1, [[16, 1], [2, 8]]), AX),
                      inc=s_dve)
        v_c1w = kb.dma("gpsimd", cc1_in, SCL[0:1, :], s_gd, waits=[(s_dve, v_scl)])
        kb.op("gpsimd", lambda e: e.collective_compute(
            "AllReduce", OP.max, replica_groups=[list(range(8))],
            ins=[cc1_in.opt()], outs=[cc1_out.opt()]),
            waits=[(s_gd, v_c1w)], inc=s_cc)
        v_c1 = s_cc.now
        kb.wait("gpsimd", s_cc, v_c1)
        v_gscl = kb.dma("gpsimd", GSCL[0:1, :], cc1_out, s_gd)

        tp_sc = dtap("scales", [1, 16])
        if tp_sc is not None:
            kb.dma("sync", tp_sc, GSCL[0:1, :], s_sp, waits=[(s_gd, v_gscl)])

        kb.wait("vector", s_gd, v_gscl)
        kb.op("vector", lambda e: e.tensor_scalar_max(GSCL[0:1, 0:2], GSCL[0:1, 0:2], 1e-6))
        kb.op("vector", lambda e: e.reciprocal(GSCL[0:1, 2:4], GSCL[0:1, 0:2]))
        v_rr = kb.op("vector", lambda e: e.tensor_scalar_mul(GSCL[0:1, 2:4], GSCL[0:1, 2:4],
                                                             0.5), inc=s_dve)
        kb.op("tensor", lambda e: e.matmul(banks[0][0:NEXP, 0:2],
                                           ONESF[0:1, 0:NEXP], GSCL[0:1, 2:4],
                                           start=True, stop=True),
              waits=[(s_dve, max(v_rr, v_ones))], inc=s_pe)
        v_rb_pe = s_pe.now
        RSC = _p1("RSC", NEXP, 2, F32)
        v_rsc = kb.op("vector", lambda e: e.tensor_copy(RSC[:, :], banks[0][0:NEXP, 0:2]),
                      waits=[(s_pe, v_rb_pe)], inc=s_dve)

        LGT = _p1("LGT", NEXP, T // 2, F32)
        kb.op("vector", lambda e: e.tensor_scalar_mul(LGT[:, :], h8anch, RSC[:, 0:1]))
        v_lg = kb.op("vector", lambda e: e.scalar_tensor_tensor(LGT[:, :], DST[:, :],
                                                                RSC[:, 1:2], LGT[:, :],
                                                                OP.mult, OP.add),
                     inc=s_dve)
        for a in range(4):
            kb.op("tensor", lambda e, a=a: e.matmul(banks[1][:, a * 8:(a + 1) * 8],
                                                    LGT[:, a * 128:(a + 1) * 128],
                                                    IDN[0:NEXP, 0:NEXP], is_transpose=True,
                                                    start=(a == 0), stop=(a == 3)),
                  waits=([(s_dve, v_lg), (s_ld, v_consts)] if a == 0 else ()),
                  inc=(s_pe if a == 3 else None))
        v_lgt = s_pe.now

        PRB = _p1("PRB", 128, 4 * NEXP, F32)
        WTK = _p1("WTK", 128, 4 * NEXP, F32)
        GST = _p1("GST", 128, 12, F32)
        kb.wait("vector", s_pe, v_lgt)
        kb.wait("scalar", s_pe, v_lgt)
        for a in range(4):
            pa = banks[1][:, a * 8:(a + 1) * 8]
            prb = PRB[:, a * 8:(a + 1) * 8]
            wtk = WTK[:, a * 8:(a + 1) * 8]
            v_mx = kb.op("vector", lambda e, pa=pa: e.reduce_max(GST[:, 0:1], pa, AX,
                                                                 negate=True), inc=s_dve)
            v_ex = kb.op("scalar", lambda e, pa=pa, prb=prb:
                         e.activation(prb, pa, AF.Exp, bias=GST[:, 0:1],
                                      accum_out=GST[:, 1:2]),
                         waits=[(s_dve, v_mx)], inc=s_act)
            kb.wait("vector", s_act, v_ex)
            kb.op("vector", lambda e: e.reciprocal(GST[:, 2:3], GST[:, 1:2]))
            kb.op("vector", lambda e, prb=prb: e.tensor_scalar_mul(prb, prb, GST[:, 2:3]))
            kb.op("vector", lambda e, prb=prb: e.reduce_max(GST[:, 3:4], prb, AX))
            kb.op("vector", lambda e, prb=prb: e.tensor_scalar(GST[:, 4:12][:, 0:8], prb,
                                                               GST[:, 3:4], None, OP.is_ge))
            kb.op("vector", lambda e: e.tensor_scalar(SCR2[:, 0:8], GST[:, 4:12], 1.0, 0.0,
                                                      OP.mult, OP.add,
                                                      accum_out=GST[:, 2:3]))
            kb.op("vector", lambda e, prb=prb:
                  e.scalar_tensor_tensor(SCR2[:, 8:16], GST[:, 4:12], -1e30, prb,
                                         OP.mult, OP.add))
            kb.op("vector", lambda e: e.reduce_max(GST[:, 5:6], SCR2[:, 8:16], AX))
            kb.op("vector", lambda e: e.tensor_scalar(GST[:, 6:7], GST[:, 2:3], 1.5, None,
                                                      OP.is_ge))
            kb.op("vector", lambda e: e.tensor_sub(GST[:, 7:8], GST[:, 3:4], GST[:, 5:6]))
            kb.op("vector", lambda e: e.tensor_mul(GST[:, 7:8], GST[:, 7:8], GST[:, 6:7]))
            kb.op("vector", lambda e: e.tensor_add(GST[:, 8:9], GST[:, 5:6], GST[:, 7:8]))
            v_thr = kb.op("vector", lambda e: e.tensor_scalar_mul(GST[:, 9:10], GST[:, 8:9],
                                                                  -2.0), inc=s_dve)
            v_msk = kb.op("scalar", lambda e, prb=prb:
                          e.activation(SCR2[:, 16:24], prb, AF.Sigmoid, scale=2.0,
                                       bias=GST[:, 9:10]),
                          waits=[(s_dve, v_thr)], inc=s_act)
            kb.wait("vector", s_act, v_msk)
            kb.op("vector", lambda e, prb=prb, wtk=wtk: e.tensor_mul(wtk, prb, SCR2[:, 16:24]))
            kb.op("vector", lambda e, wtk=wtk: e.tensor_scalar(SCR2[:, 24:32], wtk, 1.0, 0.0,
                                                               OP.mult, OP.add,
                                                               accum_out=GST[:, 10:11]))
            kb.op("vector", lambda e: e.tensor_scalar_add(GST[:, 10:11], GST[:, 10:11], 1e-9))
            kb.op("vector", lambda e: e.reciprocal(GST[:, 11:12], GST[:, 10:11]))
            v_wtk = kb.op("vector", lambda e, wtk=wtk:
                          e.tensor_scalar_mul(wtk, wtk, GST[:, 11:12]), inc=s_dve)

        tp_w = dtap("weights", [128, 32])
        if tp_w is not None:
            kb.dma("sync", tp_w, WTK[:, :], s_sp, waits=[(s_dve, v_wtk)])

        WTE = _p1("WTE", NEXP, T // 2, BF16)
        WFULL = _p1("WFULL", NEXP, T, BF16)
        for a in range(4):
            kb.op("tensor", lambda e, a=a: e.matmul(banks[2][0:NEXP, a * 128:(a + 1) * 128],
                                                    WTK[:, a * 8:(a + 1) * 8],
                                                    IDN[:, :], is_transpose=True,
                                                    start=(a == 0), stop=(a == 3)),
                  waits=([(s_dve, v_wtk)] if a == 0 else ()),
                  inc=(s_pe if a == 3 else None))
        v_wte_pe = s_pe.now
        v_wte = kb.op("vector", lambda e: e.tensor_copy(WTE[:, :], banks[2][0:NEXP, 0:512]),
                      waits=[(s_pe, v_wte_pe)], inc=s_dve)
        v_wf = kb.op("vector", lambda e: e.tensor_copy(
            WFULL[:, :], _ap(WTE, 0, [[T // 2, NEXP], [1, T // 2], [0, 2]])),
            inc=s_dve)

        for tt in range(2):
            pmb = banks[3 + tt]
            kb.op("tensor", lambda e, tt=tt, pmb=pmb: e.matmul(pmb[0:RANK, :], LIMESB[:, :],
                                                               WFULL[:, tt * 512:(tt + 1) * 512],
                                                               start=True, stop=True),
                  waits=([(s_dve, v_wf)] if tt == 0 else ()), inc=s_pe)
            v_pm = s_pe.now
            kb.op("vector", lambda e, tt=tt, pmb=pmb:
                  e.scalar_tensor_tensor(ZPMT[:, tt * 512:(tt + 1) * 512],
                                         pmb[0:RANK, :], 0.5,
                                         ZT[:, tt * 512:(tt + 1) * 512],
                                         OP.mult, OP.mult),
                  waits=[(s_pe, v_pm)], inc=s_dve)
        v_zpm = s_dve.now

        tp_z = dtap("zT", [RANK, T])
        if tp_z is not None:
            kb.dma("sync", tp_z, ZT[:, :], s_sp, waits=[(s_dve, v_zpm)])
        tp_zpm = dtap("zpmT", [RANK, T], BF16)
        if tp_zpm is not None:
            kb.dma("sync", tp_zpm, ZPMT[:, :], s_sp, waits=[(s_dve, v_zpm)])
        tp_h8 = dtap("H8T", [NEXP, T])
        if tp_h8 is not None:
            kb.dma("sync", tp_h8, H8T[:, :], s_sp, waits=[(s_dve, v_h8t)])

        scope_p1.close()
        kb.scope_barrier([s_dve, s_act, s_pe, s_gd, s_ld])

        scope_p2 = contextlib.ExitStack()
        W2T = [scope_p2.enter_context(nc.sbuf_tensor(f"W2T{i}", [128, 32 * 128], BF16))
               for i in range(2)]
        HDS = [scope_p2.enter_context(nc.sbuf_tensor(f"HDS{i}", [128, T], BF16))
               for i in range(2)]
        w2_free = [0, 0]
        hds_free = [0, 0]
        p2bank_free = [0, 0, 0, 0]
        v_hd_spills = 0
        for dmc in range(32):
            wb = dmc % 2
            v_w = kb.dma("gpsimd", W2T[wb][:, :],
                         _ap(hw2.tensor, dmc * 128 * 4096, [[4096, 128], [1, 4096]]),
                         s_w2[wb], waits=[(s_pe, w2_free[wb])])
            pb = (dmc % 2) * 2
            for dc in range(32):
                w = []
                if dc == 0:
                    w = [(s_w2[wb], v_w), (s_dve, max(p2bank_free[pb], p2bank_free[pb + 1],
                                                      v_xn))]
                for tt in range(2):
                    kb.op("tensor",
                          lambda e, wb=wb, dc=dc, tt=tt, pb=pb:
                          e.matmul(banks[pb + tt][:, :],
                                   W2T[wb][:, dc * 128:(dc + 1) * 128],
                                   XN[:, dc * T + tt * 512: dc * T + (tt + 1) * 512],
                                   start=(dc == 0), stop=False),
                          waits=(w if tt == 0 else ()))
            for tt in range(2):
                v_mm = kb.op("tensor",
                             lambda e, dmc=dmc, tt=tt, pb=pb:
                             e.matmul(banks[pb + tt][:, :],
                                      BMATB[:, dmc * 128:(dmc + 1) * 128],
                                      ZPMT[:, tt * 512:(tt + 1) * 512],
                                      start=False, stop=True),
                             waits=([(s_dve, v_zpm)] if dmc == 0 and tt == 0 else ()),
                             inc=s_pe)
            w2_free[wb] = v_mm
            hb = dmc % 2
            for tt in range(2):
                p2bank_free[pb + tt] = kb.op(
                    "vector",
                    lambda e, dmc=dmc, tt=tt, pb=pb, hb=hb:
                    e.tensor_scalar(HDS[hb][:, tt * 512:(tt + 1) * 512],
                                    banks[pb + tt][:, :],
                                    BFFN[:, dmc:dmc + 1], None, OP.add),
                    waits=[(s_pe, v_mm), (s_hdsp2[hb], hds_free[hb])])
            hd_out = _ap(hdsp.tensor, dmc * 128 * 128,
                         [[128, 128], [D * 128, 8], [1, 128]])
            hds_free[hb] = kb.dma("sync", hd_out, HDS[hb][:, :], s_hdsp2[hb],
                                  waits=[(s_dve, p2bank_free[pb + 1])])
            v_hd_spills = hds_free[hb]

        tp_hd = dtap("hd", [8 * D * 128], BF16)
        if tp_hd is not None:
            kb.dma("sync", tp_hd,
                   _ap(hdsp.tensor, 0, [[8 * D * 128, 1], [1, 8 * D * 128]]), s_sp,
                   waits=[(s_hdsp2[0], hds_free[0]), (s_hdsp2[1], hds_free[1])])

        scope_p2.close()
        kb.scope_barrier([s_dve, s_act, s_pe, s_gd, s_sp])

        scope_p3a = contextlib.ExitStack()
        _p3 = lambda name, p, f, dt: scope_p3a.enter_context(
            nc.sbuf_tensor(name, [p, f], dt))
        W1T = [_p3(f"W1T{i}", 128, 32 * 128, BF16) for i in range(2)]
        CIN = [_p3(f"CIN{i}", 128, 1027, BF16) for i in range(2)]
        SPST = [_p3(f"SPST{i}", 128, T, BF16) for i in range(2)]
        DTU = _p3("DTU", 128, T, F32)
        DA = _p3("DA", 128, T, F32)
        LNDT = _p3("LNDT", 128, T, F32)
        LC = _p3("LC", 128, T, F32)

        kb.op("vector", lambda e: e.memset(ONES128[:, :], 1.0))
        kb.op("vector", lambda e: e.memset(CIN[0][:, 0:3], 0.0))
        v_cin0 = kb.op("vector", lambda e: e.memset(CIN[1][:, 0:3], 0.0))

        w1_free = [0, 0]
        spst_free = [0, 0]
        p3bank_free = [None, None, None, None]
        cin_free = [0, 0]
        fc_order = [144] + list(range(64, 144)) + list(range(0, 64))
        v_dt_pipe = 0
        v_lct = 0
        v_ecum = 0
        v_halos = 0
        v_xbc_spill = 0
        v_gate_spill = 0
        v_q_done = 0

        for idx, fc in enumerate(fc_order):
            wb = idx % 2
            v_w = kb.dma("gpsimd", W1T[wb][:, :],
                         _ap(w1.tensor, fc * 128 * 4096, [[4096, 128], [1, 4096]]),
                         s_w2[wb], waits=[(s_pe, w1_free[wb])])
            pb = (idx % 2) * 2
            for dc in range(32):
                w = []
                if dc == 0:
                    w = [(s_w2[wb], v_w)]
                    for bf in (p3bank_free[pb], p3bank_free[pb + 1]):
                        if bf is not None:
                            w.append(bf)
                for tt in range(2):
                    v_mm = kb.op("tensor",
                                 lambda e, wb=wb, dc=dc, tt=tt, pb=pb:
                                 e.matmul(banks[pb + tt][:, :],
                                          W1T[wb][:, dc * 128:(dc + 1) * 128],
                                          XN[:, dc * T + tt * 512: dc * T + (tt + 1) * 512],
                                          start=(dc == 0), stop=(dc == 31)),
                                 waits=(w if tt == 0 else ()),
                                 inc=(s_pe if dc == 31 and tt == 1 else None))
            w1_free[wb] = v_mm

            if fc == 144:
                for tt in range(2):
                    v_u = kb.op("vector", lambda e, tt=tt, pb=pb:
                                e.tensor_scalar(DTU[:, tt * 512:(tt + 1) * 512],
                                                banks[pb + tt][:, :], DTB[:, 0:1], None,
                                                OP.add),
                                waits=[(s_pe, v_mm), (s_ld, v_consts)])
                p3bank_free[pb] = p3bank_free[pb + 1] = (s_dve, v_u)
                v_abs = kb.op("scalar", lambda e: e.activation(SCR2[:, :], DTU[:, :],
                                                               AF.Abs),
                              waits=[(s_dve, v_u)])
                v_e = kb.op("scalar", lambda e: e.activation(SCR2[:, :], SCR2[:, :],
                                                             AF.Exp, scale=-1.0))
                v_l = kb.op("scalar", lambda e: e.activation(SCR2[:, :], SCR2[:, :],
                                                             AF.Ln, bias=1.0))
                v_r = kb.op("vector", lambda e: e.tensor_scalar_max(DA[:, :], DTU[:, :],
                                                                    0.0))
                kb.wait("vector", s_act, v_l)
                v_dt = kb.op("vector", lambda e: e.tensor_add(DT[:, :], DA[:, :],
                                                              SCR2[:, :]))
                v_da = kb.op("vector", lambda e: e.tensor_scalar_mul(DA[:, :], DT[:, :],
                                                                     ANEG[:, 0:1]),
                             waits=[(s_act, v_aneg)])
                for c in range(NCHUNK):
                    v_cum = kb.op("vector", lambda e, c=c:
                                  e.tensor_tensor_scan(CUM[:, c * CS:(c + 1) * CS],
                                                       ONES128[:, :],
                                                       DA[:, c * CS:(c + 1) * CS],
                                                       0.0, OP.mult, OP.add))
                kb.wait("scalar", s_dve, v_dt)
                v_lndt = kb.op("scalar", lambda e: e.activation(LNDT[:, :], DT[:, :],
                                                                AF.Ln))
                v_ecum = kb.op("scalar", lambda e, v=v_cum: e.activation(
                    ECUM[:, :], CUM[:, :], AF.Exp), waits=[(s_dve, v_cum)])
                kb.wait("vector", s_act, v_lndt)
                v_lc = kb.op("vector", lambda e: e.tensor_sub(LC[:, :], LNDT[:, :],
                                                              CUM[:, :]))
                for g8 in range(8):
                    tb = 6 + (g8 % 2)
                    kb.op("tensor", lambda e, g8=g8, tb=tb:
                          e.matmul(banks[tb][:, 0:128], LC[:, g8 * 128:(g8 + 1) * 128],
                                   IDN[:, :], is_transpose=True, start=True, stop=True),
                          waits=[(s_dve, v_lc if g8 < 2 else 0),
                                 (s_dve, bank_free[tb])],
                          inc=s_pe)
                    v_tpe = s_pe.now
                    bank_free[tb] = kb.op("vector", lambda e, g8=g8, tb=tb:
                                          e.tensor_copy(LCT[:, g8 * 128:(g8 + 1) * 128],
                                                        banks[tb][:, 0:128]),
                                          waits=[(s_pe, v_tpe)])
                v_lct = bank_free[tb]
                kb.op("vector", lambda e: e.tensor_copy(Q1[:, :], CUM[:, :]))
                kb.op("vector", lambda e: e.tensor_tensor(SCR2[:, :], CUM[:, :], Q1[:, :],
                                                          OP.subtract))
                kb.op("vector", lambda e: e.tensor_copy(Q2[:, :], SCR2[:, :]))
                kb.op("vector", lambda e: e.tensor_tensor(SCR2[:, :], SCR2[:, :], Q2[:, :],
                                                          OP.subtract))
                v_q_done = kb.op("vector", lambda e: e.tensor_copy(R2[:, :], SCR2[:, :]))
                v_dt_pipe = v_q_done
            elif fc >= 64:
                cc = fc - 64
                cb = idx % 2
                for tt in range(2):
                    v_cc = kb.op("scalar", lambda e, tt=tt, pb=pb, cb=cb:
                                 e.copy(CIN[cb][:, 3 + tt * 512: 3 + (tt + 1) * 512],
                                        banks[pb + tt][:, :]),
                                 waits=([(s_pe, v_mm), (s_dve, cin_free[cb])]
                                        if tt == 0 else ()))
                p3bank_free[pb] = p3bank_free[pb + 1] = (s_act, v_cc)
                v_t0 = kb.op("vector", lambda e, cb=cb, cc=cc:
                             e.tensor_scalar_mul(ACC[:, :], CIN[cb][:, 0:T],
                                                 CONVW[:, cc * KC:cc * KC + 1]),
                             waits=[(s_act, v_cc)])
                for k in range(1, 4):
                    v_tk = kb.op("vector", lambda e, cb=cb, cc=cc, k=k:
                                 e.scalar_tensor_tensor(ACC[:, :], CIN[cb][:, k:k + T],
                                                        CONVW[:, cc * KC + k:cc * KC + k + 1],
                                                        ACC[:, :], OP.mult, OP.add))
                cin_free[cb] = v_tk
                kb.op("vector", lambda e, cc=cc:
                      e.tensor_copy(CR3[:, cc * 3:(cc + 1) * 3], ACC[:, 0:3]))
                v_halo = kb.op("vector", lambda e, cc=cc, cb=cb:
                               e.tensor_copy(HALOS[:, cc * 3:(cc + 1) * 3],
                                             CIN[cb][:, 1024:1027]))
                if cc == NXBC - 1:
                    v_halos = v_halo
                sbuf_i = idx % 2
                v_silu = kb.op("scalar", lambda e, cc=cc, sbuf_i=sbuf_i:
                               e.activation(SPST[sbuf_i][:, :], ACC[:, :], AF.Silu,
                                            bias=CONVB[:, cc:cc + 1]),
                               waits=[(s_dve, v_halo),
                                      (s_spst2[sbuf_i], spst_free[sbuf_i])])
                spst_free[sbuf_i] = kb.dma("sync",
                                           _ap(xbcsp.tensor, cc * 128 * T,
                                               [[T, 128], [1, T]]),
                                           SPST[sbuf_i][:, :], s_spst2[sbuf_i],
                                           waits=[(s_act, v_silu)])
                v_xbc_spill = spst_free[sbuf_i]
            else:
                sbuf_i = idx % 2
                for tt in range(2):
                    v_gn = kb.op("scalar", lambda e, tt=tt, pb=pb, sbuf_i=sbuf_i:
                                 e.copy(SPST[sbuf_i][:, tt * 512:(tt + 1) * 512],
                                        banks[pb + tt][:, :]),
                                 waits=([(s_pe, v_mm),
                                         (s_spst2[sbuf_i], spst_free[sbuf_i])]
                                        if tt == 0 else ()))
                p3bank_free[pb] = p3bank_free[pb + 1] = (s_act, v_gn)
                spst_free[sbuf_i] = kb.dma("sync",
                                           _ap(gatesp.tensor, fc * 128 * T,
                                               [[T, 128], [1, T]]),
                                           SPST[sbuf_i][:, :], s_spst2[sbuf_i],
                                           waits=[(s_act, v_gn)])
                v_gate_spill = spst_free[sbuf_i]

        tp_xbc = dtap("xbcsp", [NXBC * 128 * T], BF16)
        if tp_xbc is not None:
            kb.dma("sync", tp_xbc,
                   _ap(xbcsp.tensor, 0, [[NXBC * 128 * T, 1], [1, NXBC * 128 * T]]),
                   s_sp, waits=[(s_spst2[0], spst_free[0]), (s_spst2[1], spst_free[1])])
        tp_dt = dtap("dt", [128, T])
        if tp_dt is not None:
            kb.dma("sync", tp_dt, DT[:, :], s_sp, waits=[(s_dve, v_dt_pipe)])
        tp_cum = dtap("cum", [128, T])
        if tp_cum is not None:
            kb.dma("sync", tp_cum, CUM[:, :], s_sp, waits=[(s_dve, v_dt_pipe)])

        v_hs = kb.op("vector", lambda e: e.tensor_scalar_mul(HALOS[:, :], HALOS[:, :],
                                                             ISH0[:, 0:1]),
                     waits=[(s_dve, v_halos), (s_ld, v_consts)])
        v_c2aw = kb.dma("gpsimd", cc2a_in, HALOS[:, :], s_gd, waits=[(s_dve, v_hs)])
        kb.op("gpsimd", lambda e: e.collective_compute(
            "AllReduce", OP.add, replica_groups=[[0, 1], [2, 3], [4, 5], [6, 7]],
            ins=[cc2a_in.opt()], outs=[cc2a_out.opt()]),
            waits=[(s_gd, v_c2aw)], inc=s_cc2a, chain=False)
        kb.wait("gpsimd", s_cc2a, 1)
        v_hb = kb.dma("gpsimd", HALOS[:, :], cc2a_out, s_gd)
        v_f = kb.op("vector", lambda e: e.tensor_scalar_mul(HALOS[:, :], HALOS[:, :],
                                                            ISH1[:, 0:1]),
                    waits=[(s_gd, v_hb)])
        for t in range(3):
            for k in range(0, 3 - t):
                kb.op("vector", lambda e, t=t, k=k: e.tensor_tensor(
                    _ap(SCR2, 0, [[T, 128], [3, NXBC]]),
                    _ap(HALOS, t + k, [[NXBC * 3, 128], [3, NXBC]]),
                    _ap(CONVW, k, [[NXBC * KC, 128], [KC, NXBC]]),
                    OP.mult))
                kb.op("vector", lambda e, t=t: e.tensor_tensor(
                    _ap(CR3, t, [[NXBC * 3, 128], [3, NXBC]]),
                    _ap(CR3, t, [[NXBC * 3, 128], [3, NXBC]]),
                    _ap(SCR2, 0, [[T, 128], [3, NXBC]]),
                    OP.add))
        v_fb = kb.op("vector", lambda e: e.tensor_tensor(
            CR3[:, :], CR3[:, :],
            _ap(CONVB, 0, [[NXBC, 128], [1, NXBC], [0, 3]]), OP.add))
        v_fix = kb.op("scalar", lambda e: e.activation(XBCFIX[:, :], CR3[:, :], AF.Silu),
                      waits=[(s_dve, v_fb)])

        tp_fix = dtap("xbcfix", [128, NXBC * 3], BF16)
        if tp_fix is not None:
            kb.dma("sync", tp_fix, XBCFIX[:, :], s_sp, waits=[(s_act, v_fix)])

        scope_p3a.close()
        scope_xn.close()
        kb.scope_barrier([s_dve, s_act, s_pe, s_gd, s_sp, s_cc])

        slsp = kb.dint("slsp", [NCHUNK, 128, INTER], BF16)
        scope_p3b = contextlib.ExitStack()
        _pb = lambda name, p, f, dt: scope_p3b.enter_context(
            nc.sbuf_tensor(name, [p, f], dt))
        S = _pb("S", 128, INTER, F32)
        HSB = [_pb(f"HSB{i}", 128, 2048, BF16) for i in range(2)]
        X_TOK = [_pb(f"XTOK{i}", 128, INTER, BF16) for i in range(2)]
        XW = [_pb(f"XW{i}", 128, INTER, BF16) for i in range(2)]
        BFM3 = _pb("BFM3", 128, 2048, BF16)
        B_TOK = [_pb(f"BTOK{i}", 128, 1024, BF16) for i in range(2)]
        SB16 = _pb("SB16", 128, INTER, BF16)
        CDF = _pb("CDF", 1, 4 * 128, F32)
        CDECB = _pb("CDECB", 128, 128, F32)

        v_szero = kb.op("vector", lambda e: e.memset(S[:, :], 0.0))
        hsb_free = [(None, 0), (None, 0)]
        state_bank_free = [None] * 8
        v_cdf = 0
        v_xtok_spill = {}
        v_sl_spill = {}
        v_states_last = 0
        v_xtspill_last = 0
        v_slspill_last = 0

        for c in range(NCHUNK):
            v_g = kb.dma("sync", _ap(CDF, c * 128, [[512, 1], [1, 128]]),
                         _ap(CUM, c * CS + (CS - 1), [[T, 128], [1, 1]]), s_cdf,
                         waits=[(s_dve, v_dt_pipe)])
            v_cdf = kb.op("scalar", lambda e, c=c: e.activation(
                _ap(CDF, c * 128, [[512, 1], [1, 128]]),
                _ap(CDF, c * 128, [[512, 1], [1, 128]]), AF.Exp),
                waits=[(s_cdf, v_g)])

            v_dl = kb.op("scalar", lambda e, c=c: e.activation(
                SCR2[:, 0:CS], CUM[:, c * CS:(c + 1) * CS], AF.Exp, scale=-1.0,
                bias=CUM[:, c * CS + CS - 1:c * CS + CS]),
                waits=[(s_dve, v_dt_pipe)])
            v_w = kb.op("vector", lambda e, c=c: e.tensor_mul(
                SCR2[:, CS:2 * CS], SCR2[:, 0:CS], DT[:, c * CS:(c + 1) * CS]),
                waits=[(s_act, v_dl)])
            for jt in range(2):
                tb = 6 + jt
                v_tp = kb.op("tensor", lambda e, jt=jt, tb=tb:
                             e.matmul(banks[tb][:, 0:128],
                                      SCR2[:, CS + jt * 128:CS + (jt + 1) * 128],
                                      IDN[:, :], is_transpose=True, start=True,
                                      stop=True),
                             waits=[(s_dve, max(v_w, bank_free[tb]))], inc=s_pe)
                bank_free[tb] = kb.op("vector", lambda e, c=c, jt=jt, tb=tb:
                                      e.tensor_copy(
                                          WDTT[:, (2 * c + jt) * 128:(2 * c + jt + 1) * 128],
                                          banks[tb][:, 0:128]),
                                      waits=[(s_pe, v_tp)])

            for blk in range(8):
                hb = blk % 2
                v_h = kb.dma("sync", HSB[hb][:, :],
                             _ap(xbcsp.tensor, blk * 8 * 128 * T + c * CS,
                                 [[T, 128], [128 * T, 8], [1, CS]]), s_hsb2[hb],
                             waits=[hsb_free[hb]] if hsb_free[hb][0] else [])
                v_patch = 0
                if c == 0:
                    v_patch = kb.op("vector", lambda e, blk=blk:
                                    e.tensor_copy(
                                        _ap(HSB[blk % 2], 0, [[2048, 128], [CS, 8], [1, 3]]),
                                        _ap(XBCFIX, blk * 24, [[240, 128], [3, 8], [1, 3]])),
                                    waits=[(s_hsb2[hb], v_h), (s_act, v_fix)])
                for sub in range(8):
                    cc = blk * 8 + sub
                    for jt in range(2):
                        tb = 6 + jt
                        w = [(s_hsb2[hb], v_h), (s_dve, max(bank_free[tb], v_patch))] \
                            if sub == 0 else [(s_dve, bank_free[tb])]
                        v_tp = kb.op("tensor", lambda e, hb=hb, sub=sub, jt=jt, tb=tb:
                                     e.matmul(banksb[tb][:, 0:128],
                                              HSB[hb][:, sub * CS + jt * 128:
                                                      sub * CS + (jt + 1) * 128],
                                              IDNB[:, :], is_transpose=True,
                                              start=True, stop=True),
                                     waits=w, inc=s_pe)
                        w_ev = [(s_pe, v_tp)]
                        if blk == 0 and sub == 0 and v_xtspill_last:
                            w_ev.append((s_xtsp, v_xtspill_last))
                        bank_free[tb] = kb.op(
                            "vector", lambda e, cc=cc, jt=jt, tb=tb:
                            e.tensor_copy(X_TOK[jt][:, cc * 128:(cc + 1) * 128],
                                          banksb[tb][:, 0:128]),
                            waits=w_ev)
                hsb_free[hb] = (s_dve, bank_free[7])
            v_xtok = bank_free[7]
            v_xw = 0
            for jt in range(2):
                v_xw = kb.op("vector", lambda e, c=c, jt=jt: e.tensor_tensor(
                    XW[jt][:, :], X_TOK[jt][:, :],
                    _ap(WDTT, (2 * c + jt) * 128, [[1024, 128], [1, 128], [0, PD]]),
                    OP.mult),
                    waits=([(s_pe, v_states_last)] if jt == 0 else ()))
            for jt in range(2):
                v_xtok_spill[(c, jt)] = kb.dma(
                    "sync", _ap(xtoksp.tensor, (c * 2 + jt) * 128 * INTER,
                                [[INTER, 128], [1, INTER]]),
                    X_TOK[jt][:, :], s_xtsp, waits=[(s_dve, v_xw)])
            v_xtspill_last = v_xtok_spill[(c, 1)]

            if c > 0:
                v_sb = kb.op("vector", lambda e: e.tensor_copy(SB16[:, :], S[:, :]),
                             waits=([(s_slsp, v_slspill_last)] if v_slspill_last else ()))
                v_sl_spill[c] = kb.dma("sync",
                                       _ap(slsp.tensor, c * 128 * INTER,
                                           [[INTER, 128], [1, INTER]]),
                                       SB16[:, :], s_slsp, waits=[(s_dve, v_sb)])
                v_slspill_last = v_sl_spill[c]

            v_b = kb.dma("sync", BFM3[:, :],
                         _ap(xbcsp.tensor, 64 * 128 * T + c * CS,
                             [[T, 128], [128 * T, 8], [1, CS]]), s_bfm,
                         waits=[(s_dve, v_xw)])
            v_bpatch = 0
            if c == 0:
                v_bpatch = kb.op("vector", lambda e: e.tensor_copy(
                    _ap(BFM3, 0, [[2048, 128], [CS, 8], [1, 3]]),
                    _ap(XBCFIX, 64 * 3, [[240, 128], [3, 8], [1, 3]])),
                    waits=[(s_bfm, v_b), (s_act, v_fix)])
            for g in range(G):
                for jt in range(2):
                    tb = 6 + jt
                    w = [(s_bfm, v_b), (s_dve, max(bank_free[tb], v_bpatch))] \
                        if g == 0 else [(s_dve, bank_free[tb])]
                    v_tp = kb.op("tensor", lambda e, g=g, jt=jt, tb=tb:
                                 e.matmul(banksb[tb][:, 0:128],
                                          BFM3[:, g * CS + jt * 128:g * CS + (jt + 1) * 128],
                                          IDNB[:, :], is_transpose=True,
                                          start=True, stop=True),
                                 waits=w, inc=s_pe)
                    bank_free[tb] = kb.op(
                        "vector", lambda e, g=g, jt=jt, tb=tb:
                        e.tensor_copy(B_TOK[jt][:, g * 128:(g + 1) * 128],
                                      banksb[tb][:, 0:128]),
                        waits=[(s_pe, v_tp)])
            v_btok = bank_free[7]

            v_cb = kb.op("tensor", lambda e, c=c: e.matmul(
                banks[5][:, 0:128], ONESF[0:1, :],
                _ap(CDF, c * 128, [[512, 1], [1, 128]]),
                start=True, stop=True),
                waits=[(s_act, v_cdf), (s_dve, max(v_ones, state_bank_free[5][1]
                                                   if state_bank_free[5] else 0))],
                inc=s_pe)
            v_cdecb = kb.op("vector", lambda e: e.tensor_copy(CDECB[:, :],
                                                              banks[5][:, 0:128]),
                            waits=[(s_pe, v_cb)])
            state_bank_free[5] = (s_dve, v_cdecb)

            for g in range(G):
                bka, bkb = (0, 1) if g % 2 == 0 else (2, 3)
                for jt in range(2):
                    for hh in range(16):
                        h = g * 16 + hh
                        bk = bka if hh < 8 else bkb
                        reg = (hh % 8) * 64
                        w = []
                        if jt == 0 and hh == 0:
                            w = [(s_dve, v_btok)]
                            for bkx in (bka, bkb):
                                if state_bank_free[bkx]:
                                    w.append(state_bank_free[bkx])
                        v_smm = kb.op("tensor", lambda e, g=g, jt=jt, h=h, bk=bk, reg=reg,
                                      hh=hh:
                                      e.matmul(banks[bk][:, reg:reg + PD],
                                               B_TOK[jt][:, g * 128:(g + 1) * 128],
                                               XW[jt][:, h * PD:(h + 1) * PD],
                                               start=(jt == 0 and hh % 8 == 0),
                                               stop=(jt == 1 and hh % 8 == 7)),
                                      waits=w,
                                      inc=(s_pe if jt == 1 and hh == 15 else None))
                for hh in range(16):
                    h = g * 16 + hh
                    bk = bka if hh < 8 else bkb
                    reg = (hh % 8) * 64
                    w = [(s_pe, v_smm), (s_dve, v_cdecb)] if hh == 0 else []
                    v_su = kb.op("vector", lambda e, h=h, bk=bk, reg=reg:
                                 e.scalar_tensor_tensor(S[:, h * PD:(h + 1) * PD],
                                                        S[:, h * PD:(h + 1) * PD],
                                                        CDECB[:, h:h + 1],
                                                        banks[bk][:, reg:reg + PD],
                                                        OP.mult, OP.add),
                                 waits=w)
                state_bank_free[bka] = state_bank_free[bkb] = (s_dve, v_su)
                v_states_last = v_smm

        v_ss = kb.op("vector", lambda e: e.tensor_scalar_mul(S[:, :], S[:, :],
                                                             ISH0[:, 0:1]))
        v_c2bw = kb.dma("gpsimd", cc2b_in, S[:, :], s_gd, waits=[(s_dve, v_ss)])
        kb.op("gpsimd", lambda e: e.collective_compute(
            "AllReduce", OP.add, replica_groups=[[0, 1], [2, 3], [4, 5], [6, 7]],
            ins=[cc2b_in.opt()], outs=[cc2b_out.opt()]),
            waits=[(s_gd, v_c2bw)], inc=s_cc2b, chain=False)
        v_c2b = 1

        kb.op("vector", lambda e: e.memset(PBF[0:1, 0:128], 1.0))
        for c in range(1, NCHUNK):
            kb.op("vector", lambda e, c=c: e.tensor_mul(
                PBF[0:1, c * 128:(c + 1) * 128],
                PBF[0:1, (c - 1) * 128:c * 128],
                CDF[0:1, (c - 1) * 128:c * 128]),
                waits=([(s_act, v_cdf)] if c == 1 else []))
        v_pbf = kb.op("vector", lambda e: e.tensor_scalar_mul(PBF[0:1, :], PBF[0:1, :],
                                                              ISH1[0:1, 0:1]))
        scope_p3b.close()
        kb.scope_barrier([s_dve, s_act, s_pe, s_gd, s_sp, s_ld, s_cc])

        scope_p4 = contextlib.ExitStack()
        _p4 = lambda name, p, f, dt: scope_p4.enter_context(
            nc.sbuf_tensor(name, [p, f], dt))
        XT4 = [_p4(f"XT4_{i}", 128, INTER, BF16) for i in range(2)]
        CB4 = _p4("CB4", 128, 2048, BF16)
        BB4 = _p4("BB4", 128, 2048, BF16)
        SLB = _p4("SLB", 128, INTER, BF16)
        S4B = _p4("S4B", 128, INTER, BF16)
        S4S = _p4("S4S", 128, INTER, BF16)
        PBBE = _p4("PBBE", 128, 128, F32)
        GM = _p4("GM", 128, G * 512, BF16)
        FCB3 = [_p4(f"FCB3_{i}", 3, 2048, BF16) for i in range(2)]
        EROWS = [_p4(f"EROWS{i}", 1, 2048, BF16) for i in range(2)]
        EB = [_p4(f"EB{i}", 128, 512, BF16) for i in range(3)]
        SC4 = [_p4(f"SC4_{i}", 128, 512, BF16) for i in range(4)]
        CX4 = [_p4(f"CX4_{i}", 128, 256, BF16) for i in range(4)]
        HS4 = _p4("HS4", 128, 2048, BF16)
        GT4 = _p4("GT4", 128, 2048, BF16)
        SILG = _p4("SILG", 128, 2048, BF16)
        GG4 = _p4("GG4", 128, 2048, F32)
        GSQ4 = [_p4(f"GSQ4_{i}", 128, 256, BF16) for i in range(2)]
        RSR = _p4("RSR", 1, 256, F32)
        RSBE = _p4("RSBE", 128, 256, F32)
        YGST = [_p4(f"YGST{i}", 128, 2048, BF16) for i in range(2)]
        ONECOL = _p4("ONECOL", 128, 1, BF16)
        CROW = _p4("CROW", 1, 4, F32)

        v_onecol = kb.op("vector", lambda e: e.memset(ONECOL[:, :], 1.0))
        v_crow = kb.op("vector", lambda e: e.memset(CROW[0:1, 0:1], 1e-5))
        kb.wait("gpsimd", s_cc2b, 1)
        v_s4 = kb.dma("gpsimd", S4B[:, :], cc2b_out, s_gd)
        tp_s4 = dtap("s4", [128, INTER])
        if tp_s4 is not None:
            kb.dma("sync", tp_s4, cc2b_out, s_sp, waits=[(s_gd, v_s4)])

        if "sc0" in debug:
            dbg["sc0"] = kb.dout("dbg_sc0", [128, 512], BF16)
            dbg["e0"] = kb.dout("dbg_e0", [128, 512], BF16)
            dbg["g0"] = kb.dout("dbg_g0", [128, 512], BF16)
        yraw_tap_on = "yraw" in debug
        yraw_guard = 0
        djbank_exp = [(None, 0), (None, 0)]
        eb_free = [(None, 0)] * 3
        sc_free = [(None, 0)] * 4
        cx_free = [(None, 0)] * 4
        ybank_evac = [(None, 0), (None, 0)]
        ybank_ready = [0, 0]
        fcb_guard = [(None, 0), (None, 0)]
        ebcast_guard = (None, 0)
        gsq_guard = [(None, 0), (None, 0)]
        gm_dve = 0
        pbb_guard = 0
        silg_guard = 0
        hs4_guard = 0
        pool_gg_last = 0
        var_guard = 0
        rsb_guard = 0
        ygst_guard = [(None, 0), (None, 0)]
        v_ebc = 0
        v_yg_spill = 0
        re_guard = []

        for c in range(NCHUNK):
            w_re = re_guard
            v_x0 = kb.dma("sync", XT4[0][:, :],
                          _ap(xtoksp.tensor, (c * 2 + 0) * 128 * INTER,
                              [[INTER, 128], [1, INTER]]), s_x42[0], waits=w_re)
            v_x1 = kb.dma("sync", XT4[1][:, :],
                          _ap(xtoksp.tensor, (c * 2 + 1) * 128 * INTER,
                              [[INTER, 128], [1, INTER]]), s_x42[1], waits=w_re)
            v_bb = kb.dma("sync", BB4[:, :],
                          _ap(xbcsp.tensor, 64 * 128 * T + c * CS,
                              [[T, 128], [128 * T, 8], [1, CS]]), s_bc, waits=w_re)
            v_cb4 = kb.dma("sync", CB4[:, :],
                           _ap(xbcsp.tensor, 72 * 128 * T + c * CS,
                               [[T, 128], [128 * T, 8], [1, CS]]), s_bc, waits=w_re)
            v_slb = 0
            if c > 0:
                v_slb = kb.dma("sync", SLB[:, :],
                               _ap(slsp.tensor, c * 128 * INTER,
                                   [[INTER, 128], [1, INTER]]), s_slbr, waits=w_re)
            v_bpat = v_cpat = 0
            if c == 0:
                v_bpat = kb.op("vector", lambda e: e.tensor_copy(
                    _ap(BB4, 0, [[2048, 128], [CS, 8], [1, 3]]),
                    _ap(XBCFIX, 64 * 3, [[240, 128], [3, 8], [1, 3]])),
                    waits=[(s_bc, v_cb4), (s_act, v_fix)])
                v_cpat = kb.op("vector", lambda e: e.tensor_copy(
                    _ap(CB4, 0, [[2048, 128], [CS, 8], [1, 3]]),
                    _ap(XBCFIX, 72 * 3, [[240, 128], [3, 8], [1, 3]])),
                    waits=[(s_bc, v_cb4), (s_act, v_fix)])

            for g in range(G):
                gb = 6 + (g % 2)
                for jt in range(2):
                    w = []
                    if jt == 0:
                        w = [(s_bc, v_cb4), (s_dve, max(v_bpat, v_cpat))]
                        if g >= 2:
                            w.append((s_dve, gm_dve))
                        if c > 0 and g < 2:
                            w = w + re_guard
                    v_gmm = kb.op("tensor", lambda e, g=g, jt=jt, gb=gb:
                                  e.matmul(banks[gb][:, jt * CS:(jt + 1) * CS],
                                           BB4[:, g * CS + jt * 128:g * CS + (jt + 1) * 128],
                                           CB4[:, g * CS:(g + 1) * CS],
                                           start=(jt == 0), stop=(jt == 1)),
                                  waits=w, inc=(s_pe if jt == 1 else None))
                gm_dve = kb.op("vector", lambda e, g=g, gb=gb:
                               e.tensor_copy(GM[:, g * 512:(g + 1) * 512],
                                             banks[gb][:, :]),
                               waits=[(s_pe, v_gmm)])
            v_gm = gm_dve

            v_pbb = kb.op("tensor", lambda e, c=c: e.matmul(
                banks[5][:, 0:128], ONESF[0:1, :],
                _ap(PBF, c * 128, [[512, 1], [1, 128]]),
                start=True, stop=True),
                waits=[(s_dve, max(v_pbf, pbb_guard))], inc=s_pe)
            pbb_guard = kb.op("vector", lambda e: e.tensor_copy(PBBE[:, :],
                                                                banks[5][:, 0:128]),
                              waits=[(s_pe, v_pbb)])
            v_s4s = kb.op("vector", lambda e: e.tensor_tensor(
                S4S[:, :], S4B[:, :],
                _ap(PBBE, 0, [[128, 128], [1, 128], [0, PD]]), OP.mult),
                waits=[(s_dve, pbb_guard), (s_gd, v_s4)] + re_guard)

            for g in range(G):
                v_gt = kb.dma("sync", GT4[:, :],
                              _ap(gatesp.tensor, g * 8 * 128 * T + c * CS,
                                  [[T, 128], [128 * T, 8], [1, CS]]), s_gt4r,
                              waits=[(s_act, silg_guard)] if c + g > 0 else [])
                v_hs4 = kb.dma("sync", HS4[:, :],
                               _ap(xbcsp.tensor, g * 8 * 128 * T + c * CS,
                                   [[T, 128], [128 * T, 8], [1, CS]]), s_hs4r,
                               waits=[(s_dve, hs4_guard)] if c + g > 0 else [])
                v_hpat = 0
                if c == 0:
                    v_hpat = kb.op("vector", lambda e, g=g: e.tensor_copy(
                        _ap(HS4, 0, [[2048, 128], [CS, 8], [1, 3]]),
                        _ap(XBCFIX, g * 24, [[240, 128], [3, 8], [1, 3]])),
                        waits=[(s_hs4r, v_hs4), (s_act, v_fix)])
                silg_guard = kb.op("scalar", lambda e: e.activation(
                    SILG[:, :], GT4[:, :], AF.Silu),
                    waits=[(s_gt4r, v_gt), (s_pool, pool_gg_last)])

                for hb2 in range(2):
                    hb = g * 2 + hb2
                    fb = hb % 2
                    wfl = [fcb_guard[fb]] if fcb_guard[fb][0] else []
                    v_f1 = kb.dma("sync", _ap(FCB3[fb], 0, [[2048, 1], [1, 2048]]),
                                  _ap(Q1, hb * 8 * T + c * CS, [[T, 8], [1, CS]]),
                                  s_fcb2[fb], waits=wfl)
                    v_f2 = kb.dma("sync", _ap(FCB3[fb], 2048, [[2048, 1], [1, 2048]]),
                                  _ap(Q2, hb * 8 * T + c * CS, [[T, 8], [1, CS]]),
                                  s_fcb2[fb], waits=wfl)
                    v_f3 = kb.dma("sync", _ap(FCB3[fb], 4096, [[2048, 1], [1, 2048]]),
                                  _ap(R2, hb * 8 * T + c * CS, [[T, 8], [1, CS]]),
                                  s_fcb2[fb], waits=wfl)
                    v_f4 = kb.dma("sync", EROWS[fb][0:1, :],
                                  _ap(ECUM, hb * 8 * T + c * CS, [[T, 8], [1, CS]]),
                                  s_fcb2[fb], waits=wfl)
                    v_fl = v_f4
                    last_dmm = 0
                    for hh in range(8):
                        h = hb * 8 + hh
                        dbk = h % 2
                        ebuf = h % 3
                        sbuf4 = h % 4
                        cbuf = h % 4
                        w = [(s_fcb2[fb], v_fl)]
                        if djbank_exp[dbk][0]:
                            w.append(djbank_exp[dbk])
                        kb.op("tensor", lambda e, dbk=dbk:
                              e.matmul(banks[dbk][:, :], IDNB[:, :], MASKP[:, :],
                                       start=True, stop=False),
                              waits=w)
                        last_dmm = kb.op("tensor", lambda e, dbk=dbk, hh=hh, fb=fb:
                                         e.matmul(banks[dbk][:, :], ONESB[0:3, :],
                                                  _ap(FCB3[fb], hh * CS,
                                                      [[2048, 3], [0, 2], [1, CS]]),
                                                  start=False, stop=True),
                                         waits=[], inc=s_pe)
                        w_e = [(s_pe, last_dmm)]
                        if eb_free[ebuf][0]:
                            w_e.append(eb_free[ebuf])
                        for jt in range(2):
                            v_exp = kb.op("scalar", lambda e, dbk=dbk, ebuf=ebuf, jt=jt,
                                          c=c, h=h:
                                          e.activation(EB[ebuf][:, jt * CS:(jt + 1) * CS],
                                                       banks[dbk][:, jt * CS:(jt + 1) * CS],
                                                       AF.Exp,
                                                       bias=LCT[:, (2 * c + jt) * 128 + h:
                                                                (2 * c + jt) * 128 + h + 1]),
                                          waits=(w_e if jt == 0 else []))
                        djbank_exp[dbk] = (s_act, v_exp)
                        w_s = [(s_act, v_exp), (s_dve, v_gm)]
                        if sc_free[sbuf4][0]:
                            w_s.append(sc_free[sbuf4])
                        eng_s = "vector" if h % 2 == 0 else "gpsimd"
                        v_sc = kb.op(eng_s, lambda e, ebuf=ebuf, sbuf4=sbuf4, g=g:
                                     e.tensor_tensor(SC4[sbuf4][:, :], EB[ebuf][:, :],
                                                     GM[:, g * 512:(g + 1) * 512],
                                                     OP.mult),
                                     waits=w_s)
                        sc_ready = (s_dve if h % 2 == 0 else s_pool, v_sc)
                        if c == 0 and h == 0 and "sc0" in debug:
                            kb.dma("sync", dbg["sc0"], SC4[sbuf4][:, :], s_sp,
                                   waits=[sc_ready])
                            kb.dma("sync", dbg["e0"], EB[ebuf][:, :], s_sp,
                                   waits=[sc_ready])
                            kb.dma("sync", dbg["g0"], GM[:, 0:512], s_sp,
                                   waits=[(s_dve, v_gm)])
                        eb_free[ebuf] = sc_ready
                        if hh % 2 == 0:
                            w_eb = [(s_fcb2[fb], v_fl)]
                            if ebcast_guard[0]:
                                w_eb.append(ebcast_guard)
                            v_ebc = kb.op("tensor", lambda e, hh=hh, fb=fb:
                                          e.matmul(banks[2][:, :], ONESB[0:1, :],
                                                   EROWS[fb][0:1, hh * CS:(hh + 2) * CS],
                                                   start=True, stop=True),
                                          waits=w_eb, inc=s_pe)
                        w_c = [(s_pe, v_ebc), (s_bc, v_cb4)]
                        if cx_free[cbuf][0]:
                            w_c.append(cx_free[cbuf])
                        if c == 0:
                            w_c.append((s_dve, v_cpat))
                        v_cx = kb.op("vector", lambda e, cbuf=cbuf, g=g, hh=hh:
                                     e.tensor_tensor(CX4[cbuf][:, :],
                                                     CB4[:, g * CS:(g + 1) * CS],
                                                     banks[2][:, (hh % 2) * CS:
                                                              (hh % 2 + 1) * CS],
                                                     OP.mult),
                                     waits=w_c)
                        if hh % 2 == 1:
                            ebcast_guard = (s_dve, v_cx)
                        ybk = 3 + (h // 4) % 2
                        preg = (h % 2) * 64
                        creg = ((h % 4) // 2) * CS
                        w_y = [sc_ready, (s_x42[0], v_x0), (s_x42[1], v_x1)]
                        if h % 4 < 2 and ybank_evac[ybk - 3][0]:
                            w_y.append(ybank_evac[ybk - 3])
                        kb.op("tensor", lambda e, h=h, ybk=ybk, preg=preg, creg=creg,
                              sbuf4=sbuf4:
                              e.matmul(banks[ybk][preg:preg + PD, creg:creg + CS],
                                       XT4[0][:, h * PD:(h + 1) * PD],
                                       SC4[sbuf4][:, 0:CS],
                                       start=(h % 4 < 2), stop=False,
                                       skip_group_check=True),
                              waits=w_y)
                        kb.op("tensor", lambda e, h=h, ybk=ybk, preg=preg, creg=creg,
                              sbuf4=sbuf4:
                              e.matmul(banks[ybk][preg:preg + PD, creg:creg + CS],
                                       XT4[1][:, h * PD:(h + 1) * PD],
                                       SC4[sbuf4][:, CS:2 * CS],
                                       start=False, stop=False,
                                       skip_group_check=True))
                        w_yi = [(s_dve, v_cx)]
                        if c > 0:
                            kb.op("tensor", lambda e, h=h, ybk=ybk, preg=preg,
                                  creg=creg, cbuf=cbuf:
                                  e.matmul(banks[ybk][preg:preg + PD, creg:creg + CS],
                                           SLB[:, h * PD:(h + 1) * PD],
                                           CX4[cbuf][:, :],
                                           start=False, stop=False,
                                           skip_group_check=True),
                                  waits=[(s_slbr, v_slb), (s_dve, v_cx)])
                        v_ymm = kb.op("tensor", lambda e, h=h, ybk=ybk, preg=preg,
                                      creg=creg, cbuf=cbuf:
                                      e.matmul(banks[ybk][preg:preg + PD, creg:creg + CS],
                                               S4S[:, h * PD:(h + 1) * PD],
                                               CX4[cbuf][:, :],
                                               start=False, stop=(h % 4 >= 2),
                                               skip_group_check=True),
                                      waits=w_yi + [(s_dve, v_s4s)],
                                      inc=(s_pe if h % 4 == 3 else None))
                        if h % 4 == 3:
                            for _i in range(4):
                                sc_free[_i] = (s_pe, v_ymm)
                                cx_free[_i] = (s_pe, v_ymm)
                            ybank_ready[ybk - 3] = v_ymm
                    fcb_guard[fb] = (s_pe, s_pe.n)
                    for cc8 in range(hb2 * 4, hb2 * 4 + 4):
                        cc = g * 8 + cc8
                        ybk = 3 + ((cc % 4) // 2)
                        creg = (cc % 2) * CS
                        w_e2 = [(s_pe, ybank_ready[ybk - 3]), (s_hs4r, v_hs4),
                                (s_pool, pool_gg_last)]
                        if c == 0:
                            w_e2.append((s_dve, v_hpat))
                        if yraw_tap_on and yraw_guard:
                            w_e2 = w_e2 + [(s_sp, yraw_guard)]
                        v_gpre = kb.op("vector",
                                       lambda e, cc=cc, cc8=cc8, ybk=ybk, creg=creg:
                                       e.scalar_tensor_tensor(
                                           SCR2[:, 0:CS],
                                           HS4[:, cc8 * CS:(cc8 + 1) * CS],
                                           DGRP[:, cc:cc + 1],
                                           banks[ybk][:, creg:creg + CS],
                                           OP.mult, OP.add),
                                       waits=w_e2)
                        if yraw_tap_on:
                            yraw_guard = kb.dma("sync",
                                                _ap(yrawsp.tensor,
                                                    cc * 128 * T + c * CS,
                                                    [[T, 128], [1, CS]]),
                                                SCR2[:, 0:CS], s_sp,
                                                waits=[(s_dve, v_gpre)])
                        v_gg = kb.op("gpsimd", lambda e, cc8=cc8:
                                     e.tensor_tensor(GG4[:, cc8 * CS:(cc8 + 1) * CS],
                                                     SCR2[:, 0:CS],
                                                     SILG[:, cc8 * CS:(cc8 + 1) * CS],
                                                     OP.mult),
                                     waits=[(s_dve, v_gpre), (s_act, silg_guard)])
                        pool_gg_last = v_gg
                        gq = cc8 % 2
                        w_sq = [(s_pool, v_gg)]
                        if gsq_guard[gq][0]:
                            w_sq.append(gsq_guard[gq])
                        v_gsq = kb.op("scalar", lambda e, cc8=cc8, gq=gq:
                                      e.activation(GSQ4[gq][:, :],
                                                   GG4[:, cc8 * CS:(cc8 + 1) * CS],
                                                   AF.Square),
                                      waits=w_sq)
                        w_v = [(s_act, v_gsq)]
                        if cc8 == 0:
                            w_v.append((s_dve, max(v_onecol, var_guard, v_gm)))
                        v_var = kb.op("tensor", lambda e, cc8=cc8, gq=gq:
                                      e.matmul(banks[6][0:1, 0:CS], ONECOL[:, :],
                                               GSQ4[gq][:, :],
                                               start=(cc8 == 0), stop=(cc8 == 7)),
                                      waits=w_v, inc=s_pe)
                        gsq_guard[gq] = (s_pe, v_var)
                    ybank_evac[0] = ybank_evac[1] = (s_dve, v_gpre)
                v_rs1 = kb.op("scalar", lambda e: e.activation(
                    RSR[0:1, :], banks[6][0:1, 0:CS], AF.Sqrt, scale=1.0 / GS,
                    bias=CROW[0:1, 0:1]),
                    waits=[(s_pe, v_var), (s_dve, v_crow)])
                var_guard = kb.op("vector", lambda e: e.reciprocal(RSR[0:1, :],
                                                                   RSR[0:1, :]),
                                  waits=[(s_act, v_rs1)])
                hs4_guard = var_guard
                v_rsb = kb.op("tensor", lambda e: e.matmul(
                    banks[7][:, 0:CS], ONESF[0:1, :], RSR[0:1, :],
                    start=True, stop=True),
                    waits=[(s_dve, max(var_guard, rsb_guard))], inc=s_pe)
                v_rsbe = kb.op("vector", lambda e: e.tensor_copy(RSBE[:, :],
                                                                 banks[7][:, 0:CS]),
                               waits=[(s_pe, v_rsb)])
                rsb_guard = v_rsbe
                rsbe_guard = v_rsbe
                yb2 = g % 2
                w_y7 = [(s_dve, v_rsbe)]
                if ygst_guard[yb2][0]:
                    w_y7.append(ygst_guard[yb2])
                for cc8 in range(8):
                    cc = g * 8 + cc8
                    v_yg = kb.op("vector", lambda e, cc=cc, cc8=cc8, yb2=yb2:
                                 e.scalar_tensor_tensor(YGST[yb2][:, cc8 * CS:(cc8 + 1) * CS],
                                                        GG4[:, cc8 * CS:(cc8 + 1) * CS],
                                                        NWGRP[:, cc:cc + 1],
                                                        RSBE[:, :], OP.mult, OP.mult),
                                 waits=(w_y7 if cc8 == 0 else []))
                v_yg_spill = kb.dma("sync",
                                    _ap(ygsp.tensor, g * 8 * 128 * T + c * CS,
                                        [[T, 128], [128 * T, 8], [1, CS]]),
                                    YGST[yb2][:, :], s_ygsp, waits=[(s_dve, v_yg)])
                ygst_guard[yb2] = (s_ygsp, v_yg_spill)
            re_guard = [(s_pe, s_pe.n), (s_dve, s_dve.n), (s_pool, s_pool.n)]

        tp_yraw = dtap("yraw", [NGATE * 128 * T])
        if tp_yraw is not None:
            kb.dma("sync", tp_yraw,
                   _ap(yrawsp.tensor, 0, [[NGATE * 128 * T, 1], [1, NGATE * 128 * T]]),
                   s_sp, waits=[(s_sp, yraw_guard)])
        tp_sl = dtap("sl", [3 * 128 * INTER], BF16)
        if tp_sl is not None:
            kb.dma("sync", tp_sl,
                   _ap(slsp.tensor, 128 * INTER, [[3 * 128 * INTER, 1], [1, 3 * 128 * INTER]]),
                   s_sp, waits=[(s_sp, s_sp.n)])
        tp_yg = dtap("yg", [NGATE * 128 * T], BF16)
        if tp_yg is not None:
            kb.dma("sync", tp_yg,
                   _ap(ygsp.tensor, 0, [[NGATE * 128 * T, 1], [1, NGATE * 128 * T]]),
                   s_sp, waits=[(s_sp, v_yg_spill)])

        scope_p4.close()
        scope_dt.close()
        kb.scope_barrier([s_dve, s_act, s_pe, s_gd, s_sp, s_ld, s_pool])

        scope_p5 = contextlib.ExitStack()
        _p5 = lambda name, p, f, dt: scope_p5.enter_context(
            nc.sbuf_tensor(name, [p, f], dt))
        YG5 = _p5("YG5", 128, NGATE * T, BF16)
        OW5 = [_p5(f"OW5_{i}", 128, NGATE * 128, BF16) for i in range(2)]
        HDF = [_p5(f"HDF{i}", 128, T, BF16) for i in range(2)]
        G2 = _p5("G2", 128, T, F32)
        XRW = [_p5("XRW0", 128, T, BF16)] * 2
        OST = [_p5("OST0", 128, T, F32)] * 2

        v_yg5 = 0
        for gg in range(8):
            v_yg5 = kb.dma("sync", YG5[:, gg * 8 * T:(gg + 1) * 8 * T],
                           _ap(ygsp.tensor, gg * 8 * 128 * T,
                               [[T, 128], [128 * T, 8], [1, T]]), s_yg5)
        ow_free = [0, 0]
        hdf_free = [0, 0]
        g2_guard = 0
        xrw_free = [0, 0]
        ost_free = [0, 0]
        p5bank_free = [0, 0, 0, 0]
        v_out_last = 0
        for dmc in range(32):
            wb = dmc % 2
            v_w = kb.dma("gpsimd", OW5[wb][:, :],
                         _ap(how.tensor, dmc * 128 * NGATE * 128,
                             [[NGATE * 128, 128], [1, NGATE * 128]]), s_w2[wb],
                         waits=[(s_pe, ow_free[wb])])
            v_hdf = kb.dma("sync", HDF[wb][:, :],
                           _ap(hdsp.tensor, dmc * 128 * 128,
                               [[128, 128], [D * 128, 8], [1, 128]]), s_hdsp2[wb],
                           waits=[(s_act, hdf_free[wb])])
            v_xrw = kb.dma("sync", XRW[wb][:, :],
                           _ap(xrawsp.tensor, dmc * 128 * T, [[T, 128], [1, T]]),
                           s_xrw, waits=[(s_dve, xrw_free[wb])])
            pb = (dmc % 2) * 2
            for fc in range(NGATE):
                w = []
                if fc == 0:
                    w = [(s_w2[wb], v_w), (s_yg5, v_yg5),
                         (s_dve, max(p5bank_free[pb], p5bank_free[pb + 1]))]
                for tt in range(2):
                    v_mm = kb.op("tensor",
                                 lambda e, wb=wb, fc=fc, tt=tt, pb=pb:
                                 e.matmul(banks[pb + tt][:, :],
                                          OW5[wb][:, fc * 128:(fc + 1) * 128],
                                          YG5[:, fc * T + tt * 512: fc * T + (tt + 1) * 512],
                                          start=(fc == 0), stop=(fc == NGATE - 1)),
                                 waits=(w if tt == 0 else ()),
                                 inc=(s_pe if fc == NGATE - 1 and tt == 1 else None))
            ow_free[wb] = v_mm
            hdf_free[wb] = kb.op("scalar", lambda e, wb=wb:
                                 e.activation(G2[:, :], HDF[wb][:, :], AF.Silu),
                                 waits=[(s_hdsp2[wb], v_hdf), (s_dve, g2_guard)])
            ob = dmc % 2
            for tt in range(2):
                kb.op("vector", lambda e, tt=tt, pb=pb, ob=ob:
                      e.tensor_tensor(OST[ob][:, tt * 512:(tt + 1) * 512],
                                      banks[pb + tt][:, :], G2[:, tt * 512:(tt + 1) * 512],
                                      OP.mult),
                      waits=([(s_pe, v_mm), (s_act, hdf_free[wb]),
                              (s_out, ost_free[ob])] if tt == 0 else ()))
            p5bank_free[pb] = p5bank_free[pb + 1] = kb.op(
                "vector", lambda e, ob=ob, wb=wb: e.tensor_tensor(
                    OST[ob][:, :], OST[ob][:, :], XRW[wb][:, :], OP.add),
                waits=[(s_xrw, v_xrw)])
            g2_guard = p5bank_free[pb]
            xrw_free[wb] = p5bank_free[pb]
            ost_free[ob] = kb.dma("sync", out_d[dmc * 128:(dmc + 1) * 128, :],
                                  OST[ob][:, :], s_out,
                                  waits=[(s_dve, p5bank_free[pb])])
            v_out_last = ost_free[ob]
        kb.wait("sync", s_out, v_out_last)
        scope_p5.close()

        block = es.enter_context(nc.Block())
        for e in ENGINES:
            fns = kb.ops[e]
            if fns:
                getattr(block, e)(lambda eng, fns=fns: [f(eng) for f in fns])

    return nc, dbg


def _pack_inputs(inputs):
    f32 = np.float32
    ipw = np.ascontiguousarray(inputs["L0_attn_in_proj_w"], dtype=f32)
    ffnw = np.ascontiguousarray(inputs["L0_ffn_W"], dtype=f32)
    outw = np.ascontiguousarray(inputs["L0_attn_out_w"], dtype=f32)
    shared = {
        "w1": np.ascontiguousarray(
            ipw.reshape(FCH, 128, 32, 128).transpose(0, 3, 2, 1)).reshape(FCH, 128, 32 * 128),
        "hw2": np.ascontiguousarray(
            ffnw.reshape(32, 128, 32, 128).transpose(0, 3, 2, 1)).reshape(32, 128, 32 * 128),
        "how": np.ascontiguousarray(
            outw.T.reshape(NGATE, 128, 32, 128).transpose(2, 1, 0, 3)).reshape(
                32, 128, NGATE * 128),
        "affn": np.ascontiguousarray(inputs["L0_ffn_A"], dtype=f32).reshape(32, 128, RANK),
        "w8": np.ascontiguousarray(ffnw[:NEXP].T).reshape(32, 128, NEXP),
        "bmat": np.ascontiguousarray(inputs["L0_ffn_Bmat"], dtype=f32),
        "limes": np.ascontiguousarray(inputs["L0_ffn_LiMEs"], dtype=f32),
        "bffn": np.ascontiguousarray(
            np.asarray(inputs["L0_ffn_b"], dtype=f32).reshape(32, 128).T),
        "convw": np.ascontiguousarray(
            np.asarray(inputs["L0_attn_conv_w"], dtype=f32).reshape(NXBC, 128, KC)
            .transpose(1, 0, 2)).reshape(128, NXBC * KC),
        "convb": np.ascontiguousarray(
            np.asarray(inputs["L0_attn_conv_b"], dtype=f32).reshape(NXBC, 128).T),
        "dtb": np.ascontiguousarray(
            np.asarray(inputs["L0_attn_dt_bias"], dtype=f32).reshape(128, 1)),
        "alog": np.ascontiguousarray(
            np.asarray(inputs["L0_attn_A_log"], dtype=f32).reshape(128, 1)),
        "dgrp": np.ascontiguousarray(
            np.repeat(np.asarray(inputs["L0_attn_D"], dtype=f32).reshape(64, 2).T, 64, axis=0)),
        "nwgrp": np.ascontiguousarray(
            np.asarray(inputs["L0_attn_norm_w"], dtype=f32).reshape(NGATE, 128).T),
        "idn": np.eye(128, dtype=f32),
    }
    j = np.arange(128)[:, None]
    i = np.arange(CS)[None, :]
    m0 = np.where(i >= j, 0.0, -1e6).astype(f32)
    m1 = np.where(i >= j + 128, 0.0, -1e6).astype(f32)
    shared["maskp"] = np.ascontiguousarray(np.concatenate([m0, m1], axis=1))

    x_full = np.asarray(inputs["L0_attn_input_states"], dtype=f32)
    maps = []
    for core in range(8):
        b, h = core // 2, core % 2
        m = dict(shared)
        m["x"] = np.ascontiguousarray(x_full[b, h * T:(h + 1) * T, :])
        m["ish0"] = np.full((128, 1), 1.0 - h, dtype=f32)
        m["ish1"] = np.full((128, 1), float(h), dtype=f32)
        maps.append(m)
    return maps


_NC_CACHE = {}


def _get_nc(debug=()):
    key = tuple(sorted(debug))
    if key not in _NC_CACHE:
        _NC_CACHE[key] = build_program(debug)
    return _NC_CACHE[key]


def kernel(**inputs):
    nc, _ = _get_nc()
    maps = _pack_inputs(inputs)
    res = run_bass_kernel_spmd(nc, maps, core_ids=list(range(8)))
    out = np.empty((B_, SEQ, D), dtype=np.float32)
    for core in range(8):
        b, h = core // 2, core % 2
        out[b, h * T:(h + 1) * T, :] = np.asarray(res.results[core]["out"]).T
    return out


# version 11
# speedup vs baseline: 1.0416x; 1.0015x over previous
import contextlib
import numpy as np
import ml_dtypes
import concourse.bass as bass
import concourse.mybir as mybir
from concourse.bass_utils import run_bass_kernel_spmd

F32 = mybir.dt.float32
F32R = mybir.dt.float32r
BF16 = mybir.dt.bfloat16
AF = mybir.ActivationFunctionType
OP = mybir.AluOpType
AP = bass.AP
AX = mybir.AxisListType.X

B_, SEQ, D = 4, 2048, 4096
T = 1024
H, PD, NS, G, KC, CS = 128, 64, 128, 8, 4, 256
NCHUNK = T // CS
INTER = H * PD
CONVC = INTER + 2 * G * NS
PROJ = INTER + CONVC + H
FCH = PROJ // 128
NXBC, NGATE = 80, 64
RANK, NEXP = 32, 8
GS = INTER // G
ENGINES = ("tensor", "vector", "scalar", "gpsimd", "sync")


class Sem:
    def __init__(self, handle):
        self.h = handle
        self.n = 0

    def inc(self, amt=1):
        self.n += amt
        return self.n

    @property
    def now(self):
        return self.n


class KB:

    def __init__(self):
        self.nc = bass.Bass("TRN2", target_bir_lowering=False)
        self.ops = {e: [] for e in ENGINES}
        self.dr = {}
        self.chain = {}
        self.guard = None
        self.guard_pending = set()

    def set_chain_sems(self, s_dve, s_act, s_pool):
        self.chain = {"vector": s_dve, "scalar": s_act, "gpsimd": s_pool}

    def scope_barrier(self, sems):
        cur = [(s, s.n) for s in sems if s.n > 0]
        if self.guard:
            old = dict((id(s), (s, v)) for s, v in self.guard)
            for s, v in cur:
                old[id(s)] = (s, v)
            cur = list(old.values())
        self.guard = cur
        self.guard_pending = set(ENGINES)

    def op(self, eng, fn, waits=(), inc=None, inc_amt=1, chain=True):
        allw = [(s, v) for (s, v) in waits if v > 0]
        if self.guard and eng in self.guard_pending:
            self.guard_pending.discard(eng)
            allw = list(self.guard) + allw
        ret = None
        cs = self.chain.get(eng) if chain else None
        if cs is not None and inc is None:
            if cs.n > 0:
                allw.append((cs, cs.n))
            ret = cs.inc(1)
            inc_h, inc_amt = cs.h, 1
        elif inc is not None:
            if cs is not None and cs.n > 0:
                allw.append((cs, cs.n))
            inc.inc(inc_amt)
            inc_h = inc.h
            ret = inc.now
        else:
            inc_h = None
        allw = tuple((s.h, v) for (s, v) in allw)

        def run(e, fn=fn, waits=allw, inc_h=inc_h, inc_amt=inc_amt):
            for hh, v in waits:
                e.wait_ge(hh, v)
            r = fn(e)
            if inc_h is not None:
                r.then_inc(inc_h, inc_amt)
        self.ops[eng].append(run)
        return ret

    def wait(self, eng, sem, val):
        if val > 0:
            self.ops[eng].append(lambda e, hh=sem.h, v=val: e.wait_ge(hh, v))

    def dma(self, eng, out_ap, in_ap, sem, waits=(), **kw):
        return self.op(eng, lambda e, o=out_ap, i=in_ap, kw=kw:
                       e.dma_start(out=o, in_=i, **kw),
                       waits=waits, inc=sem, inc_amt=16, chain=False)

    def din(self, name, shape, dt=F32):
        ap = self.nc.dram_tensor(name, shape, dt, kind="ExternalInput").ap()
        self.dr[name] = ap
        return ap

    def dout(self, name, shape, dt=F32):
        ap = self.nc.dram_tensor(name, shape, dt, kind="ExternalOutput").ap()
        self.dr[name] = ap
        return ap

    def dint(self, name, shape, dt):
        ap = self.nc.dram_tensor(name, shape, dt).ap()
        self.dr[name] = ap
        return ap


def _ap(handle, offset, dims):
    return AP(handle, offset, dims)


def build_program(debug=()):
    kb = KB()
    nc = kb.nc
    debug = set(debug)
    dbg = {}

    x_in = kb.din("x", [T, D])
    w1 = kb.din("w1", [FCH, 128, 32 * 128])
    hw2 = kb.din("hw2", [32, 128, 32 * 128])
    how = kb.din("how", [32, 128, NGATE * 128])
    affn = kb.din("affn", [32, 128, RANK])
    w8 = kb.din("w8", [32, 128, NEXP])
    bmat = kb.din("bmat", [RANK, D])
    limes = kb.din("limes", [NEXP, RANK])
    bffn = kb.din("bffn", [128, 32])
    convw = kb.din("convw", [128, NXBC * KC])
    convb = kb.din("convb", [128, NXBC])
    dtbias = kb.din("dtb", [128, 1])
    alog = kb.din("alog", [128, 1])
    dgrp = kb.din("dgrp", [128, NGATE])
    nwgrp = kb.din("nwgrp", [128, NGATE])
    idn_d = kb.din("idn", [128, 128])
    maskp_d = kb.din("maskp", [128, 2 * CS])
    ish0 = kb.din("ish0", [128, 1])
    ish1 = kb.din("ish1", [128, 1])

    out_d = kb.dout("out", [D, T], F32)

    def dtap(name, shape, dt=F32):
        if name in debug:
            dbg[name] = kb.dout("dbg_" + name, shape, dt)
            return dbg[name]
        return None

    xrawsp = kb.dint("xrawsp", [D, T], BF16)
    xbcsp = kb.dint("xbcsp", [NXBC, 128, T], BF16)
    gatesp = kb.dint("gatesp", [NGATE, 128, T], BF16)
    xtoksp = kb.dint("xtoksp", [NCHUNK, 2, 128, INTER], BF16)
    hdsp = kb.dint("hdsp", [8, D, 128], BF16)
    yrawsp = kb.dint("yrawsp", [NGATE, 128, T], F32)
    ygsp = kb.dint("ygsp", [NGATE, 128, T], BF16)
    cc1_in = kb.dint("cc1_in", [1, 16], F32)
    cc1_out = kb.dint("cc1_out", [1, 16], F32)
    cc2a_in = kb.dint("cc2a_in", [128, NXBC * 3], F32)
    cc2a_out = kb.dint("cc2a_out", [128, NXBC * 3], F32)
    cc2b_in = kb.dint("cc2b_in", [128, INTER], F32)
    cc2b_out = kb.dint("cc2b_out", [128, INTER], F32)

    es = contextlib.ExitStack()

    def sb(name, pdim, fdim, dt):
        return es.enter_context(nc.sbuf_tensor(name, [pdim, fdim], dt))

    def sem(name):
        return Sem(es.enter_context(nc.semaphore(name)))

    with es:
        banks = [es.enter_context(nc.psum_tensor(f"bank{i}", [128, 512], F32))
                 for i in range(8)]
        banksb = [b.ap().bitcast(BF16) for b in banks]

        s_ld = sem("s_ld")
        s_gd = sem("s_gd")
        s_pe = sem("s_pe")
        s_dve = sem("s_dve")
        s_act = sem("s_act")
        s_pool = sem("s_pool")
        s_sp = sem("s_sp")
        s_cc = sem("s_cc")
        s_cc2a = sem("s_cc2a")
        s_cc2b = sem("s_cc2b")
        s_xt2 = [sem("s_xt0"), sem("s_xt1")]
        s_xrs2 = [sem("s_xrs0"), sem("s_xrs1")]
        s_w2 = [sem("s_w0"), sem("s_w1")]
        s_hdsp2 = [sem("s_hdsp0"), sem("s_hdsp1")]
        s_spst2 = [sem("s_spst0"), sem("s_spst1")]
        s_cdf = sem("s_cdf")
        s_hsb2 = [sem("s_hsb0"), sem("s_hsb1")]
        s_bfm = sem("s_bfm")
        s_xtsp = sem("s_xtsp")
        s_slsp = sem("s_slsp")
        s_x42 = [sem("s_x40"), sem("s_x41")]
        s_bc = sem("s_bc")
        s_slbr = sem("s_slbr")
        s_hs4r = sem("s_hs4r")
        s_gt4r = sem("s_gt4r")
        s_fcb2 = [sem("s_fcb0"), sem("s_fcb1")]
        s_ygsp = sem("s_ygsp")
        s_yg5 = sem("s_yg5")
        s_xrw = sem("s_xrw")
        s_out = sem("s_out")
        kb.set_chain_sems(s_dve, s_act, s_pool)
        ALLS = [s_ld, s_gd, s_pe, s_dve, s_act, s_pool, s_sp,
                s_xt2[0], s_xt2[1], s_xrs2[0], s_xrs2[1], s_w2[0], s_w2[1],
                s_hdsp2[0], s_hdsp2[1], s_spst2[0], s_spst2[1], s_cdf,
                s_hsb2[0], s_hsb2[1], s_bfm, s_xtsp, s_slsp, s_x42[0], s_x42[1],
                s_bc, s_slbr, s_hs4r, s_gt4r, s_fcb2[0], s_fcb2[1], s_ygsp]

        IDN = sb("IDN", 128, 128, F32)
        IDNB = sb("IDNB", 128, 128, BF16)
        MASKP = sb("MASKP", 128, 2 * CS, BF16)
        CONVW = sb("CONVW", 128, NXBC * KC, F32)
        CONVB = sb("CONVB", 128, NXBC, F32)
        DTB = sb("DTB", 128, 1, F32)
        ANEG = sb("ANEG", 128, 1, F32)
        DGRP = sb("DGRP", 128, NGATE, F32)
        NWGRP = sb("NWGRP", 128, NGATE, F32)
        BFFN = sb("BFFN", 128, 32, F32)
        ONESB = sb("ONESB", 4, 128, BF16)
        ONESF = sb("ONESF", 1, 128, F32)
        ISH0 = sb("ISH0", 128, 1, F32)
        ISH1 = sb("ISH1", 128, 1, F32)
        BMATB = sb("BMATB", RANK, D, BF16)
        LIMESB = sb("LIMESB", NEXP, RANK, BF16)
        STT = sb("STT", 128, 8, F32)
        SCR2 = sb("SCR2", 128, T, F32)
        ACC = sb("ACC", 128, T, F32)
        ONES128 = sb("ONES128", 128, CS, F32)
        CR3 = sb("CR3", 128, NXBC * 3, F32)
        HALOS = sb("HALOS", 128, NXBC * 3, F32)
        XBCFIX = sb("XBCFIX", 128, NXBC * 3, BF16)
        ZPMT = sb("ZPMT", RANK, T, BF16)
        scope_dt = contextlib.ExitStack()
        _pdt = lambda name, p, f, dt: scope_dt.enter_context(
            nc.sbuf_tensor(name, [p, f], dt))
        DT = _pdt("DT", 128, T, F32)
        CUM = _pdt("CUM", 128, T, F32)
        LCT = _pdt("LCT", 128, 8 * 128, F32)
        WDTT = _pdt("WDTT", 128, 8 * 128, BF16)
        Q1 = _pdt("Q1", 128, T, BF16)
        Q2 = _pdt("Q2", 128, T, BF16)
        R2 = _pdt("R2", 128, T, BF16)
        ECUM = _pdt("ECUM", 128, T, BF16)
        PBF = _pdt("PBF", 1, 4 * 128, F32)

        kb.dma("sync", IDN[:, :], idn_d, s_ld)
        kb.dma("sync", CONVW[:, :], convw, s_ld)
        kb.dma("sync", CONVB[:, :], convb, s_ld)
        kb.dma("sync", DTB[:, :], dtbias, s_ld)
        kb.dma("sync", DGRP[:, :], dgrp, s_ld)
        kb.dma("sync", NWGRP[:, :], nwgrp, s_ld)
        kb.dma("sync", BFFN[:, :], bffn, s_ld)
        kb.dma("sync", ISH0[:, :], ish0, s_ld)
        kb.dma("sync", ISH1[:, :], ish1, s_ld)
        kb.dma("gpsimd", MASKP[:, :], maskp_d, s_gd)
        v_alog = kb.dma("sync", ANEG[:, :], alog, s_ld)
        kb.dma("gpsimd", BMATB[:, :], bmat, s_gd)
        kb.dma("gpsimd", LIMESB[:, :], limes, s_gd)
        v_idnb = kb.dma("gpsimd", IDNB[:, :], idn_d, s_gd)
        v_consts = s_ld.now

        kb.op("vector", lambda e: e.memset(ONESB[:, :], 1.0))
        v_ones = kb.op("vector", lambda e: e.memset(ONESF[:, :], 1.0), inc=s_dve)
        kb.op("scalar", lambda e: e.activation(ANEG[:, :], ANEG[:, :], AF.Exp),
              waits=[(s_ld, v_consts)])
        v_aneg = kb.op("scalar", lambda e: e.mul(ANEG[:, :], ANEG[:, :], -1.0),
                       inc=s_act)

        scope_xn = contextlib.ExitStack()
        XN = scope_xn.enter_context(nc.sbuf_tensor("XN", [128, 32 * T], BF16))
        scope_p0 = contextlib.ExitStack()
        XT = [scope_p0.enter_context(nc.sbuf_tensor(f"XT{i}", [128, D], F32))
              for i in range(2)]
        XTN = [scope_p0.enter_context(nc.sbuf_tensor(f"XTN{i}", [128, D], BF16))
               for i in range(2)]
        SQS = scope_p0.enter_context(nc.sbuf_tensor("SQS", [128, D], F32))

        XRB = [scope_p0.enter_context(nc.sbuf_tensor(f"XRB{i}", [128, 512], BF16))
               for i in range(2)]
        xr_bank_free = [0, 0]
        xt_free_v = [0, 0]
        xt_free_a = [0, 0]
        xt_free_pe = [0, 0]
        xtn_free = [0, 0]
        sqs_free = 0
        bank_free = {6: 0, 7: 0}
        v_xn = 0

        for i in range(8):
            b = i % 2
            v_x = kb.dma("sync", XT[b][:, :], x_in[i * 128:(i + 1) * 128, :], s_xt2[b],
                         waits=[(s_dve, xt_free_v[b]), (s_act, xt_free_a[b]),
                                (s_pe, xt_free_pe[b])])
            v_ms = kb.op("vector", lambda e, b=b: e.reduce_sum(STT[:, 0:1], XT[b][:, :], AX),
                         waits=[(s_xt2[b], v_x)], inc=s_dve)
            v_sq = kb.op("scalar", lambda e, b=b: e.activation(SQS[:, :], XT[b][:, :],
                                                               AF.Square),
                         waits=[(s_xt2[b], v_x), (s_dve, sqs_free)], inc=s_act)
            xt_free_a[b] = v_sq
            v_ss = kb.op("vector", lambda e: e.reduce_sum(STT[:, 1:2], SQS[:, :], AX),
                         waits=[(s_act, v_sq)], inc=s_dve)
            sqs_free = v_ss
            kb.op("vector", lambda e: e.tensor_scalar_mul(STT[:, 2:3], STT[:, 0:1], 1.0 / D))
            kb.op("vector", lambda e: e.tensor_mul(STT[:, 3:4], STT[:, 2:3], STT[:, 2:3]))
            kb.op("vector", lambda e: e.tensor_scalar(STT[:, 4:5], STT[:, 1:2], 1.0 / D,
                                                      1e-5, OP.mult, OP.add))
            v_var = kb.op("vector", lambda e: e.tensor_sub(STT[:, 4:5], STT[:, 4:5],
                                                           STT[:, 3:4]), inc=s_dve)
            v_sd = kb.op("scalar", lambda e: e.activation(STT[:, 5:6], STT[:, 4:5], AF.Sqrt),
                         waits=[(s_dve, v_var)], inc=s_act)
            kb.wait("vector", s_act, v_sd)
            kb.op("vector", lambda e: e.reciprocal(STT[:, 6:7], STT[:, 5:6]))
            v_nrm = kb.op("vector",
                          lambda e, b=b: e.tensor_scalar(XTN[b][:, :], XT[b][:, :],
                                                         STT[:, 2:3], STT[:, 6:7],
                                                         OP.subtract, OP.mult),
                          waits=[(s_pe, xtn_free[b])], inc=s_dve)
            xt_free_v[b] = v_nrm
            for gg in range(4):
                pb = 6 + (gg % 2)
                bk = banksb[pb]
                for k in range(8):
                    dc = gg * 8 + k
                    w = []
                    if k == 0:
                        w = [(s_dve, v_nrm), (s_gd, v_idnb), (s_dve, bank_free[pb])]
                    v_tp = kb.op("tensor",
                                 lambda e, bk=bk, k=k, dc=dc, b=b, st=(k == 0), sp=(k == 7):
                                 e.matmul(bk[:, k * 128:(k + 1) * 128],
                                          XTN[b][:, dc * 128:(dc + 1) * 128],
                                          IDNB[:, :], is_transpose=True,
                                          start=st, stop=sp),
                                 waits=w, inc=(s_pe if k == 7 else None))
                if gg == 3:
                    xtn_free[b] = v_tp
                xn_out = _ap(XN, gg * 8 * T + i * 128, [[32 * T, 128], [T, 8], [1, 128]])
                bank_free[pb] = kb.op("vector",
                                      lambda e, bk=bk, xo=xn_out:
                                      e.tensor_copy(xo, bk[:, :]),
                                      waits=[(s_pe, v_tp)], inc=s_dve)
                v_xn = bank_free[pb]
            for gg in range(8):
                pb4 = 4 + (gg % 2)
                for k in range(4):
                    dc = gg * 4 + k
                    w = [(s_xt2[b], v_x)]
                    if k == 0:
                        w.append((s_xrs2[pb4 - 4], xr_bank_free[pb4 - 4]))
                    v_tp4 = kb.op("tensor",
                                  lambda e, pb4=pb4, k=k, dc=dc, b=b,
                                  st=(k == 0), sp=(k == 3):
                                  e.matmul(banks[pb4][:, k * 128:(k + 1) * 128],
                                           XT[b][:, dc * 128:(dc + 1) * 128],
                                           IDN[:, :], is_transpose=True,
                                           start=st, stop=sp),
                                  waits=w, inc=(s_pe if k == 3 else None))
                xrb = gg % 2
                v_xre = kb.op("vector",
                              lambda e, pb4=pb4, xrb=xrb:
                              e.tensor_copy(XRB[xrb][:, :], banks[pb4][:, 0:512]),
                              waits=[(s_pe, v_tp4)])
                xr_bank_free[pb4 - 4] = kb.dma(
                    "sync",
                    _ap(xrawsp.tensor, gg * 4 * 128 * T + i * 128,
                        [[T, 128], [128 * T, 4], [1, 128]]),
                    XRB[xrb][:, :], s_xrs2[xrb], waits=[(s_dve, v_xre)])
            xt_free_pe[b] = v_tp4

        p0_dve_end = s_dve.now
        tp_xn = dtap("xn", [128, 32 * T], BF16)
        if tp_xn is not None:
            kb.dma("sync", tp_xn, _ap(XN, 0, [[32 * T, 128], [1, 32 * T]]), s_sp,
                   waits=[(s_dve, v_xn)])
        scope_p0.close()
        kb.scope_barrier([s_dve, s_act, s_pe, s_ld])

        scope_p1 = contextlib.ExitStack()
        _p1 = lambda name, p, f, dt: scope_p1.enter_context(
            nc.sbuf_tensor(name, [p, f], dt))
        AZ = _p1("AZ", 128, 32 * RANK, BF16)
        W8B = _p1("W8B", 128, 32 * NEXP, BF16)
        ZT = _p1("ZT", RANK, T, F32)
        ZTB = _p1("ZTB", RANK, T, BF16)
        H8T = _p1("H8T", NEXP, T, F32)
        DST = _p1("DST", NEXP, T // 2, F32)
        SCL = _p1("SCL", 1, 16, F32)
        GSCL = _p1("GSCL", 1, 16, F32)

        v_az = kb.dma("gpsimd", AZ[:, :], _ap(affn.tensor, 0, [[RANK, 128], [128 * RANK, 32], [1, RANK]]),
                      s_gd)
        v_w8 = kb.dma("gpsimd", W8B[:, :], _ap(w8.tensor, 0, [[NEXP, 128], [128 * NEXP, 32], [1, NEXP]]),
                      s_gd)

        for tt in range(2):
            for dc in range(32):
                kb.op("tensor",
                      lambda e, tt=tt, dc=dc:
                      e.matmul(banks[tt][0:RANK, :],
                               AZ[:, dc * RANK:(dc + 1) * RANK],
                               XN[:, dc * T + tt * 512: dc * T + (tt + 1) * 512],
                               start=(dc == 0), stop=(dc == 31)),
                      waits=([(s_gd, v_w8), (s_dve, v_xn)] if dc == 0 else ()),
                      inc=(s_pe if dc == 31 else None))
            v_z = s_pe.now
            kb.op("vector", lambda e, tt=tt: e.tensor_copy(ZT[:, tt * 512:(tt + 1) * 512],
                                                           banks[tt][0:RANK, :]),
                  waits=[(s_pe, v_z)])
            v_ztb = kb.op("vector", lambda e, tt=tt: e.tensor_copy(ZTB[:, tt * 512:(tt + 1) * 512],
                                                                   banks[tt][0:RANK, :]),
                          inc=s_dve)
        v_zt = v_ztb

        for tt in range(2):
            for dc in range(32):
                kb.op("tensor",
                      lambda e, tt=tt, dc=dc:
                      e.matmul(banks[2 + tt][0:NEXP, :],
                               W8B[:, dc * NEXP:(dc + 1) * NEXP],
                               XN[:, dc * T + tt * 512: dc * T + (tt + 1) * 512],
                               start=(dc == 0), stop=(dc == 31)),
                      waits=([(s_gd, v_w8)] if dc == 0 else ()),
                      inc=(s_pe if dc == 31 else None))
            v_h8 = s_pe.now
            kb.op("vector",
                  lambda e, tt=tt: e.tensor_scalar(H8T[:, tt * 512:(tt + 1) * 512],
                                                   banks[2 + tt][0:NEXP, :],
                                                   BFFN[0:NEXP, 0:1], None, OP.add),
                  waits=[(s_pe, v_h8), (s_ld, v_consts)], inc=s_dve)
        v_h8t = s_dve.now

        LIMT = _p1("LIMT", RANK, NEXP, BF16)
        kb.op("tensor", lambda e: e.matmul(banksb[4][0:RANK, 0:NEXP], LIMESB[:, :],
                                           IDNB[0:NEXP, 0:NEXP], is_transpose=True,
                                           start=True, stop=True),
              waits=[(s_gd, v_idnb), (s_dve, p0_dve_end)], inc=s_pe)
        v_limt_pe = s_pe.now
        v_limt = kb.op("vector", lambda e: e.tensor_copy(LIMT[:, :], banksb[4][0:RANK, 0:NEXP]),
                       waits=[(s_pe, v_limt_pe)], inc=s_dve)
        zanch = _ap(ZTB, 1, [[T, RANK], [2, T // 2]])
        kb.op("tensor", lambda e: e.matmul(banks[5][0:NEXP, :], LIMT[:, :], zanch,
                                           start=True, stop=True),
              waits=[(s_dve, max(v_limt, v_zt))], inc=s_pe)
        v_ds_pe = s_pe.now
        v_ds = kb.op("vector", lambda e: e.tensor_copy(DST[:, :], banks[5][0:NEXP, :]),
                     waits=[(s_pe, v_ds_pe)], inc=s_dve)

        h8anch = _ap(H8T, 1, [[T, NEXP], [2, T // 2]])
        kb.wait("vector", s_dve, v_h8t)
        kb.op("vector", lambda e: e.reduce_max(STT[0:NEXP, 0:1], h8anch, AX,
                                               apply_absolute_value=True))
        v_lm = kb.op("vector", lambda e: e.reduce_max(STT[0:NEXP, 1:2], DST[:, :], AX,
                                                      apply_absolute_value=True),
                     inc=s_dve)
        SCRO = _p1("SCRO", 1, 16, F32)
        v_t = kb.dma("gpsimd", SCRO[0:1, :],
                     _ap(STT, 0, [[8, 8], [1, 2]]), s_gd,
                     waits=[(s_dve, v_lm)])
        kb.wait("vector", s_gd, v_t)
        kb.op("vector", lambda e: e.memset(SCL[:, :], 0.0))
        kb.op("vector", lambda e: e.reduce_max(SCL[0:1, 0:1],
                                               _ap(SCRO, 0, [[16, 1], [2, 8]]), AX))
        v_scl = kb.op("vector", lambda e: e.reduce_max(SCL[0:1, 1:2],
                                                       _ap(SCRO, 1, [[16, 1], [2, 8]]), AX),
                      inc=s_dve)
        v_c1w = kb.dma("gpsimd", cc1_in, SCL[0:1, :], s_gd, waits=[(s_dve, v_scl)])
        kb.op("gpsimd", lambda e: e.collective_compute(
            "AllReduce", OP.max, replica_groups=[list(range(8))],
            ins=[cc1_in.opt()], outs=[cc1_out.opt()]),
            waits=[(s_gd, v_c1w)], inc=s_cc)
        v_c1 = s_cc.now
        kb.wait("gpsimd", s_cc, v_c1)
        v_gscl = kb.dma("gpsimd", GSCL[0:1, :], cc1_out, s_gd)

        tp_sc = dtap("scales", [1, 16])
        if tp_sc is not None:
            kb.dma("sync", tp_sc, GSCL[0:1, :], s_sp, waits=[(s_gd, v_gscl)])

        kb.wait("vector", s_gd, v_gscl)
        kb.op("vector", lambda e: e.tensor_scalar_max(GSCL[0:1, 0:2], GSCL[0:1, 0:2], 1e-6))
        kb.op("vector", lambda e: e.reciprocal(GSCL[0:1, 2:4], GSCL[0:1, 0:2]))
        v_rr = kb.op("vector", lambda e: e.tensor_scalar_mul(GSCL[0:1, 2:4], GSCL[0:1, 2:4],
                                                             0.5), inc=s_dve)
        kb.op("tensor", lambda e: e.matmul(banks[0][0:NEXP, 0:2],
                                           ONESF[0:1, 0:NEXP], GSCL[0:1, 2:4],
                                           start=True, stop=True),
              waits=[(s_dve, max(v_rr, v_ones))], inc=s_pe)
        v_rb_pe = s_pe.now
        RSC = _p1("RSC", NEXP, 2, F32)
        v_rsc = kb.op("vector", lambda e: e.tensor_copy(RSC[:, :], banks[0][0:NEXP, 0:2]),
                      waits=[(s_pe, v_rb_pe)], inc=s_dve)

        LGT = _p1("LGT", NEXP, T // 2, F32)
        kb.op("vector", lambda e: e.tensor_scalar_mul(LGT[:, :], h8anch, RSC[:, 0:1]))
        v_lg = kb.op("vector", lambda e: e.scalar_tensor_tensor(LGT[:, :], DST[:, :],
                                                                RSC[:, 1:2], LGT[:, :],
                                                                OP.mult, OP.add),
                     inc=s_dve)
        for a in range(4):
            kb.op("tensor", lambda e, a=a: e.matmul(banks[1][:, a * 8:(a + 1) * 8],
                                                    LGT[:, a * 128:(a + 1) * 128],
                                                    IDN[0:NEXP, 0:NEXP], is_transpose=True,
                                                    start=(a == 0), stop=(a == 3)),
                  waits=([(s_dve, v_lg), (s_ld, v_consts)] if a == 0 else ()),
                  inc=(s_pe if a == 3 else None))
        v_lgt = s_pe.now

        PRB = _p1("PRB", 128, 4 * NEXP, F32)
        WTK = _p1("WTK", 128, 4 * NEXP, F32)
        GST = _p1("GST", 128, 12, F32)
        kb.wait("vector", s_pe, v_lgt)
        kb.wait("scalar", s_pe, v_lgt)
        for a in range(4):
            pa = banks[1][:, a * 8:(a + 1) * 8]
            prb = PRB[:, a * 8:(a + 1) * 8]
            wtk = WTK[:, a * 8:(a + 1) * 8]
            v_mx = kb.op("vector", lambda e, pa=pa: e.reduce_max(GST[:, 0:1], pa, AX,
                                                                 negate=True), inc=s_dve)
            v_ex = kb.op("scalar", lambda e, pa=pa, prb=prb:
                         e.activation(prb, pa, AF.Exp, bias=GST[:, 0:1],
                                      accum_out=GST[:, 1:2]),
                         waits=[(s_dve, v_mx)], inc=s_act)
            kb.wait("vector", s_act, v_ex)
            kb.op("vector", lambda e: e.reciprocal(GST[:, 2:3], GST[:, 1:2]))
            kb.op("vector", lambda e, prb=prb: e.tensor_scalar_mul(prb, prb, GST[:, 2:3]))
            kb.op("vector", lambda e, prb=prb: e.reduce_max(GST[:, 3:4], prb, AX))
            kb.op("vector", lambda e, prb=prb: e.tensor_scalar(GST[:, 4:12][:, 0:8], prb,
                                                               GST[:, 3:4], None, OP.is_ge))
            kb.op("vector", lambda e: e.tensor_scalar(SCR2[:, 0:8], GST[:, 4:12], 1.0, 0.0,
                                                      OP.mult, OP.add,
                                                      accum_out=GST[:, 2:3]))
            kb.op("vector", lambda e, prb=prb:
                  e.scalar_tensor_tensor(SCR2[:, 8:16], GST[:, 4:12], -1e30, prb,
                                         OP.mult, OP.add))
            kb.op("vector", lambda e: e.reduce_max(GST[:, 5:6], SCR2[:, 8:16], AX))
            kb.op("vector", lambda e: e.tensor_scalar(GST[:, 6:7], GST[:, 2:3], 1.5, None,
                                                      OP.is_ge))
            kb.op("vector", lambda e: e.tensor_sub(GST[:, 7:8], GST[:, 3:4], GST[:, 5:6]))
            kb.op("vector", lambda e: e.tensor_mul(GST[:, 7:8], GST[:, 7:8], GST[:, 6:7]))
            kb.op("vector", lambda e: e.tensor_add(GST[:, 8:9], GST[:, 5:6], GST[:, 7:8]))
            v_thr = kb.op("vector", lambda e: e.tensor_scalar_mul(GST[:, 9:10], GST[:, 8:9],
                                                                  -2.0), inc=s_dve)
            v_msk = kb.op("scalar", lambda e, prb=prb:
                          e.activation(SCR2[:, 16:24], prb, AF.Sigmoid, scale=2.0,
                                       bias=GST[:, 9:10]),
                          waits=[(s_dve, v_thr)], inc=s_act)
            kb.wait("vector", s_act, v_msk)
            kb.op("vector", lambda e, prb=prb, wtk=wtk: e.tensor_mul(wtk, prb, SCR2[:, 16:24]))
            kb.op("vector", lambda e, wtk=wtk: e.tensor_scalar(SCR2[:, 24:32], wtk, 1.0, 0.0,
                                                               OP.mult, OP.add,
                                                               accum_out=GST[:, 10:11]))
            kb.op("vector", lambda e: e.tensor_scalar_add(GST[:, 10:11], GST[:, 10:11], 1e-9))
            kb.op("vector", lambda e: e.reciprocal(GST[:, 11:12], GST[:, 10:11]))
            v_wtk = kb.op("vector", lambda e, wtk=wtk:
                          e.tensor_scalar_mul(wtk, wtk, GST[:, 11:12]), inc=s_dve)

        tp_w = dtap("weights", [128, 32])
        if tp_w is not None:
            kb.dma("sync", tp_w, WTK[:, :], s_sp, waits=[(s_dve, v_wtk)])

        WTE = _p1("WTE", NEXP, T // 2, BF16)
        WFULL = _p1("WFULL", NEXP, T, BF16)
        for a in range(4):
            kb.op("tensor", lambda e, a=a: e.matmul(banks[2][0:NEXP, a * 128:(a + 1) * 128],
                                                    WTK[:, a * 8:(a + 1) * 8],
                                                    IDN[:, :], is_transpose=True,
                                                    start=(a == 0), stop=(a == 3)),
                  waits=([(s_dve, v_wtk)] if a == 0 else ()),
                  inc=(s_pe if a == 3 else None))
        v_wte_pe = s_pe.now
        v_wte = kb.op("vector", lambda e: e.tensor_copy(WTE[:, :], banks[2][0:NEXP, 0:512]),
                      waits=[(s_pe, v_wte_pe)], inc=s_dve)
        v_wf = kb.op("vector", lambda e: e.tensor_copy(
            WFULL[:, :], _ap(WTE, 0, [[T // 2, NEXP], [1, T // 2], [0, 2]])),
            inc=s_dve)

        for tt in range(2):
            pmb = banks[3 + tt]
            kb.op("tensor", lambda e, tt=tt, pmb=pmb: e.matmul(pmb[0:RANK, :], LIMESB[:, :],
                                                               WFULL[:, tt * 512:(tt + 1) * 512],
                                                               start=True, stop=True),
                  waits=([(s_dve, v_wf)] if tt == 0 else ()), inc=s_pe)
            v_pm = s_pe.now
            kb.op("vector", lambda e, tt=tt, pmb=pmb:
                  e.scalar_tensor_tensor(ZPMT[:, tt * 512:(tt + 1) * 512],
                                         pmb[0:RANK, :], 0.5,
                                         ZT[:, tt * 512:(tt + 1) * 512],
                                         OP.mult, OP.mult),
                  waits=[(s_pe, v_pm)], inc=s_dve)
        v_zpm = s_dve.now

        tp_z = dtap("zT", [RANK, T])
        if tp_z is not None:
            kb.dma("sync", tp_z, ZT[:, :], s_sp, waits=[(s_dve, v_zpm)])
        tp_zpm = dtap("zpmT", [RANK, T], BF16)
        if tp_zpm is not None:
            kb.dma("sync", tp_zpm, ZPMT[:, :], s_sp, waits=[(s_dve, v_zpm)])
        tp_h8 = dtap("H8T", [NEXP, T])
        if tp_h8 is not None:
            kb.dma("sync", tp_h8, H8T[:, :], s_sp, waits=[(s_dve, v_h8t)])

        scope_p1.close()
        kb.scope_barrier([s_dve, s_act, s_pe, s_gd, s_ld])

        scope_p2 = contextlib.ExitStack()
        W2T = [scope_p2.enter_context(nc.sbuf_tensor(f"W2T{i}", [128, 32 * 128], BF16))
               for i in range(2)]
        HDS = [scope_p2.enter_context(nc.sbuf_tensor(f"HDS{i}", [128, T], BF16))
               for i in range(2)]
        w2_free = [0, 0]
        hds_free = [0, 0]
        p2bank_free = [0, 0, 0, 0, 0, 0]
        v_hd_spills = 0
        for dmc in range(32):
            wb = dmc % 2
            v_w = kb.dma("gpsimd", W2T[wb][:, :],
                         _ap(hw2.tensor, dmc * 128 * 4096, [[4096, 128], [1, 4096]]),
                         s_w2[wb], waits=[(s_pe, w2_free[wb])])
            pb = (dmc % 3) * 2
            for dc in range(32):
                w = []
                if dc == 0:
                    w = [(s_w2[wb], v_w), (s_dve, max(p2bank_free[pb], p2bank_free[pb + 1],
                                                      v_xn))]
                for tt in range(2):
                    kb.op("tensor",
                          lambda e, wb=wb, dc=dc, tt=tt, pb=pb:
                          e.matmul(banks[pb + tt][:, :],
                                   W2T[wb][:, dc * 128:(dc + 1) * 128],
                                   XN[:, dc * T + tt * 512: dc * T + (tt + 1) * 512],
                                   start=(dc == 0), stop=False),
                          waits=(w if tt == 0 else ()))
            for tt in range(2):
                v_mm = kb.op("tensor",
                             lambda e, dmc=dmc, tt=tt, pb=pb:
                             e.matmul(banks[pb + tt][:, :],
                                      BMATB[:, dmc * 128:(dmc + 1) * 128],
                                      ZPMT[:, tt * 512:(tt + 1) * 512],
                                      start=False, stop=True),
                             waits=([(s_dve, v_zpm)] if dmc == 0 and tt == 0 else ()),
                             inc=s_pe)
            w2_free[wb] = v_mm
            hb = dmc % 2
            for tt in range(2):
                p2bank_free[pb + tt] = kb.op(
                    "vector",
                    lambda e, dmc=dmc, tt=tt, pb=pb, hb=hb:
                    e.tensor_scalar(HDS[hb][:, tt * 512:(tt + 1) * 512],
                                    banks[pb + tt][:, :],
                                    BFFN[:, dmc:dmc + 1], None, OP.add),
                    waits=[(s_pe, v_mm), (s_hdsp2[hb], hds_free[hb])])
            hd_out = _ap(hdsp.tensor, dmc * 128 * 128,
                         [[128, 128], [D * 128, 8], [1, 128]])
            hds_free[hb] = kb.dma("sync", hd_out, HDS[hb][:, :], s_hdsp2[hb],
                                  waits=[(s_dve, p2bank_free[pb + 1])])
            v_hd_spills = hds_free[hb]

        tp_hd = dtap("hd", [8 * D * 128], BF16)
        if tp_hd is not None:
            kb.dma("sync", tp_hd,
                   _ap(hdsp.tensor, 0, [[8 * D * 128, 1], [1, 8 * D * 128]]), s_sp,
                   waits=[(s_hdsp2[0], hds_free[0]), (s_hdsp2[1], hds_free[1])])

        scope_p2.close()
        kb.scope_barrier([s_dve, s_act, s_pe, s_gd, s_sp])

        scope_p3a = contextlib.ExitStack()
        _p3 = lambda name, p, f, dt: scope_p3a.enter_context(
            nc.sbuf_tensor(name, [p, f], dt))
        W1T = [_p3(f"W1T{i}", 128, 32 * 128, BF16) for i in range(2)]
        CIN = [_p3(f"CIN{i}", 128, 1027, BF16) for i in range(2)]
        SPST = [_p3(f"SPST{i}", 128, T, BF16) for i in range(2)]
        DTU = _p3("DTU", 128, T, F32)
        DA = _p3("DA", 128, T, F32)
        LNDT = _p3("LNDT", 128, T, F32)
        LC = _p3("LC", 128, T, F32)

        kb.op("vector", lambda e: e.memset(ONES128[:, :], 1.0))
        kb.op("vector", lambda e: e.memset(CIN[0][:, 0:3], 0.0))
        v_cin0 = kb.op("vector", lambda e: e.memset(CIN[1][:, 0:3], 0.0))

        w1_free = [0, 0]
        spst_free = [0, 0]
        p3bank_free = [None, None, None, None, None, None]
        cin_free = [0, 0]
        fc_order = [144] + list(range(64, 144)) + list(range(0, 64))
        v_dt_pipe = 0
        v_lct = 0
        v_ecum = 0
        v_halos = 0
        v_xbc_spill = 0
        v_gate_spill = 0
        v_q_done = 0

        for idx, fc in enumerate(fc_order):
            wb = idx % 2
            v_w = kb.dma("gpsimd", W1T[wb][:, :],
                         _ap(w1.tensor, fc * 128 * 4096, [[4096, 128], [1, 4096]]),
                         s_w2[wb], waits=[(s_pe, w1_free[wb])])
            pb = (idx % 3) * 2
            for dc in range(32):
                w = []
                if dc == 0:
                    w = [(s_w2[wb], v_w)]
                    for bf in (p3bank_free[pb], p3bank_free[pb + 1]):
                        if bf is not None:
                            w.append(bf)
                for tt in range(2):
                    v_mm = kb.op("tensor",
                                 lambda e, wb=wb, dc=dc, tt=tt, pb=pb:
                                 e.matmul(banks[pb + tt][:, :],
                                          W1T[wb][:, dc * 128:(dc + 1) * 128],
                                          XN[:, dc * T + tt * 512: dc * T + (tt + 1) * 512],
                                          start=(dc == 0), stop=(dc == 31)),
                                 waits=(w if tt == 0 else ()),
                                 inc=(s_pe if dc == 31 and tt == 1 else None))
            w1_free[wb] = v_mm

            if fc == 144:
                for tt in range(2):
                    v_u = kb.op("vector", lambda e, tt=tt, pb=pb:
                                e.tensor_scalar(DTU[:, tt * 512:(tt + 1) * 512],
                                                banks[pb + tt][:, :], DTB[:, 0:1], None,
                                                OP.add),
                                waits=[(s_pe, v_mm), (s_ld, v_consts)])
                p3bank_free[pb] = p3bank_free[pb + 1] = (s_dve, v_u)
                v_abs = kb.op("scalar", lambda e: e.activation(SCR2[:, :], DTU[:, :],
                                                               AF.Abs),
                              waits=[(s_dve, v_u)])
                v_e = kb.op("scalar", lambda e: e.activation(SCR2[:, :], SCR2[:, :],
                                                             AF.Exp, scale=-1.0))
                v_l = kb.op("scalar", lambda e: e.activation(SCR2[:, :], SCR2[:, :],
                                                             AF.Ln, bias=1.0))
                v_r = kb.op("vector", lambda e: e.tensor_scalar_max(DA[:, :], DTU[:, :],
                                                                    0.0))
                kb.wait("vector", s_act, v_l)
                v_dt = kb.op("vector", lambda e: e.tensor_add(DT[:, :], DA[:, :],
                                                              SCR2[:, :]))
                v_da = kb.op("vector", lambda e: e.tensor_scalar_mul(DA[:, :], DT[:, :],
                                                                     ANEG[:, 0:1]),
                             waits=[(s_act, v_aneg)])
                for c in range(NCHUNK):
                    v_cum = kb.op("vector", lambda e, c=c:
                                  e.tensor_tensor_scan(CUM[:, c * CS:(c + 1) * CS],
                                                       ONES128[:, :],
                                                       DA[:, c * CS:(c + 1) * CS],
                                                       0.0, OP.mult, OP.add))
                kb.wait("scalar", s_dve, v_dt)
                v_lndt = kb.op("scalar", lambda e: e.activation(LNDT[:, :], DT[:, :],
                                                                AF.Ln))
                v_ecum = kb.op("scalar", lambda e, v=v_cum: e.activation(
                    ECUM[:, :], CUM[:, :], AF.Exp), waits=[(s_dve, v_cum)])
                kb.wait("vector", s_act, v_lndt)
                v_lc = kb.op("vector", lambda e: e.tensor_sub(LC[:, :], LNDT[:, :],
                                                              CUM[:, :]))
                for g8 in range(8):
                    tb = 6 + (g8 % 2)
                    kb.op("tensor", lambda e, g8=g8, tb=tb:
                          e.matmul(banks[tb][:, 0:128], LC[:, g8 * 128:(g8 + 1) * 128],
                                   IDN[:, :], is_transpose=True, start=True, stop=True),
                          waits=[(s_dve, v_lc if g8 < 2 else 0),
                                 (s_dve, bank_free[tb])],
                          inc=s_pe)
                    v_tpe = s_pe.now
                    bank_free[tb] = kb.op("vector", lambda e, g8=g8, tb=tb:
                                          e.tensor_copy(LCT[:, g8 * 128:(g8 + 1) * 128],
                                                        banks[tb][:, 0:128]),
                                          waits=[(s_pe, v_tpe)])
                v_lct = bank_free[tb]
                kb.op("vector", lambda e: e.tensor_copy(Q1[:, :], CUM[:, :]))
                kb.op("vector", lambda e: e.tensor_tensor(SCR2[:, :], CUM[:, :], Q1[:, :],
                                                          OP.subtract))
                kb.op("vector", lambda e: e.tensor_copy(Q2[:, :], SCR2[:, :]))
                kb.op("vector", lambda e: e.tensor_tensor(SCR2[:, :], SCR2[:, :], Q2[:, :],
                                                          OP.subtract))
                v_q_done = kb.op("vector", lambda e: e.tensor_copy(R2[:, :], SCR2[:, :]))
                v_dt_pipe = v_q_done
            elif fc >= 64:
                cc = fc - 64
                cb = idx % 2
                for tt in range(2):
                    v_cc = kb.op("scalar", lambda e, tt=tt, pb=pb, cb=cb:
                                 e.copy(CIN[cb][:, 3 + tt * 512: 3 + (tt + 1) * 512],
                                        banks[pb + tt][:, :]),
                                 waits=([(s_pe, v_mm), (s_dve, cin_free[cb])]
                                        if tt == 0 else ()))
                p3bank_free[pb] = p3bank_free[pb + 1] = (s_act, v_cc)
                v_t0 = kb.op("vector", lambda e, cb=cb, cc=cc:
                             e.tensor_scalar_mul(ACC[:, :], CIN[cb][:, 0:T],
                                                 CONVW[:, cc * KC:cc * KC + 1]),
                             waits=[(s_act, v_cc)])
                for k in range(1, 4):
                    v_tk = kb.op("vector", lambda e, cb=cb, cc=cc, k=k:
                                 e.scalar_tensor_tensor(ACC[:, :], CIN[cb][:, k:k + T],
                                                        CONVW[:, cc * KC + k:cc * KC + k + 1],
                                                        ACC[:, :], OP.mult, OP.add))
                cin_free[cb] = v_tk
                kb.op("vector", lambda e, cc=cc:
                      e.tensor_copy(CR3[:, cc * 3:(cc + 1) * 3], ACC[:, 0:3]))
                v_halo = kb.op("vector", lambda e, cc=cc, cb=cb:
                               e.tensor_copy(HALOS[:, cc * 3:(cc + 1) * 3],
                                             CIN[cb][:, 1024:1027]))
                if cc == NXBC - 1:
                    v_halos = v_halo
                sbuf_i = idx % 2
                v_silu = kb.op("scalar", lambda e, cc=cc, sbuf_i=sbuf_i:
                               e.activation(SPST[sbuf_i][:, :], ACC[:, :], AF.Silu,
                                            bias=CONVB[:, cc:cc + 1]),
                               waits=[(s_dve, v_halo),
                                      (s_spst2[sbuf_i], spst_free[sbuf_i])])
                spst_free[sbuf_i] = kb.dma("sync",
                                           _ap(xbcsp.tensor, cc * 128 * T,
                                               [[T, 128], [1, T]]),
                                           SPST[sbuf_i][:, :], s_spst2[sbuf_i],
                                           waits=[(s_act, v_silu)])
                v_xbc_spill = spst_free[sbuf_i]
            else:
                sbuf_i = idx % 2
                for tt in range(2):
                    v_gn = kb.op("scalar", lambda e, tt=tt, pb=pb, sbuf_i=sbuf_i:
                                 e.copy(SPST[sbuf_i][:, tt * 512:(tt + 1) * 512],
                                        banks[pb + tt][:, :]),
                                 waits=([(s_pe, v_mm),
                                         (s_spst2[sbuf_i], spst_free[sbuf_i])]
                                        if tt == 0 else ()))
                p3bank_free[pb] = p3bank_free[pb + 1] = (s_act, v_gn)
                spst_free[sbuf_i] = kb.dma("sync",
                                           _ap(gatesp.tensor, fc * 128 * T,
                                               [[T, 128], [1, T]]),
                                           SPST[sbuf_i][:, :], s_spst2[sbuf_i],
                                           waits=[(s_act, v_gn)])
                v_gate_spill = spst_free[sbuf_i]

        tp_xbc = dtap("xbcsp", [NXBC * 128 * T], BF16)
        if tp_xbc is not None:
            kb.dma("sync", tp_xbc,
                   _ap(xbcsp.tensor, 0, [[NXBC * 128 * T, 1], [1, NXBC * 128 * T]]),
                   s_sp, waits=[(s_spst2[0], spst_free[0]), (s_spst2[1], spst_free[1])])
        tp_dt = dtap("dt", [128, T])
        if tp_dt is not None:
            kb.dma("sync", tp_dt, DT[:, :], s_sp, waits=[(s_dve, v_dt_pipe)])
        tp_cum = dtap("cum", [128, T])
        if tp_cum is not None:
            kb.dma("sync", tp_cum, CUM[:, :], s_sp, waits=[(s_dve, v_dt_pipe)])

        v_hs = kb.op("vector", lambda e: e.tensor_scalar_mul(HALOS[:, :], HALOS[:, :],
                                                             ISH0[:, 0:1]),
                     waits=[(s_dve, v_halos), (s_ld, v_consts)])
        v_c2aw = kb.dma("gpsimd", cc2a_in, HALOS[:, :], s_gd, waits=[(s_dve, v_hs)])
        kb.op("gpsimd", lambda e: e.collective_compute(
            "AllReduce", OP.add, replica_groups=[[0, 1], [2, 3], [4, 5], [6, 7]],
            ins=[cc2a_in.opt()], outs=[cc2a_out.opt()]),
            waits=[(s_gd, v_c2aw)], inc=s_cc2a, chain=False)
        kb.wait("gpsimd", s_cc2a, 1)
        v_hb = kb.dma("gpsimd", HALOS[:, :], cc2a_out, s_gd)
        v_f = kb.op("vector", lambda e: e.tensor_scalar_mul(HALOS[:, :], HALOS[:, :],
                                                            ISH1[:, 0:1]),
                    waits=[(s_gd, v_hb)])
        for t in range(3):
            for k in range(0, 3 - t):
                kb.op("vector", lambda e, t=t, k=k: e.tensor_tensor(
                    _ap(SCR2, 0, [[T, 128], [3, NXBC]]),
                    _ap(HALOS, t + k, [[NXBC * 3, 128], [3, NXBC]]),
                    _ap(CONVW, k, [[NXBC * KC, 128], [KC, NXBC]]),
                    OP.mult))
                kb.op("vector", lambda e, t=t: e.tensor_tensor(
                    _ap(CR3, t, [[NXBC * 3, 128], [3, NXBC]]),
                    _ap(CR3, t, [[NXBC * 3, 128], [3, NXBC]]),
                    _ap(SCR2, 0, [[T, 128], [3, NXBC]]),
                    OP.add))
        v_fb = kb.op("vector", lambda e: e.tensor_tensor(
            CR3[:, :], CR3[:, :],
            _ap(CONVB, 0, [[NXBC, 128], [1, NXBC], [0, 3]]), OP.add))
        v_fix = kb.op("scalar", lambda e: e.activation(XBCFIX[:, :], CR3[:, :], AF.Silu),
                      waits=[(s_dve, v_fb)])

        tp_fix = dtap("xbcfix", [128, NXBC * 3], BF16)
        if tp_fix is not None:
            kb.dma("sync", tp_fix, XBCFIX[:, :], s_sp, waits=[(s_act, v_fix)])

        scope_p3a.close()
        scope_xn.close()
        kb.scope_barrier([s_dve, s_act, s_pe, s_gd, s_sp, s_cc])

        slsp = kb.dint("slsp", [NCHUNK, 128, INTER], BF16)
        scope_p3b = contextlib.ExitStack()
        _pb = lambda name, p, f, dt: scope_p3b.enter_context(
            nc.sbuf_tensor(name, [p, f], dt))
        S = _pb("S", 128, INTER, F32)
        HSB = [_pb(f"HSB{i}", 128, 2048, BF16) for i in range(2)]
        X_TOK = [_pb(f"XTOK{i}", 128, INTER, BF16) for i in range(2)]
        XW = [_pb(f"XW{i}", 128, INTER, BF16) for i in range(2)]
        BFM3 = _pb("BFM3", 128, 2048, BF16)
        B_TOK = [_pb(f"BTOK{i}", 128, 1024, BF16) for i in range(2)]
        SB16 = _pb("SB16", 128, INTER, BF16)
        CDF = _pb("CDF", 1, 4 * 128, F32)
        CDECB = _pb("CDECB", 128, 128, F32)

        v_szero = kb.op("vector", lambda e: e.memset(S[:, :], 0.0))
        hsb_free = [(None, 0), (None, 0)]
        state_bank_free = [None] * 8
        v_cdf = 0
        v_xtok_spill = {}
        v_sl_spill = {}
        v_states_last = 0
        v_xtspill_last = 0
        v_slspill_last = 0

        for c in range(NCHUNK):
            v_g = kb.dma("sync", _ap(CDF, c * 128, [[512, 1], [1, 128]]),
                         _ap(CUM, c * CS + (CS - 1), [[T, 128], [1, 1]]), s_cdf,
                         waits=[(s_dve, v_dt_pipe)])
            v_cdf = kb.op("scalar", lambda e, c=c: e.activation(
                _ap(CDF, c * 128, [[512, 1], [1, 128]]),
                _ap(CDF, c * 128, [[512, 1], [1, 128]]), AF.Exp),
                waits=[(s_cdf, v_g)])

            v_dl = kb.op("scalar", lambda e, c=c: e.activation(
                SCR2[:, 0:CS], CUM[:, c * CS:(c + 1) * CS], AF.Exp, scale=-1.0,
                bias=CUM[:, c * CS + CS - 1:c * CS + CS]),
                waits=[(s_dve, v_dt_pipe)])
            v_w = kb.op("vector", lambda e, c=c: e.tensor_mul(
                SCR2[:, CS:2 * CS], SCR2[:, 0:CS], DT[:, c * CS:(c + 1) * CS]),
                waits=[(s_act, v_dl)])
            for jt in range(2):
                tb = 6 + jt
                v_tp = kb.op("tensor", lambda e, jt=jt, tb=tb:
                             e.matmul(banks[tb][:, 0:128],
                                      SCR2[:, CS + jt * 128:CS + (jt + 1) * 128],
                                      IDN[:, :], is_transpose=True, start=True,
                                      stop=True),
                             waits=[(s_dve, max(v_w, bank_free[tb]))], inc=s_pe)
                bank_free[tb] = kb.op("vector", lambda e, c=c, jt=jt, tb=tb:
                                      e.tensor_copy(
                                          WDTT[:, (2 * c + jt) * 128:(2 * c + jt + 1) * 128],
                                          banks[tb][:, 0:128]),
                                      waits=[(s_pe, v_tp)])

            for blk in range(8):
                hb = blk % 2
                v_h = kb.dma("sync", HSB[hb][:, :],
                             _ap(xbcsp.tensor, blk * 8 * 128 * T + c * CS,
                                 [[T, 128], [128 * T, 8], [1, CS]]), s_hsb2[hb],
                             waits=[hsb_free[hb]] if hsb_free[hb][0] else [])
                v_patch = 0
                if c == 0:
                    v_patch = kb.op("vector", lambda e, blk=blk:
                                    e.tensor_copy(
                                        _ap(HSB[blk % 2], 0, [[2048, 128], [CS, 8], [1, 3]]),
                                        _ap(XBCFIX, blk * 24, [[240, 128], [3, 8], [1, 3]])),
                                    waits=[(s_hsb2[hb], v_h), (s_act, v_fix)])
                for sub in range(8):
                    cc = blk * 8 + sub
                    for jt in range(2):
                        tb = 6 + jt
                        w = [(s_hsb2[hb], v_h), (s_dve, max(bank_free[tb], v_patch))] \
                            if sub == 0 else [(s_dve, bank_free[tb])]
                        v_tp = kb.op("tensor", lambda e, hb=hb, sub=sub, jt=jt, tb=tb:
                                     e.matmul(banksb[tb][:, 0:128],
                                              HSB[hb][:, sub * CS + jt * 128:
                                                      sub * CS + (jt + 1) * 128],
                                              IDNB[:, :], is_transpose=True,
                                              start=True, stop=True),
                                     waits=w, inc=s_pe)
                        w_ev = [(s_pe, v_tp)]
                        if blk == 0 and sub == 0 and v_xtspill_last:
                            w_ev.append((s_xtsp, v_xtspill_last))
                        bank_free[tb] = kb.op(
                            "vector", lambda e, cc=cc, jt=jt, tb=tb:
                            e.tensor_copy(X_TOK[jt][:, cc * 128:(cc + 1) * 128],
                                          banksb[tb][:, 0:128]),
                            waits=w_ev)
                hsb_free[hb] = (s_dve, bank_free[7])
            v_xtok = bank_free[7]
            v_xw = 0
            for jt in range(2):
                v_xw = kb.op("vector", lambda e, c=c, jt=jt: e.tensor_tensor(
                    XW[jt][:, :], X_TOK[jt][:, :],
                    _ap(WDTT, (2 * c + jt) * 128, [[1024, 128], [1, 128], [0, PD]]),
                    OP.mult),
                    waits=([(s_pe, v_states_last)] if jt == 0 else ()))
            for jt in range(2):
                v_xtok_spill[(c, jt)] = kb.dma(
                    "sync", _ap(xtoksp.tensor, (c * 2 + jt) * 128 * INTER,
                                [[INTER, 128], [1, INTER]]),
                    X_TOK[jt][:, :], s_xtsp, waits=[(s_dve, v_xw)])
            v_xtspill_last = v_xtok_spill[(c, 1)]

            if c > 0:
                v_sb = kb.op("vector", lambda e: e.tensor_copy(SB16[:, :], S[:, :]),
                             waits=([(s_slsp, v_slspill_last)] if v_slspill_last else ()))
                v_sl_spill[c] = kb.dma("sync",
                                       _ap(slsp.tensor, c * 128 * INTER,
                                           [[INTER, 128], [1, INTER]]),
                                       SB16[:, :], s_slsp, waits=[(s_dve, v_sb)])
                v_slspill_last = v_sl_spill[c]

            v_b = kb.dma("sync", BFM3[:, :],
                         _ap(xbcsp.tensor, 64 * 128 * T + c * CS,
                             [[T, 128], [128 * T, 8], [1, CS]]), s_bfm,
                         waits=[(s_dve, v_xw)])
            v_bpatch = 0
            if c == 0:
                v_bpatch = kb.op("vector", lambda e: e.tensor_copy(
                    _ap(BFM3, 0, [[2048, 128], [CS, 8], [1, 3]]),
                    _ap(XBCFIX, 64 * 3, [[240, 128], [3, 8], [1, 3]])),
                    waits=[(s_bfm, v_b), (s_act, v_fix)])
            for g in range(G):
                for jt in range(2):
                    tb = 6 + jt
                    w = [(s_bfm, v_b), (s_dve, max(bank_free[tb], v_bpatch))] \
                        if g == 0 else [(s_dve, bank_free[tb])]
                    v_tp = kb.op("tensor", lambda e, g=g, jt=jt, tb=tb:
                                 e.matmul(banksb[tb][:, 0:128],
                                          BFM3[:, g * CS + jt * 128:g * CS + (jt + 1) * 128],
                                          IDNB[:, :], is_transpose=True,
                                          start=True, stop=True),
                                 waits=w, inc=s_pe)
                    bank_free[tb] = kb.op(
                        "vector", lambda e, g=g, jt=jt, tb=tb:
                        e.tensor_copy(B_TOK[jt][:, g * 128:(g + 1) * 128],
                                      banksb[tb][:, 0:128]),
                        waits=[(s_pe, v_tp)])
            v_btok = bank_free[7]

            v_cb = kb.op("tensor", lambda e, c=c: e.matmul(
                banks[5][:, 0:128], ONESF[0:1, :],
                _ap(CDF, c * 128, [[512, 1], [1, 128]]),
                start=True, stop=True),
                waits=[(s_act, v_cdf), (s_dve, max(v_ones, state_bank_free[5][1]
                                                   if state_bank_free[5] else 0))],
                inc=s_pe)
            v_cdecb = kb.op("vector", lambda e: e.tensor_copy(CDECB[:, :],
                                                              banks[5][:, 0:128]),
                            waits=[(s_pe, v_cb)])
            state_bank_free[5] = (s_dve, v_cdecb)

            for g in range(G):
                bka, bkb = (0, 1) if g % 2 == 0 else (2, 3)
                for jt in range(2):
                    for hh in range(16):
                        h = g * 16 + hh
                        bk = bka if hh < 8 else bkb
                        reg = (hh % 8) * 64
                        w = []
                        if jt == 0 and hh == 0:
                            w = [(s_dve, v_btok)]
                            for bkx in (bka, bkb):
                                if state_bank_free[bkx]:
                                    w.append(state_bank_free[bkx])
                        v_smm = kb.op("tensor", lambda e, g=g, jt=jt, h=h, bk=bk, reg=reg,
                                      hh=hh:
                                      e.matmul(banks[bk][:, reg:reg + PD],
                                               B_TOK[jt][:, g * 128:(g + 1) * 128],
                                               XW[jt][:, h * PD:(h + 1) * PD],
                                               start=(jt == 0 and hh % 8 == 0),
                                               stop=(jt == 1 and hh % 8 == 7)),
                                      waits=w,
                                      inc=(s_pe if jt == 1 and hh == 15 else None))
                for hh in range(16):
                    h = g * 16 + hh
                    bk = bka if hh < 8 else bkb
                    reg = (hh % 8) * 64
                    w = [(s_pe, v_smm), (s_dve, v_cdecb)] if hh == 0 else []
                    v_su = kb.op("vector", lambda e, h=h, bk=bk, reg=reg:
                                 e.scalar_tensor_tensor(S[:, h * PD:(h + 1) * PD],
                                                        S[:, h * PD:(h + 1) * PD],
                                                        CDECB[:, h:h + 1],
                                                        banks[bk][:, reg:reg + PD],
                                                        OP.mult, OP.add),
                                 waits=w)
                state_bank_free[bka] = state_bank_free[bkb] = (s_dve, v_su)
                v_states_last = v_smm

        v_ss = kb.op("vector", lambda e: e.tensor_scalar_mul(S[:, :], S[:, :],
                                                             ISH0[:, 0:1]))
        v_c2bw = kb.dma("gpsimd", cc2b_in, S[:, :], s_gd, waits=[(s_dve, v_ss)])
        kb.op("gpsimd", lambda e: e.collective_compute(
            "AllReduce", OP.add, replica_groups=[[0, 1], [2, 3], [4, 5], [6, 7]],
            ins=[cc2b_in.opt()], outs=[cc2b_out.opt()]),
            waits=[(s_gd, v_c2bw)], inc=s_cc2b, chain=False)
        v_c2b = 1

        kb.op("vector", lambda e: e.memset(PBF[0:1, 0:128], 1.0))
        for c in range(1, NCHUNK):
            kb.op("vector", lambda e, c=c: e.tensor_mul(
                PBF[0:1, c * 128:(c + 1) * 128],
                PBF[0:1, (c - 1) * 128:c * 128],
                CDF[0:1, (c - 1) * 128:c * 128]),
                waits=([(s_act, v_cdf)] if c == 1 else []))
        v_pbf = kb.op("vector", lambda e: e.tensor_scalar_mul(PBF[0:1, :], PBF[0:1, :],
                                                              ISH1[0:1, 0:1]))
        scope_p3b.close()
        kb.scope_barrier([s_dve, s_act, s_pe, s_gd, s_sp, s_ld, s_cc])

        scope_p4 = contextlib.ExitStack()
        _p4 = lambda name, p, f, dt: scope_p4.enter_context(
            nc.sbuf_tensor(name, [p, f], dt))
        XT4 = [_p4(f"XT4_{i}", 128, INTER, BF16) for i in range(2)]
        CB4 = _p4("CB4", 128, 2048, BF16)
        BB4 = _p4("BB4", 128, 2048, BF16)
        SLB = _p4("SLB", 128, INTER, BF16)
        S4B = _p4("S4B", 128, INTER, BF16)
        S4S = _p4("S4S", 128, INTER, BF16)
        PBBE = _p4("PBBE", 128, 128, F32)
        GM = _p4("GM", 128, G * 512, BF16)
        FCB3 = [_p4(f"FCB3_{i}", 3, 2048, BF16) for i in range(2)]
        EROWS = [_p4(f"EROWS{i}", 1, 2048, BF16) for i in range(2)]
        EB = [_p4(f"EB{i}", 128, 512, BF16) for i in range(3)]
        SC4 = [_p4(f"SC4_{i}", 128, 512, BF16) for i in range(4)]
        CX4 = [_p4(f"CX4_{i}", 128, 256, BF16) for i in range(4)]
        HS4 = _p4("HS4", 128, 2048, BF16)
        GT4 = _p4("GT4", 128, 2048, BF16)
        SILG = _p4("SILG", 128, 2048, BF16)
        GG4 = _p4("GG4", 128, 2048, F32)
        GSQ4 = [_p4(f"GSQ4_{i}", 128, 256, BF16) for i in range(2)]
        RSR = _p4("RSR", 1, 256, F32)
        RSBE = _p4("RSBE", 128, 256, F32)
        YGST = [_p4(f"YGST{i}", 128, 2048, BF16) for i in range(2)]
        ONECOL = _p4("ONECOL", 128, 1, BF16)
        CROW = _p4("CROW", 1, 4, F32)

        v_onecol = kb.op("vector", lambda e: e.memset(ONECOL[:, :], 1.0))
        v_crow = kb.op("vector", lambda e: e.memset(CROW[0:1, 0:1], 1e-5))
        kb.wait("gpsimd", s_cc2b, 1)
        v_s4 = kb.dma("gpsimd", S4B[:, :], cc2b_out, s_gd)
        tp_s4 = dtap("s4", [128, INTER])
        if tp_s4 is not None:
            kb.dma("sync", tp_s4, cc2b_out, s_sp, waits=[(s_gd, v_s4)])

        if "sc0" in debug:
            dbg["sc0"] = kb.dout("dbg_sc0", [128, 512], BF16)
            dbg["e0"] = kb.dout("dbg_e0", [128, 512], BF16)
            dbg["g0"] = kb.dout("dbg_g0", [128, 512], BF16)
        yraw_tap_on = "yraw" in debug
        yraw_guard = 0
        djbank_exp = [(None, 0), (None, 0)]
        eb_free = [(None, 0)] * 3
        sc_free = [(None, 0)] * 4
        cx_free = [(None, 0)] * 4
        ybank_evac = [(None, 0), (None, 0)]
        ybank_ready = [0, 0]
        fcb_guard = [(None, 0), (None, 0)]
        ebcast_guard = (None, 0)
        gsq_guard = [(None, 0), (None, 0)]
        gm_dve = 0
        pbb_guard = 0
        silg_guard = 0
        hs4_guard = 0
        pool_gg_last = 0
        var_guard = 0
        rsb_guard = 0
        ygst_guard = [(None, 0), (None, 0)]
        v_ebc = 0
        v_yg_spill = 0
        re_guard = []

        for c in range(NCHUNK):
            w_re = re_guard
            v_x0 = kb.dma("sync", XT4[0][:, :],
                          _ap(xtoksp.tensor, (c * 2 + 0) * 128 * INTER,
                              [[INTER, 128], [1, INTER]]), s_x42[0], waits=w_re)
            v_x1 = kb.dma("sync", XT4[1][:, :],
                          _ap(xtoksp.tensor, (c * 2 + 1) * 128 * INTER,
                              [[INTER, 128], [1, INTER]]), s_x42[1], waits=w_re)
            v_bb = kb.dma("sync", BB4[:, :],
                          _ap(xbcsp.tensor, 64 * 128 * T + c * CS,
                              [[T, 128], [128 * T, 8], [1, CS]]), s_bc, waits=w_re)
            v_cb4 = kb.dma("sync", CB4[:, :],
                           _ap(xbcsp.tensor, 72 * 128 * T + c * CS,
                               [[T, 128], [128 * T, 8], [1, CS]]), s_bc, waits=w_re)
            v_slb = 0
            if c > 0:
                v_slb = kb.dma("sync", SLB[:, :],
                               _ap(slsp.tensor, c * 128 * INTER,
                                   [[INTER, 128], [1, INTER]]), s_slbr, waits=w_re)
            v_bpat = v_cpat = 0
            if c == 0:
                v_bpat = kb.op("vector", lambda e: e.tensor_copy(
                    _ap(BB4, 0, [[2048, 128], [CS, 8], [1, 3]]),
                    _ap(XBCFIX, 64 * 3, [[240, 128], [3, 8], [1, 3]])),
                    waits=[(s_bc, v_cb4), (s_act, v_fix)])
                v_cpat = kb.op("vector", lambda e: e.tensor_copy(
                    _ap(CB4, 0, [[2048, 128], [CS, 8], [1, 3]]),
                    _ap(XBCFIX, 72 * 3, [[240, 128], [3, 8], [1, 3]])),
                    waits=[(s_bc, v_cb4), (s_act, v_fix)])

            for g in range(G):
                gb = 6 + (g % 2)
                for jt in range(2):
                    w = []
                    if jt == 0:
                        w = [(s_bc, v_cb4), (s_dve, max(v_bpat, v_cpat))]
                        if g >= 2:
                            w.append((s_dve, gm_dve))
                        if c > 0 and g < 2:
                            w = w + re_guard
                    v_gmm = kb.op("tensor", lambda e, g=g, jt=jt, gb=gb:
                                  e.matmul(banks[gb][:, jt * CS:(jt + 1) * CS],
                                           BB4[:, g * CS + jt * 128:g * CS + (jt + 1) * 128],
                                           CB4[:, g * CS:(g + 1) * CS],
                                           start=(jt == 0), stop=(jt == 1)),
                                  waits=w, inc=(s_pe if jt == 1 else None))
                gm_dve = kb.op("vector", lambda e, g=g, gb=gb:
                               e.tensor_copy(GM[:, g * 512:(g + 1) * 512],
                                             banks[gb][:, :]),
                               waits=[(s_pe, v_gmm)])
            v_gm = gm_dve

            v_pbb = kb.op("tensor", lambda e, c=c: e.matmul(
                banks[5][:, 0:128], ONESF[0:1, :],
                _ap(PBF, c * 128, [[512, 1], [1, 128]]),
                start=True, stop=True),
                waits=[(s_dve, max(v_pbf, pbb_guard))], inc=s_pe)
            pbb_guard = kb.op("vector", lambda e: e.tensor_copy(PBBE[:, :],
                                                                banks[5][:, 0:128]),
                              waits=[(s_pe, v_pbb)])
            v_s4s = kb.op("vector", lambda e: e.tensor_tensor(
                S4S[:, :], S4B[:, :],
                _ap(PBBE, 0, [[128, 128], [1, 128], [0, PD]]), OP.mult),
                waits=[(s_dve, pbb_guard), (s_gd, v_s4)] + re_guard)

            for g in range(G):
                v_gt = kb.dma("sync", GT4[:, :],
                              _ap(gatesp.tensor, g * 8 * 128 * T + c * CS,
                                  [[T, 128], [128 * T, 8], [1, CS]]), s_gt4r,
                              waits=[(s_act, silg_guard)] if c + g > 0 else [])
                v_hs4 = kb.dma("sync", HS4[:, :],
                               _ap(xbcsp.tensor, g * 8 * 128 * T + c * CS,
                                   [[T, 128], [128 * T, 8], [1, CS]]), s_hs4r,
                               waits=[(s_dve, hs4_guard)] if c + g > 0 else [])
                v_hpat = 0
                if c == 0:
                    v_hpat = kb.op("vector", lambda e, g=g: e.tensor_copy(
                        _ap(HS4, 0, [[2048, 128], [CS, 8], [1, 3]]),
                        _ap(XBCFIX, g * 24, [[240, 128], [3, 8], [1, 3]])),
                        waits=[(s_hs4r, v_hs4), (s_act, v_fix)])
                silg_guard = kb.op("scalar", lambda e: e.activation(
                    SILG[:, :], GT4[:, :], AF.Silu),
                    waits=[(s_gt4r, v_gt), (s_pool, pool_gg_last)])

                for hb2 in range(2):
                    hb = g * 2 + hb2
                    fb = hb % 2
                    wfl = [fcb_guard[fb]] if fcb_guard[fb][0] else []
                    v_f1 = kb.dma("sync", _ap(FCB3[fb], 0, [[2048, 1], [1, 2048]]),
                                  _ap(Q1, hb * 8 * T + c * CS, [[T, 8], [1, CS]]),
                                  s_fcb2[fb], waits=wfl)
                    v_f2 = kb.dma("sync", _ap(FCB3[fb], 2048, [[2048, 1], [1, 2048]]),
                                  _ap(Q2, hb * 8 * T + c * CS, [[T, 8], [1, CS]]),
                                  s_fcb2[fb], waits=wfl)
                    v_f3 = kb.dma("sync", _ap(FCB3[fb], 4096, [[2048, 1], [1, 2048]]),
                                  _ap(R2, hb * 8 * T + c * CS, [[T, 8], [1, CS]]),
                                  s_fcb2[fb], waits=wfl)
                    v_f4 = kb.dma("sync", EROWS[fb][0:1, :],
                                  _ap(ECUM, hb * 8 * T + c * CS, [[T, 8], [1, CS]]),
                                  s_fcb2[fb], waits=wfl)
                    v_fl = v_f4
                    last_dmm = 0
                    for hh in range(8):
                        h = hb * 8 + hh
                        dbk = h % 2
                        ebuf = h % 3
                        sbuf4 = h % 4
                        cbuf = h % 4
                        w = [(s_fcb2[fb], v_fl)]
                        if djbank_exp[dbk][0]:
                            w.append(djbank_exp[dbk])
                        kb.op("tensor", lambda e, dbk=dbk:
                              e.matmul(banks[dbk][:, :], IDNB[:, :], MASKP[:, :],
                                       start=True, stop=False),
                              waits=w)
                        last_dmm = kb.op("tensor", lambda e, dbk=dbk, hh=hh, fb=fb:
                                         e.matmul(banks[dbk][:, :], ONESB[0:3, :],
                                                  _ap(FCB3[fb], hh * CS,
                                                      [[2048, 3], [0, 2], [1, CS]]),
                                                  start=False, stop=True),
                                         waits=[], inc=s_pe)
                        w_e = [(s_pe, last_dmm)]
                        if eb_free[ebuf][0]:
                            w_e.append(eb_free[ebuf])
                        for jt in range(2):
                            v_exp = kb.op("scalar", lambda e, dbk=dbk, ebuf=ebuf, jt=jt,
                                          c=c, h=h:
                                          e.activation(EB[ebuf][:, jt * CS:(jt + 1) * CS],
                                                       banks[dbk][:, jt * CS:(jt + 1) * CS],
                                                       AF.Exp,
                                                       bias=LCT[:, (2 * c + jt) * 128 + h:
                                                                (2 * c + jt) * 128 + h + 1]),
                                          waits=(w_e if jt == 0 else []))
                        djbank_exp[dbk] = (s_act, v_exp)
                        w_s = [(s_act, v_exp), (s_dve, v_gm)]
                        if sc_free[sbuf4][0]:
                            w_s.append(sc_free[sbuf4])
                        eng_s = "vector" if h % 2 == 0 else "gpsimd"
                        v_sc = kb.op(eng_s, lambda e, ebuf=ebuf, sbuf4=sbuf4, g=g:
                                     e.tensor_tensor(SC4[sbuf4][:, :], EB[ebuf][:, :],
                                                     GM[:, g * 512:(g + 1) * 512],
                                                     OP.mult),
                                     waits=w_s)
                        sc_ready = (s_dve if h % 2 == 0 else s_pool, v_sc)
                        if c == 0 and h == 0 and "sc0" in debug:
                            kb.dma("sync", dbg["sc0"], SC4[sbuf4][:, :], s_sp,
                                   waits=[sc_ready])
                            kb.dma("sync", dbg["e0"], EB[ebuf][:, :], s_sp,
                                   waits=[sc_ready])
                            kb.dma("sync", dbg["g0"], GM[:, 0:512], s_sp,
                                   waits=[(s_dve, v_gm)])
                        eb_free[ebuf] = sc_ready
                        if hh % 2 == 0:
                            w_eb = [(s_fcb2[fb], v_fl)]
                            if ebcast_guard[0]:
                                w_eb.append(ebcast_guard)
                            v_ebc = kb.op("tensor", lambda e, hh=hh, fb=fb:
                                          e.matmul(banks[2][:, :], ONESB[0:1, :],
                                                   EROWS[fb][0:1, hh * CS:(hh + 2) * CS],
                                                   start=True, stop=True),
                                          waits=w_eb, inc=s_pe)
                        w_c = [(s_pe, v_ebc), (s_bc, v_cb4)]
                        if cx_free[cbuf][0]:
                            w_c.append(cx_free[cbuf])
                        if c == 0:
                            w_c.append((s_dve, v_cpat))
                        v_cx = kb.op("vector", lambda e, cbuf=cbuf, g=g, hh=hh:
                                     e.tensor_tensor(CX4[cbuf][:, :],
                                                     CB4[:, g * CS:(g + 1) * CS],
                                                     banks[2][:, (hh % 2) * CS:
                                                              (hh % 2 + 1) * CS],
                                                     OP.mult),
                                     waits=w_c)
                        if hh % 2 == 1:
                            ebcast_guard = (s_dve, v_cx)
                        ybk = 3 + (h // 4) % 2
                        preg = (h % 2) * 64
                        creg = ((h % 4) // 2) * CS
                        w_y = [sc_ready, (s_x42[0], v_x0), (s_x42[1], v_x1)]
                        if h % 4 < 2 and ybank_evac[ybk - 3][0]:
                            w_y.append(ybank_evac[ybk - 3])
                        kb.op("tensor", lambda e, h=h, ybk=ybk, preg=preg, creg=creg,
                              sbuf4=sbuf4:
                              e.matmul(banks[ybk][preg:preg + PD, creg:creg + CS],
                                       XT4[0][:, h * PD:(h + 1) * PD],
                                       SC4[sbuf4][:, 0:CS],
                                       start=(h % 4 < 2), stop=False,
                                       skip_group_check=True),
                              waits=w_y)
                        kb.op("tensor", lambda e, h=h, ybk=ybk, preg=preg, creg=creg,
                              sbuf4=sbuf4:
                              e.matmul(banks[ybk][preg:preg + PD, creg:creg + CS],
                                       XT4[1][:, h * PD:(h + 1) * PD],
                                       SC4[sbuf4][:, CS:2 * CS],
                                       start=False, stop=False,
                                       skip_group_check=True))
                        w_yi = [(s_dve, v_cx)]
                        if c > 0:
                            kb.op("tensor", lambda e, h=h, ybk=ybk, preg=preg,
                                  creg=creg, cbuf=cbuf:
                                  e.matmul(banks[ybk][preg:preg + PD, creg:creg + CS],
                                           SLB[:, h * PD:(h + 1) * PD],
                                           CX4[cbuf][:, :],
                                           start=False, stop=False,
                                           skip_group_check=True),
                                  waits=[(s_slbr, v_slb), (s_dve, v_cx)])
                        v_ymm = kb.op("tensor", lambda e, h=h, ybk=ybk, preg=preg,
                                      creg=creg, cbuf=cbuf:
                                      e.matmul(banks[ybk][preg:preg + PD, creg:creg + CS],
                                               S4S[:, h * PD:(h + 1) * PD],
                                               CX4[cbuf][:, :],
                                               start=False, stop=(h % 4 >= 2),
                                               skip_group_check=True),
                                      waits=w_yi + [(s_dve, v_s4s)],
                                      inc=(s_pe if h % 4 == 3 else None))
                        if h % 4 == 3:
                            for _i in range(4):
                                sc_free[_i] = (s_pe, v_ymm)
                                cx_free[_i] = (s_pe, v_ymm)
                            ybank_ready[ybk - 3] = v_ymm
                    fcb_guard[fb] = (s_pe, s_pe.n)
                    for cc8 in range(hb2 * 4, hb2 * 4 + 4):
                        cc = g * 8 + cc8
                        ybk = 3 + ((cc % 4) // 2)
                        creg = (cc % 2) * CS
                        w_e2 = [(s_pe, ybank_ready[ybk - 3]), (s_hs4r, v_hs4),
                                (s_pool, pool_gg_last)]
                        if c == 0:
                            w_e2.append((s_dve, v_hpat))
                        if yraw_tap_on and yraw_guard:
                            w_e2 = w_e2 + [(s_sp, yraw_guard)]
                        v_gpre = kb.op("vector",
                                       lambda e, cc=cc, cc8=cc8, ybk=ybk, creg=creg:
                                       e.scalar_tensor_tensor(
                                           SCR2[:, 0:CS],
                                           HS4[:, cc8 * CS:(cc8 + 1) * CS],
                                           DGRP[:, cc:cc + 1],
                                           banks[ybk][:, creg:creg + CS],
                                           OP.mult, OP.add),
                                       waits=w_e2)
                        if yraw_tap_on:
                            yraw_guard = kb.dma("sync",
                                                _ap(yrawsp.tensor,
                                                    cc * 128 * T + c * CS,
                                                    [[T, 128], [1, CS]]),
                                                SCR2[:, 0:CS], s_sp,
                                                waits=[(s_dve, v_gpre)])
                        v_gg = kb.op("gpsimd", lambda e, cc8=cc8:
                                     e.tensor_tensor(GG4[:, cc8 * CS:(cc8 + 1) * CS],
                                                     SCR2[:, 0:CS],
                                                     SILG[:, cc8 * CS:(cc8 + 1) * CS],
                                                     OP.mult),
                                     waits=[(s_dve, v_gpre), (s_act, silg_guard)])
                        pool_gg_last = v_gg
                        gq = cc8 % 2
                        w_sq = [(s_pool, v_gg)]
                        if gsq_guard[gq][0]:
                            w_sq.append(gsq_guard[gq])
                        v_gsq = kb.op("scalar", lambda e, cc8=cc8, gq=gq:
                                      e.activation(GSQ4[gq][:, :],
                                                   GG4[:, cc8 * CS:(cc8 + 1) * CS],
                                                   AF.Square),
                                      waits=w_sq)
                        w_v = [(s_act, v_gsq)]
                        if cc8 == 0:
                            w_v.append((s_dve, max(v_onecol, var_guard, v_gm)))
                        v_var = kb.op("tensor", lambda e, cc8=cc8, gq=gq:
                                      e.matmul(banks[6][0:1, 0:CS], ONECOL[:, :],
                                               GSQ4[gq][:, :],
                                               start=(cc8 == 0), stop=(cc8 == 7)),
                                      waits=w_v, inc=s_pe)
                        gsq_guard[gq] = (s_pe, v_var)
                    ybank_evac[0] = ybank_evac[1] = (s_dve, v_gpre)
                v_rs1 = kb.op("scalar", lambda e: e.activation(
                    RSR[0:1, :], banks[6][0:1, 0:CS], AF.Sqrt, scale=1.0 / GS,
                    bias=CROW[0:1, 0:1]),
                    waits=[(s_pe, v_var), (s_dve, v_crow)])
                var_guard = kb.op("vector", lambda e: e.reciprocal(RSR[0:1, :],
                                                                   RSR[0:1, :]),
                                  waits=[(s_act, v_rs1)])
                hs4_guard = var_guard
                v_rsb = kb.op("tensor", lambda e: e.matmul(
                    banks[7][:, 0:CS], ONESF[0:1, :], RSR[0:1, :],
                    start=True, stop=True),
                    waits=[(s_dve, max(var_guard, rsb_guard))], inc=s_pe)
                v_rsbe = kb.op("vector", lambda e: e.tensor_copy(RSBE[:, :],
                                                                 banks[7][:, 0:CS]),
                               waits=[(s_pe, v_rsb)])
                rsb_guard = v_rsbe
                rsbe_guard = v_rsbe
                yb2 = g % 2
                w_y7 = [(s_dve, v_rsbe)]
                if ygst_guard[yb2][0]:
                    w_y7.append(ygst_guard[yb2])
                for cc8 in range(8):
                    cc = g * 8 + cc8
                    v_yg = kb.op("vector", lambda e, cc=cc, cc8=cc8, yb2=yb2:
                                 e.scalar_tensor_tensor(YGST[yb2][:, cc8 * CS:(cc8 + 1) * CS],
                                                        GG4[:, cc8 * CS:(cc8 + 1) * CS],
                                                        NWGRP[:, cc:cc + 1],
                                                        RSBE[:, :], OP.mult, OP.mult),
                                 waits=(w_y7 if cc8 == 0 else []))
                v_yg_spill = kb.dma("sync",
                                    _ap(ygsp.tensor, g * 8 * 128 * T + c * CS,
                                        [[T, 128], [128 * T, 8], [1, CS]]),
                                    YGST[yb2][:, :], s_ygsp, waits=[(s_dve, v_yg)])
                ygst_guard[yb2] = (s_ygsp, v_yg_spill)
            re_guard = [(s_pe, s_pe.n), (s_dve, s_dve.n), (s_pool, s_pool.n)]

        tp_yraw = dtap("yraw", [NGATE * 128 * T])
        if tp_yraw is not None:
            kb.dma("sync", tp_yraw,
                   _ap(yrawsp.tensor, 0, [[NGATE * 128 * T, 1], [1, NGATE * 128 * T]]),
                   s_sp, waits=[(s_sp, yraw_guard)])
        tp_sl = dtap("sl", [3 * 128 * INTER], BF16)
        if tp_sl is not None:
            kb.dma("sync", tp_sl,
                   _ap(slsp.tensor, 128 * INTER, [[3 * 128 * INTER, 1], [1, 3 * 128 * INTER]]),
                   s_sp, waits=[(s_sp, s_sp.n)])
        tp_yg = dtap("yg", [NGATE * 128 * T], BF16)
        if tp_yg is not None:
            kb.dma("sync", tp_yg,
                   _ap(ygsp.tensor, 0, [[NGATE * 128 * T, 1], [1, NGATE * 128 * T]]),
                   s_sp, waits=[(s_sp, v_yg_spill)])

        scope_p4.close()
        scope_dt.close()
        kb.scope_barrier([s_dve, s_act, s_pe, s_gd, s_sp, s_ld, s_pool])

        scope_p5 = contextlib.ExitStack()
        _p5 = lambda name, p, f, dt: scope_p5.enter_context(
            nc.sbuf_tensor(name, [p, f], dt))
        YG5 = _p5("YG5", 128, NGATE * T, BF16)
        OW5 = [_p5(f"OW5_{i}", 128, NGATE * 128, BF16) for i in range(2)]
        HDF = [_p5(f"HDF{i}", 128, T, BF16) for i in range(2)]
        G2 = _p5("G2", 128, T, F32)
        XRW = [_p5("XRW0", 128, T, BF16)] * 2
        OST = [_p5("OST0", 128, T, F32)] * 2

        v_yg5 = 0
        for gg in range(8):
            v_yg5 = kb.dma("sync", YG5[:, gg * 8 * T:(gg + 1) * 8 * T],
                           _ap(ygsp.tensor, gg * 8 * 128 * T,
                               [[T, 128], [128 * T, 8], [1, T]]), s_yg5)
        ow_free = [0, 0]
        hdf_free = [0, 0]
        g2_guard = 0
        xrw_free = [0, 0]
        ost_free = [0, 0]
        p5bank_free = [0, 0, 0, 0, 0, 0]
        v_out_last = 0
        for dmc in range(32):
            wb = dmc % 2
            v_w = kb.dma("gpsimd", OW5[wb][:, :],
                         _ap(how.tensor, dmc * 128 * NGATE * 128,
                             [[NGATE * 128, 128], [1, NGATE * 128]]), s_w2[wb],
                         waits=[(s_pe, ow_free[wb])])
            v_hdf = kb.dma("sync", HDF[wb][:, :],
                           _ap(hdsp.tensor, dmc * 128 * 128,
                               [[128, 128], [D * 128, 8], [1, 128]]), s_hdsp2[wb],
                           waits=[(s_act, hdf_free[wb])])
            v_xrw = kb.dma("sync", XRW[wb][:, :],
                           _ap(xrawsp.tensor, dmc * 128 * T, [[T, 128], [1, T]]),
                           s_xrw, waits=[(s_dve, xrw_free[wb])])
            pb = (dmc % 3) * 2
            for fc in range(NGATE):
                w = []
                if fc == 0:
                    w = [(s_w2[wb], v_w), (s_yg5, v_yg5),
                         (s_dve, max(p5bank_free[pb], p5bank_free[pb + 1]))]
                for tt in range(2):
                    v_mm = kb.op("tensor",
                                 lambda e, wb=wb, fc=fc, tt=tt, pb=pb:
                                 e.matmul(banks[pb + tt][:, :],
                                          OW5[wb][:, fc * 128:(fc + 1) * 128],
                                          YG5[:, fc * T + tt * 512: fc * T + (tt + 1) * 512],
                                          start=(fc == 0), stop=(fc == NGATE - 1)),
                                 waits=(w if tt == 0 else ()),
                                 inc=(s_pe if fc == NGATE - 1 and tt == 1 else None))
            ow_free[wb] = v_mm
            hdf_free[wb] = kb.op("scalar", lambda e, wb=wb:
                                 e.activation(G2[:, :], HDF[wb][:, :], AF.Silu),
                                 waits=[(s_hdsp2[wb], v_hdf), (s_dve, g2_guard)])
            ob = dmc % 2
            for tt in range(2):
                kb.op("vector", lambda e, tt=tt, pb=pb, ob=ob:
                      e.tensor_tensor(OST[ob][:, tt * 512:(tt + 1) * 512],
                                      banks[pb + tt][:, :], G2[:, tt * 512:(tt + 1) * 512],
                                      OP.mult),
                      waits=([(s_pe, v_mm), (s_act, hdf_free[wb]),
                              (s_out, ost_free[ob])] if tt == 0 else ()))
            p5bank_free[pb] = p5bank_free[pb + 1] = kb.op(
                "vector", lambda e, ob=ob, wb=wb: e.tensor_tensor(
                    OST[ob][:, :], OST[ob][:, :], XRW[wb][:, :], OP.add),
                waits=[(s_xrw, v_xrw)])
            g2_guard = p5bank_free[pb]
            xrw_free[wb] = p5bank_free[pb]
            ost_free[ob] = kb.dma("sync", out_d[dmc * 128:(dmc + 1) * 128, :],
                                  OST[ob][:, :], s_out,
                                  waits=[(s_dve, p5bank_free[pb])])
            v_out_last = ost_free[ob]
        kb.wait("sync", s_out, v_out_last)
        scope_p5.close()

        block = es.enter_context(nc.Block())
        for e in ENGINES:
            fns = kb.ops[e]
            if fns:
                getattr(block, e)(lambda eng, fns=fns: [f(eng) for f in fns])

    return nc, dbg


def _pack_inputs(inputs):
    f32 = np.float32
    ipw = np.ascontiguousarray(inputs["L0_attn_in_proj_w"], dtype=f32)
    ffnw = np.ascontiguousarray(inputs["L0_ffn_W"], dtype=f32)
    outw = np.ascontiguousarray(inputs["L0_attn_out_w"], dtype=f32)
    shared = {
        "w1": np.ascontiguousarray(
            ipw.reshape(FCH, 128, 32, 128).transpose(0, 3, 2, 1)).reshape(FCH, 128, 32 * 128),
        "hw2": np.ascontiguousarray(
            ffnw.reshape(32, 128, 32, 128).transpose(0, 3, 2, 1)).reshape(32, 128, 32 * 128),
        "how": np.ascontiguousarray(
            outw.T.reshape(NGATE, 128, 32, 128).transpose(2, 1, 0, 3)).reshape(
                32, 128, NGATE * 128),
        "affn": np.ascontiguousarray(inputs["L0_ffn_A"], dtype=f32).reshape(32, 128, RANK),
        "w8": np.ascontiguousarray(ffnw[:NEXP].T).reshape(32, 128, NEXP),
        "bmat": np.ascontiguousarray(inputs["L0_ffn_Bmat"], dtype=f32),
        "limes": np.ascontiguousarray(inputs["L0_ffn_LiMEs"], dtype=f32),
        "bffn": np.ascontiguousarray(
            np.asarray(inputs["L0_ffn_b"], dtype=f32).reshape(32, 128).T),
        "convw": np.ascontiguousarray(
            np.asarray(inputs["L0_attn_conv_w"], dtype=f32).reshape(NXBC, 128, KC)
            .transpose(1, 0, 2)).reshape(128, NXBC * KC),
        "convb": np.ascontiguousarray(
            np.asarray(inputs["L0_attn_conv_b"], dtype=f32).reshape(NXBC, 128).T),
        "dtb": np.ascontiguousarray(
            np.asarray(inputs["L0_attn_dt_bias"], dtype=f32).reshape(128, 1)),
        "alog": np.ascontiguousarray(
            np.asarray(inputs["L0_attn_A_log"], dtype=f32).reshape(128, 1)),
        "dgrp": np.ascontiguousarray(
            np.repeat(np.asarray(inputs["L0_attn_D"], dtype=f32).reshape(64, 2).T, 64, axis=0)),
        "nwgrp": np.ascontiguousarray(
            np.asarray(inputs["L0_attn_norm_w"], dtype=f32).reshape(NGATE, 128).T),
        "idn": np.eye(128, dtype=f32),
    }
    j = np.arange(128)[:, None]
    i = np.arange(CS)[None, :]
    m0 = np.where(i >= j, 0.0, -1e6).astype(f32)
    m1 = np.where(i >= j + 128, 0.0, -1e6).astype(f32)
    shared["maskp"] = np.ascontiguousarray(np.concatenate([m0, m1], axis=1))

    x_full = np.asarray(inputs["L0_attn_input_states"], dtype=f32)
    maps = []
    for core in range(8):
        b, h = core // 2, core % 2
        m = dict(shared)
        m["x"] = np.ascontiguousarray(x_full[b, h * T:(h + 1) * T, :])
        m["ish0"] = np.full((128, 1), 1.0 - h, dtype=f32)
        m["ish1"] = np.full((128, 1), float(h), dtype=f32)
        maps.append(m)
    return maps


_NC_CACHE = {}


def _get_nc(debug=()):
    key = tuple(sorted(debug))
    if key not in _NC_CACHE:
        _NC_CACHE[key] = build_program(debug)
    return _NC_CACHE[key]


def kernel(**inputs):
    nc, _ = _get_nc()
    maps = _pack_inputs(inputs)
    res = run_bass_kernel_spmd(nc, maps, core_ids=list(range(8)))
    out = np.empty((B_, SEQ, D), dtype=np.float32)
    for core in range(8):
        b, h = core // 2, core % 2
        out[b, h * T:(h + 1) * T, :] = np.asarray(res.results[core]["out"]).T
    return out


# version 12 (final)
# speedup vs baseline: 1.0601x; 1.0178x over previous
import contextlib
import numpy as np
import ml_dtypes
import concourse.bass as bass
import concourse.mybir as mybir
from concourse.bass_utils import run_bass_kernel_spmd

F32 = mybir.dt.float32
F32R = mybir.dt.float32r
BF16 = mybir.dt.bfloat16
AF = mybir.ActivationFunctionType
OP = mybir.AluOpType
AP = bass.AP
AX = mybir.AxisListType.X

B_, SEQ, D = 4, 2048, 4096
T = 1024
H, PD, NS, G, KC, CS = 128, 64, 128, 8, 4, 256
NCHUNK = T // CS
INTER = H * PD
CONVC = INTER + 2 * G * NS
PROJ = INTER + CONVC + H
FCH = PROJ // 128
NXBC, NGATE = 80, 64
RANK, NEXP = 32, 8
GS = INTER // G
ENGINES = ("tensor", "vector", "scalar", "gpsimd", "sync")


class Sem:
    def __init__(self, handle):
        self.h = handle
        self.n = 0

    def inc(self, amt=1):
        self.n += amt
        return self.n

    @property
    def now(self):
        return self.n


class KB:

    def __init__(self):
        self.nc = bass.Bass("TRN2", target_bir_lowering=False)
        self.ops = {e: [] for e in ENGINES}
        self.dr = {}
        self.chain = {}
        self.guard = None
        self.guard_pending = set()

    def set_chain_sems(self, s_dve, s_act, s_pool):
        self.chain = {"vector": s_dve, "scalar": s_act, "gpsimd": s_pool}

    def scope_barrier(self, sems):
        cur = [(s, s.n) for s in sems if s.n > 0]
        if self.guard:
            old = dict((id(s), (s, v)) for s, v in self.guard)
            for s, v in cur:
                old[id(s)] = (s, v)
            cur = list(old.values())
        self.guard = cur
        self.guard_pending = set(ENGINES)

    def op(self, eng, fn, waits=(), inc=None, inc_amt=1, chain=True):
        allw = [(s, v) for (s, v) in waits if v > 0]
        if self.guard and eng in self.guard_pending:
            self.guard_pending.discard(eng)
            allw = list(self.guard) + allw
        ret = None
        cs = self.chain.get(eng) if chain else None
        if cs is not None and inc is None:
            if cs.n > 0:
                allw.append((cs, cs.n))
            ret = cs.inc(1)
            inc_h, inc_amt = cs.h, 1
        elif inc is not None:
            if cs is not None and cs.n > 0:
                allw.append((cs, cs.n))
            inc.inc(inc_amt)
            inc_h = inc.h
            ret = inc.now
        else:
            inc_h = None
        allw = tuple((s.h, v) for (s, v) in allw)

        def run(e, fn=fn, waits=allw, inc_h=inc_h, inc_amt=inc_amt):
            for hh, v in waits:
                e.wait_ge(hh, v)
            r = fn(e)
            if inc_h is not None:
                r.then_inc(inc_h, inc_amt)
        self.ops[eng].append(run)
        return ret

    def wait(self, eng, sem, val):
        if val > 0:
            self.ops[eng].append(lambda e, hh=sem.h, v=val: e.wait_ge(hh, v))

    def dma(self, eng, out_ap, in_ap, sem, waits=(), **kw):
        return self.op(eng, lambda e, o=out_ap, i=in_ap, kw=kw:
                       e.dma_start(out=o, in_=i, **kw),
                       waits=waits, inc=sem, inc_amt=16, chain=False)

    def din(self, name, shape, dt=F32):
        ap = self.nc.dram_tensor(name, shape, dt, kind="ExternalInput").ap()
        self.dr[name] = ap
        return ap

    def dout(self, name, shape, dt=F32):
        ap = self.nc.dram_tensor(name, shape, dt, kind="ExternalOutput").ap()
        self.dr[name] = ap
        return ap

    def dint(self, name, shape, dt):
        ap = self.nc.dram_tensor(name, shape, dt).ap()
        self.dr[name] = ap
        return ap


def _ap(handle, offset, dims):
    return AP(handle, offset, dims)


def build_program(debug=()):
    kb = KB()
    nc = kb.nc
    debug = set(debug)
    dbg = {}

    x_in = kb.din("x", [T, D])
    w1 = kb.din("w1", [FCH, 128, 32 * 128])
    hw2 = kb.din("hw2", [32, 128, 32 * 128])
    how = kb.din("how", [32, 128, NGATE * 128])
    affn = kb.din("affn", [32, 128, RANK])
    w8 = kb.din("w8", [32, 128, NEXP])
    bmat = kb.din("bmat", [RANK, D])
    limes = kb.din("limes", [NEXP, RANK])
    bffn = kb.din("bffn", [128, 32])
    convw = kb.din("convw", [128, NXBC * KC])
    convb = kb.din("convb", [128, NXBC])
    dtbias = kb.din("dtb", [128, 1])
    alog = kb.din("alog", [128, 1])
    dgrp = kb.din("dgrp", [128, NGATE])
    nwgrp = kb.din("nwgrp", [128, NGATE])
    idn_d = kb.din("idn", [128, 128])
    maskp_d = kb.din("maskp", [128, 2 * CS])
    ish0 = kb.din("ish0", [128, 1])
    ish1 = kb.din("ish1", [128, 1])

    out_d = kb.dout("out", [D, T], F32)

    def dtap(name, shape, dt=F32):
        if name in debug:
            dbg[name] = kb.dout("dbg_" + name, shape, dt)
            return dbg[name]
        return None

    xrawsp = kb.dint("xrawsp", [D, T], BF16)
    xbcsp = kb.dint("xbcsp", [NXBC, 128, T], BF16)
    gatesp = kb.dint("gatesp", [NGATE, 128, T], BF16)
    xtoksp = kb.dint("xtoksp", [NCHUNK, 2, 128, INTER], BF16)
    hdsp = kb.dint("hdsp", [8, D, 128], BF16)
    yrawsp = kb.dint("yrawsp", [NGATE, 128, T], F32)
    ygsp = kb.dint("ygsp", [NGATE, 128, T], BF16)
    cc1_in = kb.dint("cc1_in", [1, 16], F32)
    cc1_out = kb.dint("cc1_out", [1, 16], F32)
    cc2a_in = kb.dint("cc2a_in", [128, NXBC * 3], F32)
    cc2a_out = kb.dint("cc2a_out", [128, NXBC * 3], F32)
    cc2b_in = kb.dint("cc2b_in", [128, INTER], F32)
    cc2b_out = kb.dint("cc2b_out", [128, INTER], F32)

    es = contextlib.ExitStack()

    def sb(name, pdim, fdim, dt):
        return es.enter_context(nc.sbuf_tensor(name, [pdim, fdim], dt))

    def sem(name):
        return Sem(es.enter_context(nc.semaphore(name)))

    with es:
        banks = [es.enter_context(nc.psum_tensor(f"bank{i}", [128, 512], F32))
                 for i in range(8)]
        banksb = [b.ap().bitcast(BF16) for b in banks]

        s_ld = sem("s_ld")
        s_gd = sem("s_gd")
        s_pe = sem("s_pe")
        s_dve = sem("s_dve")
        s_act = sem("s_act")
        s_pool = sem("s_pool")
        s_sp = sem("s_sp")
        s_cc = sem("s_cc")
        s_cc2a = sem("s_cc2a")
        s_cc2b = sem("s_cc2b")
        s_xt2 = [sem("s_xt0"), sem("s_xt1")]
        s_xrs2 = [sem("s_xrs0"), sem("s_xrs1")]
        s_w2 = [sem("s_w0"), sem("s_w1")]
        s_hdsp2 = [sem("s_hdsp0"), sem("s_hdsp1")]
        s_spst2 = [sem("s_spst0"), sem("s_spst1")]
        s_cdf = sem("s_cdf")
        s_hsb2 = [sem("s_hsb0"), sem("s_hsb1")]
        s_bfm = sem("s_bfm")
        s_xtsp = sem("s_xtsp")
        s_slsp = sem("s_slsp")
        s_x42 = [sem("s_x40"), sem("s_x41")]
        s_bc = sem("s_bc")
        s_slbr = sem("s_slbr")
        s_hs4r = sem("s_hs4r")
        s_gt4r = sem("s_gt4r")
        s_fcb2 = [sem("s_fcb0"), sem("s_fcb1")]
        s_ygsp = sem("s_ygsp")
        s_yg5 = sem("s_yg5")
        s_xrw = sem("s_xrw")
        s_out = sem("s_out")
        kb.set_chain_sems(s_dve, s_act, s_pool)
        ALLS = [s_ld, s_gd, s_pe, s_dve, s_act, s_pool, s_sp,
                s_xt2[0], s_xt2[1], s_xrs2[0], s_xrs2[1], s_w2[0], s_w2[1],
                s_hdsp2[0], s_hdsp2[1], s_spst2[0], s_spst2[1], s_cdf,
                s_hsb2[0], s_hsb2[1], s_bfm, s_xtsp, s_slsp, s_x42[0], s_x42[1],
                s_bc, s_slbr, s_hs4r, s_gt4r, s_fcb2[0], s_fcb2[1], s_ygsp]

        IDN = sb("IDN", 128, 128, F32)
        IDNB = sb("IDNB", 128, 128, BF16)
        MASKP = sb("MASKP", 128, 2 * CS, BF16)
        CONVW = sb("CONVW", 128, NXBC * KC, F32)
        CONVB = sb("CONVB", 128, NXBC, F32)
        DTB = sb("DTB", 128, 1, F32)
        ANEG = sb("ANEG", 128, 1, F32)
        DGRP = sb("DGRP", 128, NGATE, F32)
        NWGRP = sb("NWGRP", 128, NGATE, F32)
        BFFN = sb("BFFN", 128, 32, F32)
        ONESB = sb("ONESB", 4, 128, BF16)
        ONESF = sb("ONESF", 1, 128, F32)
        ISH0 = sb("ISH0", 128, 1, F32)
        ISH1 = sb("ISH1", 128, 1, F32)
        BMATB = sb("BMATB", RANK, D, BF16)
        LIMESB = sb("LIMESB", NEXP, RANK, BF16)
        STT = sb("STT", 128, 8, F32)
        SCR2 = sb("SCR2", 128, T, F32)
        ACC = sb("ACC", 128, T, F32)
        ONES128 = sb("ONES128", 128, CS, F32)
        CR3 = sb("CR3", 128, NXBC * 3, F32)
        HALOS = sb("HALOS", 128, NXBC * 3, F32)
        XBCFIX = sb("XBCFIX", 128, NXBC * 3, BF16)
        ZPMT = sb("ZPMT", RANK, T, BF16)
        scope_dt = contextlib.ExitStack()
        _pdt = lambda name, p, f, dt: scope_dt.enter_context(
            nc.sbuf_tensor(name, [p, f], dt))
        DT = _pdt("DT", 128, T, F32)
        CUM = _pdt("CUM", 128, T, F32)
        LCT = _pdt("LCT", 128, 8 * 128, F32)
        WDTT = _pdt("WDTT", 128, 8 * 128, BF16)
        Q1 = _pdt("Q1", 128, T, BF16)
        Q2 = _pdt("Q2", 128, T, BF16)
        R2 = _pdt("R2", 128, T, BF16)
        ECUM = _pdt("ECUM", 128, T, BF16)
        PBF = _pdt("PBF", 1, 4 * 128, F32)

        kb.dma("sync", IDN[:, :], idn_d, s_ld)
        kb.dma("sync", CONVW[:, :], convw, s_ld)
        kb.dma("sync", CONVB[:, :], convb, s_ld)
        kb.dma("sync", DTB[:, :], dtbias, s_ld)
        kb.dma("sync", DGRP[:, :], dgrp, s_ld)
        kb.dma("sync", NWGRP[:, :], nwgrp, s_ld)
        kb.dma("sync", BFFN[:, :], bffn, s_ld)
        kb.dma("sync", ISH0[:, :], ish0, s_ld)
        kb.dma("sync", ISH1[:, :], ish1, s_ld)
        kb.dma("gpsimd", MASKP[:, :], maskp_d, s_gd)
        v_alog = kb.dma("sync", ANEG[:, :], alog, s_ld)
        kb.dma("gpsimd", BMATB[:, :], bmat, s_gd)
        kb.dma("gpsimd", LIMESB[:, :], limes, s_gd)
        v_idnb = kb.dma("gpsimd", IDNB[:, :], idn_d, s_gd)
        v_consts = s_ld.now

        kb.op("vector", lambda e: e.memset(ONESB[:, :], 1.0))
        v_ones = kb.op("vector", lambda e: e.memset(ONESF[:, :], 1.0), inc=s_dve)
        kb.op("scalar", lambda e: e.activation(ANEG[:, :], ANEG[:, :], AF.Exp),
              waits=[(s_ld, v_consts)])
        v_aneg = kb.op("scalar", lambda e: e.mul(ANEG[:, :], ANEG[:, :], -1.0),
                       inc=s_act)

        scope_xn = contextlib.ExitStack()
        XN = scope_xn.enter_context(nc.sbuf_tensor("XN", [128, 32 * T], BF16))
        scope_p0 = contextlib.ExitStack()
        XT = [scope_p0.enter_context(nc.sbuf_tensor(f"XT{i}", [128, D], F32))
              for i in range(2)]
        XTN = [scope_p0.enter_context(nc.sbuf_tensor(f"XTN{i}", [128, D], BF16))
               for i in range(2)]
        SQS = scope_p0.enter_context(nc.sbuf_tensor("SQS", [128, D], F32))

        XRB = [scope_p0.enter_context(nc.sbuf_tensor(f"XRB{i}", [128, 512], BF16))
               for i in range(2)]
        xr_bank_free = [0, 0]
        xt_free_v = [0, 0]
        xt_free_a = [0, 0]
        xt_free_pe = [0, 0]
        xtn_free = [0, 0]
        sqs_free = 0
        bank_free = {6: 0, 7: 0}
        v_xn = 0

        for i in range(8):
            b = i % 2
            v_x = kb.dma("sync", XT[b][:, :], x_in[i * 128:(i + 1) * 128, :], s_xt2[b],
                         waits=[(s_dve, xt_free_v[b]), (s_act, xt_free_a[b]),
                                (s_pe, xt_free_pe[b])])
            v_ms = kb.op("vector", lambda e, b=b: e.reduce_sum(STT[:, 0:1], XT[b][:, :], AX),
                         waits=[(s_xt2[b], v_x)], inc=s_dve)
            v_sq = kb.op("scalar", lambda e, b=b: e.activation(SQS[:, :], XT[b][:, :],
                                                               AF.Square),
                         waits=[(s_xt2[b], v_x), (s_dve, sqs_free)], inc=s_act)
            xt_free_a[b] = v_sq
            v_ss = kb.op("vector", lambda e: e.reduce_sum(STT[:, 1:2], SQS[:, :], AX),
                         waits=[(s_act, v_sq)], inc=s_dve)
            sqs_free = v_ss
            kb.op("vector", lambda e: e.tensor_scalar_mul(STT[:, 2:3], STT[:, 0:1], 1.0 / D))
            kb.op("vector", lambda e: e.tensor_mul(STT[:, 3:4], STT[:, 2:3], STT[:, 2:3]))
            kb.op("vector", lambda e: e.tensor_scalar(STT[:, 4:5], STT[:, 1:2], 1.0 / D,
                                                      1e-5, OP.mult, OP.add))
            v_var = kb.op("vector", lambda e: e.tensor_sub(STT[:, 4:5], STT[:, 4:5],
                                                           STT[:, 3:4]), inc=s_dve)
            v_sd = kb.op("scalar", lambda e: e.activation(STT[:, 5:6], STT[:, 4:5], AF.Sqrt),
                         waits=[(s_dve, v_var)], inc=s_act)
            kb.wait("vector", s_act, v_sd)
            kb.op("vector", lambda e: e.reciprocal(STT[:, 6:7], STT[:, 5:6]))
            v_nrm = kb.op("vector",
                          lambda e, b=b: e.tensor_scalar(XTN[b][:, :], XT[b][:, :],
                                                         STT[:, 2:3], STT[:, 6:7],
                                                         OP.subtract, OP.mult),
                          waits=[(s_pe, xtn_free[b])], inc=s_dve)
            xt_free_v[b] = v_nrm
            for gg in range(4):
                pb = 6 + (gg % 2)
                bk = banksb[pb]
                for k in range(8):
                    dc = gg * 8 + k
                    w = []
                    if k == 0:
                        w = [(s_dve, v_nrm), (s_gd, v_idnb), (s_dve, bank_free[pb])]
                    v_tp = kb.op("tensor",
                                 lambda e, bk=bk, k=k, dc=dc, b=b, st=(k == 0), sp=(k == 7):
                                 e.matmul(bk[:, k * 128:(k + 1) * 128],
                                          XTN[b][:, dc * 128:(dc + 1) * 128],
                                          IDNB[:, :], is_transpose=True,
                                          start=st, stop=sp),
                                 waits=w, inc=(s_pe if k == 7 else None))
                if gg == 3:
                    xtn_free[b] = v_tp
                xn_out = _ap(XN, gg * 8 * T + i * 128, [[32 * T, 128], [T, 8], [1, 128]])
                bank_free[pb] = kb.op("vector",
                                      lambda e, bk=bk, xo=xn_out:
                                      e.tensor_copy(xo, bk[:, :]),
                                      waits=[(s_pe, v_tp)], inc=s_dve)
                v_xn = bank_free[pb]
            for gg in range(8):
                pb4 = 4 + (gg % 2)
                for k in range(4):
                    dc = gg * 4 + k
                    w = [(s_xt2[b], v_x)]
                    if k == 0:
                        w.append((s_xrs2[pb4 - 4], xr_bank_free[pb4 - 4]))
                    v_tp4 = kb.op("tensor",
                                  lambda e, pb4=pb4, k=k, dc=dc, b=b,
                                  st=(k == 0), sp=(k == 3):
                                  e.matmul(banks[pb4][:, k * 128:(k + 1) * 128],
                                           XT[b][:, dc * 128:(dc + 1) * 128],
                                           IDN[:, :], is_transpose=True,
                                           start=st, stop=sp),
                                  waits=w, inc=(s_pe if k == 3 else None))
                xrb = gg % 2
                v_xre = kb.op("vector",
                              lambda e, pb4=pb4, xrb=xrb:
                              e.tensor_copy(XRB[xrb][:, :], banks[pb4][:, 0:512]),
                              waits=[(s_pe, v_tp4)])
                xr_bank_free[pb4 - 4] = kb.dma(
                    "sync",
                    _ap(xrawsp.tensor, gg * 4 * 128 * T + i * 128,
                        [[T, 128], [128 * T, 4], [1, 128]]),
                    XRB[xrb][:, :], s_xrs2[xrb], waits=[(s_dve, v_xre)])
            xt_free_pe[b] = v_tp4

        p0_dve_end = s_dve.now
        tp_xn = dtap("xn", [128, 32 * T], BF16)
        if tp_xn is not None:
            kb.dma("sync", tp_xn, _ap(XN, 0, [[32 * T, 128], [1, 32 * T]]), s_sp,
                   waits=[(s_dve, v_xn)])
        scope_p0.close()
        kb.scope_barrier([s_dve, s_act, s_pe, s_ld])

        scope_p1 = contextlib.ExitStack()
        _p1 = lambda name, p, f, dt: scope_p1.enter_context(
            nc.sbuf_tensor(name, [p, f], dt))
        AZ = _p1("AZ", 128, 32 * RANK, BF16)
        W8B = _p1("W8B", 128, 32 * NEXP, BF16)
        ZT = _p1("ZT", RANK, T, F32)
        ZTB = _p1("ZTB", RANK, T, BF16)
        H8T = _p1("H8T", NEXP, T, F32)
        DST = _p1("DST", NEXP, T // 2, F32)
        SCL = _p1("SCL", 1, 16, F32)
        GSCL = _p1("GSCL", 1, 16, F32)

        v_az = kb.dma("gpsimd", AZ[:, :], _ap(affn.tensor, 0, [[RANK, 128], [128 * RANK, 32], [1, RANK]]),
                      s_gd)
        v_w8 = kb.dma("gpsimd", W8B[:, :], _ap(w8.tensor, 0, [[NEXP, 128], [128 * NEXP, 32], [1, NEXP]]),
                      s_gd)

        for tt in range(2):
            for dc in range(32):
                kb.op("tensor",
                      lambda e, tt=tt, dc=dc:
                      e.matmul(banks[tt][0:RANK, :],
                               AZ[:, dc * RANK:(dc + 1) * RANK],
                               XN[:, dc * T + tt * 512: dc * T + (tt + 1) * 512],
                               start=(dc == 0), stop=(dc == 31)),
                      waits=([(s_gd, v_w8), (s_dve, v_xn)] if dc == 0 else ()),
                      inc=(s_pe if dc == 31 else None))
            v_z = s_pe.now
            kb.op("vector", lambda e, tt=tt: e.tensor_copy(ZT[:, tt * 512:(tt + 1) * 512],
                                                           banks[tt][0:RANK, :]),
                  waits=[(s_pe, v_z)])
            v_ztb = kb.op("vector", lambda e, tt=tt: e.tensor_copy(ZTB[:, tt * 512:(tt + 1) * 512],
                                                                   banks[tt][0:RANK, :]),
                          inc=s_dve)
        v_zt = v_ztb

        for tt in range(2):
            for dc in range(32):
                kb.op("tensor",
                      lambda e, tt=tt, dc=dc:
                      e.matmul(banks[2 + tt][0:NEXP, :],
                               W8B[:, dc * NEXP:(dc + 1) * NEXP],
                               XN[:, dc * T + tt * 512: dc * T + (tt + 1) * 512],
                               start=(dc == 0), stop=(dc == 31)),
                      waits=([(s_gd, v_w8)] if dc == 0 else ()),
                      inc=(s_pe if dc == 31 else None))
            v_h8 = s_pe.now
            kb.op("vector",
                  lambda e, tt=tt: e.tensor_scalar(H8T[:, tt * 512:(tt + 1) * 512],
                                                   banks[2 + tt][0:NEXP, :],
                                                   BFFN[0:NEXP, 0:1], None, OP.add),
                  waits=[(s_pe, v_h8), (s_ld, v_consts)], inc=s_dve)
        v_h8t = s_dve.now

        LIMT = _p1("LIMT", RANK, NEXP, BF16)
        kb.op("tensor", lambda e: e.matmul(banksb[4][0:RANK, 0:NEXP], LIMESB[:, :],
                                           IDNB[0:NEXP, 0:NEXP], is_transpose=True,
                                           start=True, stop=True),
              waits=[(s_gd, v_idnb), (s_dve, p0_dve_end)], inc=s_pe)
        v_limt_pe = s_pe.now
        v_limt = kb.op("vector", lambda e: e.tensor_copy(LIMT[:, :], banksb[4][0:RANK, 0:NEXP]),
                       waits=[(s_pe, v_limt_pe)], inc=s_dve)
        zanch = _ap(ZTB, 1, [[T, RANK], [2, T // 2]])
        kb.op("tensor", lambda e: e.matmul(banks[5][0:NEXP, :], LIMT[:, :], zanch,
                                           start=True, stop=True),
              waits=[(s_dve, max(v_limt, v_zt))], inc=s_pe)
        v_ds_pe = s_pe.now
        v_ds = kb.op("vector", lambda e: e.tensor_copy(DST[:, :], banks[5][0:NEXP, :]),
                     waits=[(s_pe, v_ds_pe)], inc=s_dve)

        h8anch = _ap(H8T, 1, [[T, NEXP], [2, T // 2]])
        kb.wait("vector", s_dve, v_h8t)
        kb.op("vector", lambda e: e.reduce_max(STT[0:NEXP, 0:1], h8anch, AX,
                                               apply_absolute_value=True))
        v_lm = kb.op("vector", lambda e: e.reduce_max(STT[0:NEXP, 1:2], DST[:, :], AX,
                                                      apply_absolute_value=True),
                     inc=s_dve)
        SCRO = _p1("SCRO", 1, 16, F32)
        v_t = kb.dma("gpsimd", SCRO[0:1, :],
                     _ap(STT, 0, [[8, 8], [1, 2]]), s_gd,
                     waits=[(s_dve, v_lm)])
        kb.wait("vector", s_gd, v_t)
        kb.op("vector", lambda e: e.memset(SCL[:, :], 0.0))
        kb.op("vector", lambda e: e.reduce_max(SCL[0:1, 0:1],
                                               _ap(SCRO, 0, [[16, 1], [2, 8]]), AX))
        v_scl = kb.op("vector", lambda e: e.reduce_max(SCL[0:1, 1:2],
                                                       _ap(SCRO, 1, [[16, 1], [2, 8]]), AX),
                      inc=s_dve)
        v_c1w = kb.dma("gpsimd", cc1_in, SCL[0:1, :], s_gd, waits=[(s_dve, v_scl)])
        kb.op("gpsimd", lambda e: e.collective_compute(
            "AllReduce", OP.max, replica_groups=[list(range(8))],
            ins=[cc1_in.opt()], outs=[cc1_out.opt()]),
            waits=[(s_gd, v_c1w)], inc=s_cc)
        v_c1 = s_cc.now
        kb.wait("gpsimd", s_cc, v_c1)
        v_gscl = kb.dma("gpsimd", GSCL[0:1, :], cc1_out, s_gd)

        tp_sc = dtap("scales", [1, 16])
        if tp_sc is not None:
            kb.dma("sync", tp_sc, GSCL[0:1, :], s_sp, waits=[(s_gd, v_gscl)])

        kb.wait("vector", s_gd, v_gscl)
        kb.op("vector", lambda e: e.tensor_scalar_max(GSCL[0:1, 0:2], GSCL[0:1, 0:2], 1e-6))
        kb.op("vector", lambda e: e.reciprocal(GSCL[0:1, 2:4], GSCL[0:1, 0:2]))
        v_rr = kb.op("vector", lambda e: e.tensor_scalar_mul(GSCL[0:1, 2:4], GSCL[0:1, 2:4],
                                                             0.5), inc=s_dve)
        kb.op("tensor", lambda e: e.matmul(banks[0][0:NEXP, 0:2],
                                           ONESF[0:1, 0:NEXP], GSCL[0:1, 2:4],
                                           start=True, stop=True),
              waits=[(s_dve, max(v_rr, v_ones))], inc=s_pe)
        v_rb_pe = s_pe.now
        RSC = _p1("RSC", NEXP, 2, F32)
        v_rsc = kb.op("vector", lambda e: e.tensor_copy(RSC[:, :], banks[0][0:NEXP, 0:2]),
                      waits=[(s_pe, v_rb_pe)], inc=s_dve)

        LGT = _p1("LGT", NEXP, T // 2, F32)
        kb.op("vector", lambda e: e.tensor_scalar_mul(LGT[:, :], h8anch, RSC[:, 0:1]))
        v_lg = kb.op("vector", lambda e: e.scalar_tensor_tensor(LGT[:, :], DST[:, :],
                                                                RSC[:, 1:2], LGT[:, :],
                                                                OP.mult, OP.add),
                     inc=s_dve)
        for a in range(4):
            kb.op("tensor", lambda e, a=a: e.matmul(banks[1][:, a * 8:(a + 1) * 8],
                                                    LGT[:, a * 128:(a + 1) * 128],
                                                    IDN[0:NEXP, 0:NEXP], is_transpose=True,
                                                    start=(a == 0), stop=(a == 3)),
                  waits=([(s_dve, v_lg), (s_ld, v_consts)] if a == 0 else ()),
                  inc=(s_pe if a == 3 else None))
        v_lgt = s_pe.now

        PRB = _p1("PRB", 128, 4 * NEXP, F32)
        WTK = _p1("WTK", 128, 4 * NEXP, F32)
        GST = _p1("GST", 128, 12, F32)
        kb.wait("vector", s_pe, v_lgt)
        kb.wait("scalar", s_pe, v_lgt)
        for a in range(4):
            pa = banks[1][:, a * 8:(a + 1) * 8]
            prb = PRB[:, a * 8:(a + 1) * 8]
            wtk = WTK[:, a * 8:(a + 1) * 8]
            v_mx = kb.op("vector", lambda e, pa=pa: e.reduce_max(GST[:, 0:1], pa, AX,
                                                                 negate=True), inc=s_dve)
            v_ex = kb.op("scalar", lambda e, pa=pa, prb=prb:
                         e.activation(prb, pa, AF.Exp, bias=GST[:, 0:1],
                                      accum_out=GST[:, 1:2]),
                         waits=[(s_dve, v_mx)], inc=s_act)
            kb.wait("vector", s_act, v_ex)
            kb.op("vector", lambda e: e.reciprocal(GST[:, 2:3], GST[:, 1:2]))
            kb.op("vector", lambda e, prb=prb: e.tensor_scalar_mul(prb, prb, GST[:, 2:3]))
            kb.op("vector", lambda e, prb=prb: e.reduce_max(GST[:, 3:4], prb, AX))
            kb.op("vector", lambda e, prb=prb: e.tensor_scalar(GST[:, 4:12][:, 0:8], prb,
                                                               GST[:, 3:4], None, OP.is_ge))
            kb.op("vector", lambda e: e.tensor_scalar(SCR2[:, 0:8], GST[:, 4:12], 1.0, 0.0,
                                                      OP.mult, OP.add,
                                                      accum_out=GST[:, 2:3]))
            kb.op("vector", lambda e, prb=prb:
                  e.scalar_tensor_tensor(SCR2[:, 8:16], GST[:, 4:12], -1e30, prb,
                                         OP.mult, OP.add))
            kb.op("vector", lambda e: e.reduce_max(GST[:, 5:6], SCR2[:, 8:16], AX))
            kb.op("vector", lambda e: e.tensor_scalar(GST[:, 6:7], GST[:, 2:3], 1.5, None,
                                                      OP.is_ge))
            kb.op("vector", lambda e: e.tensor_sub(GST[:, 7:8], GST[:, 3:4], GST[:, 5:6]))
            kb.op("vector", lambda e: e.tensor_mul(GST[:, 7:8], GST[:, 7:8], GST[:, 6:7]))
            kb.op("vector", lambda e: e.tensor_add(GST[:, 8:9], GST[:, 5:6], GST[:, 7:8]))
            v_thr = kb.op("vector", lambda e: e.tensor_scalar_mul(GST[:, 9:10], GST[:, 8:9],
                                                                  -2.0), inc=s_dve)
            v_msk = kb.op("scalar", lambda e, prb=prb:
                          e.activation(SCR2[:, 16:24], prb, AF.Sigmoid, scale=2.0,
                                       bias=GST[:, 9:10]),
                          waits=[(s_dve, v_thr)], inc=s_act)
            kb.wait("vector", s_act, v_msk)
            kb.op("vector", lambda e, prb=prb, wtk=wtk: e.tensor_mul(wtk, prb, SCR2[:, 16:24]))
            kb.op("vector", lambda e, wtk=wtk: e.tensor_scalar(SCR2[:, 24:32], wtk, 1.0, 0.0,
                                                               OP.mult, OP.add,
                                                               accum_out=GST[:, 10:11]))
            kb.op("vector", lambda e: e.tensor_scalar_add(GST[:, 10:11], GST[:, 10:11], 1e-9))
            kb.op("vector", lambda e: e.reciprocal(GST[:, 11:12], GST[:, 10:11]))
            v_wtk = kb.op("vector", lambda e, wtk=wtk:
                          e.tensor_scalar_mul(wtk, wtk, GST[:, 11:12]), inc=s_dve)

        tp_w = dtap("weights", [128, 32])
        if tp_w is not None:
            kb.dma("sync", tp_w, WTK[:, :], s_sp, waits=[(s_dve, v_wtk)])

        WTE = _p1("WTE", NEXP, T // 2, BF16)
        WFULL = _p1("WFULL", NEXP, T, BF16)
        for a in range(4):
            kb.op("tensor", lambda e, a=a: e.matmul(banks[2][0:NEXP, a * 128:(a + 1) * 128],
                                                    WTK[:, a * 8:(a + 1) * 8],
                                                    IDN[:, :], is_transpose=True,
                                                    start=(a == 0), stop=(a == 3)),
                  waits=([(s_dve, v_wtk)] if a == 0 else ()),
                  inc=(s_pe if a == 3 else None))
        v_wte_pe = s_pe.now
        v_wte = kb.op("vector", lambda e: e.tensor_copy(WTE[:, :], banks[2][0:NEXP, 0:512]),
                      waits=[(s_pe, v_wte_pe)], inc=s_dve)
        v_wf = kb.op("vector", lambda e: e.tensor_copy(
            WFULL[:, :], _ap(WTE, 0, [[T // 2, NEXP], [1, T // 2], [0, 2]])),
            inc=s_dve)

        for tt in range(2):
            pmb = banks[3 + tt]
            kb.op("tensor", lambda e, tt=tt, pmb=pmb: e.matmul(pmb[0:RANK, :], LIMESB[:, :],
                                                               WFULL[:, tt * 512:(tt + 1) * 512],
                                                               start=True, stop=True),
                  waits=([(s_dve, v_wf)] if tt == 0 else ()), inc=s_pe)
            v_pm = s_pe.now
            kb.op("vector", lambda e, tt=tt, pmb=pmb:
                  e.scalar_tensor_tensor(ZPMT[:, tt * 512:(tt + 1) * 512],
                                         pmb[0:RANK, :], 0.5,
                                         ZT[:, tt * 512:(tt + 1) * 512],
                                         OP.mult, OP.mult),
                  waits=[(s_pe, v_pm)], inc=s_dve)
        v_zpm = s_dve.now

        tp_z = dtap("zT", [RANK, T])
        if tp_z is not None:
            kb.dma("sync", tp_z, ZT[:, :], s_sp, waits=[(s_dve, v_zpm)])
        tp_zpm = dtap("zpmT", [RANK, T], BF16)
        if tp_zpm is not None:
            kb.dma("sync", tp_zpm, ZPMT[:, :], s_sp, waits=[(s_dve, v_zpm)])
        tp_h8 = dtap("H8T", [NEXP, T])
        if tp_h8 is not None:
            kb.dma("sync", tp_h8, H8T[:, :], s_sp, waits=[(s_dve, v_h8t)])

        scope_p1.close()
        kb.scope_barrier([s_dve, s_act, s_pe, s_gd, s_ld])

        scope_p2 = contextlib.ExitStack()
        W2T = [scope_p2.enter_context(nc.sbuf_tensor(f"W2T{i}", [128, 32 * 128], BF16))
               for i in range(2)]
        HDS = [scope_p2.enter_context(nc.sbuf_tensor(f"HDS{i}", [128, T], BF16))
               for i in range(2)]
        w2_free = [0, 0]
        hds_free = [0, 0]
        p2bank_free = [0, 0, 0, 0]
        v_hd_spills = 0
        for dmc in range(32):
            wb = dmc % 2
            v_w = kb.dma("gpsimd", W2T[wb][:, :],
                         _ap(hw2.tensor, dmc * 128 * 4096, [[4096, 128], [1, 4096]]),
                         s_w2[wb], waits=[(s_pe, w2_free[wb])])
            pb = (dmc % 2) * 2
            for dc in range(32):
                w = []
                if dc == 0:
                    w = [(s_w2[wb], v_w), (s_dve, max(p2bank_free[pb], p2bank_free[pb + 1],
                                                      v_xn))]
                for tt in range(2):
                    kb.op("tensor",
                          lambda e, wb=wb, dc=dc, tt=tt, pb=pb:
                          e.matmul(banks[pb + tt][:, :],
                                   W2T[wb][:, dc * 128:(dc + 1) * 128],
                                   XN[:, dc * T + tt * 512: dc * T + (tt + 1) * 512],
                                   start=(dc == 0), stop=False),
                          waits=(w if tt == 0 else ()))
            for tt in range(2):
                v_mm = kb.op("tensor",
                             lambda e, dmc=dmc, tt=tt, pb=pb:
                             e.matmul(banks[pb + tt][:, :],
                                      BMATB[:, dmc * 128:(dmc + 1) * 128],
                                      ZPMT[:, tt * 512:(tt + 1) * 512],
                                      start=False, stop=True),
                             waits=([(s_dve, v_zpm)] if dmc == 0 and tt == 0 else ()),
                             inc=s_pe)
            w2_free[wb] = v_mm
            hb = dmc % 2
            for tt in range(2):
                p2bank_free[pb + tt] = kb.op(
                    "vector",
                    lambda e, dmc=dmc, tt=tt, pb=pb, hb=hb:
                    e.tensor_scalar(HDS[hb][:, tt * 512:(tt + 1) * 512],
                                    banks[pb + tt][:, :],
                                    BFFN[:, dmc:dmc + 1], None, OP.add),
                    waits=[(s_pe, v_mm), (s_hdsp2[hb], hds_free[hb])])
            hd_out = _ap(hdsp.tensor, dmc * 128 * 128,
                         [[128, 128], [D * 128, 8], [1, 128]])
            hds_free[hb] = kb.dma("sync", hd_out, HDS[hb][:, :], s_hdsp2[hb],
                                  waits=[(s_dve, p2bank_free[pb + 1])])
            v_hd_spills = hds_free[hb]

        tp_hd = dtap("hd", [8 * D * 128], BF16)
        if tp_hd is not None:
            kb.dma("sync", tp_hd,
                   _ap(hdsp.tensor, 0, [[8 * D * 128, 1], [1, 8 * D * 128]]), s_sp,
                   waits=[(s_hdsp2[0], hds_free[0]), (s_hdsp2[1], hds_free[1])])

        scope_p2.close()
        kb.scope_barrier([s_dve, s_act, s_pe, s_gd, s_sp])

        scope_p3a = contextlib.ExitStack()
        _p3 = lambda name, p, f, dt: scope_p3a.enter_context(
            nc.sbuf_tensor(name, [p, f], dt))
        W1T = [_p3(f"W1T{i}", 128, 32 * 128, BF16) for i in range(2)]
        CIN = [_p3(f"CIN{i}", 128, 1027, BF16) for i in range(2)]
        SPST = [_p3(f"SPST{i}", 128, T, BF16) for i in range(2)]
        DTU = _p3("DTU", 128, T, F32)
        DA = _p3("DA", 128, T, F32)
        LNDT = _p3("LNDT", 128, T, F32)
        LC = _p3("LC", 128, T, F32)

        kb.op("vector", lambda e: e.memset(ONES128[:, :], 1.0))
        kb.op("vector", lambda e: e.memset(CIN[0][:, 0:3], 0.0))
        v_cin0 = kb.op("vector", lambda e: e.memset(CIN[1][:, 0:3], 0.0))

        w1_free = [0, 0]
        spst_free = [0, 0]
        p3bank_free = [None, None, None, None]
        cin_free = [0, 0]
        fc_order = [144] + list(range(64, 144)) + list(range(0, 64))
        v_dt_pipe = 0
        v_lct = 0
        v_ecum = 0
        v_halos = 0
        v_xbc_spill = 0
        v_gate_spill = 0
        v_q_done = 0

        for idx, fc in enumerate(fc_order):
            wb = idx % 2
            v_w = kb.dma("gpsimd", W1T[wb][:, :],
                         _ap(w1.tensor, fc * 128 * 4096, [[4096, 128], [1, 4096]]),
                         s_w2[wb], waits=[(s_pe, w1_free[wb])])
            pb = (idx % 2) * 2
            for dc in range(32):
                w = []
                if dc == 0:
                    w = [(s_w2[wb], v_w)]
                    for bf in (p3bank_free[pb], p3bank_free[pb + 1]):
                        if bf is not None:
                            w.append(bf)
                for tt in range(2):
                    v_mm = kb.op("tensor",
                                 lambda e, wb=wb, dc=dc, tt=tt, pb=pb:
                                 e.matmul(banks[pb + tt][:, :],
                                          W1T[wb][:, dc * 128:(dc + 1) * 128],
                                          XN[:, dc * T + tt * 512: dc * T + (tt + 1) * 512],
                                          start=(dc == 0), stop=(dc == 31)),
                                 waits=(w if tt == 0 else ()),
                                 inc=(s_pe if dc == 31 and tt == 1 else None))
            w1_free[wb] = v_mm

            if fc == 144:
                for tt in range(2):
                    v_u = kb.op("vector", lambda e, tt=tt, pb=pb:
                                e.tensor_scalar(DTU[:, tt * 512:(tt + 1) * 512],
                                                banks[pb + tt][:, :], DTB[:, 0:1], None,
                                                OP.add),
                                waits=[(s_pe, v_mm), (s_ld, v_consts)])
                p3bank_free[pb] = p3bank_free[pb + 1] = (s_dve, v_u)
                v_abs = kb.op("scalar", lambda e: e.activation(SCR2[:, :], DTU[:, :],
                                                               AF.Abs),
                              waits=[(s_dve, v_u)])
                v_e = kb.op("scalar", lambda e: e.activation(SCR2[:, :], SCR2[:, :],
                                                             AF.Exp, scale=-1.0))
                v_l = kb.op("scalar", lambda e: e.activation(SCR2[:, :], SCR2[:, :],
                                                             AF.Ln, bias=1.0))
                v_r = kb.op("vector", lambda e: e.tensor_scalar_max(DA[:, :], DTU[:, :],
                                                                    0.0))
                kb.wait("vector", s_act, v_l)
                v_dt = kb.op("vector", lambda e: e.tensor_add(DT[:, :], DA[:, :],
                                                              SCR2[:, :]))
                v_da = kb.op("vector", lambda e: e.tensor_scalar_mul(DA[:, :], DT[:, :],
                                                                     ANEG[:, 0:1]),
                             waits=[(s_act, v_aneg)])
                for c in range(NCHUNK):
                    v_cum = kb.op("vector", lambda e, c=c:
                                  e.tensor_tensor_scan(CUM[:, c * CS:(c + 1) * CS],
                                                       ONES128[:, :],
                                                       DA[:, c * CS:(c + 1) * CS],
                                                       0.0, OP.mult, OP.add))
                kb.wait("scalar", s_dve, v_dt)
                v_lndt = kb.op("scalar", lambda e: e.activation(LNDT[:, :], DT[:, :],
                                                                AF.Ln))
                v_ecum = kb.op("scalar", lambda e, v=v_cum: e.activation(
                    ECUM[:, :], CUM[:, :], AF.Exp), waits=[(s_dve, v_cum)])
                kb.wait("vector", s_act, v_lndt)
                v_lc = kb.op("vector", lambda e: e.tensor_sub(LC[:, :], LNDT[:, :],
                                                              CUM[:, :]))
                for g8 in range(8):
                    tb = 6 + (g8 % 2)
                    kb.op("tensor", lambda e, g8=g8, tb=tb:
                          e.matmul(banks[tb][:, 0:128], LC[:, g8 * 128:(g8 + 1) * 128],
                                   IDN[:, :], is_transpose=True, start=True, stop=True),
                          waits=[(s_dve, v_lc if g8 < 2 else 0),
                                 (s_dve, bank_free[tb])],
                          inc=s_pe)
                    v_tpe = s_pe.now
                    bank_free[tb] = kb.op("vector", lambda e, g8=g8, tb=tb:
                                          e.tensor_copy(LCT[:, g8 * 128:(g8 + 1) * 128],
                                                        banks[tb][:, 0:128]),
                                          waits=[(s_pe, v_tpe)])
                v_lct = bank_free[tb]
                kb.op("vector", lambda e: e.tensor_copy(Q1[:, :], CUM[:, :]))
                kb.op("vector", lambda e: e.tensor_tensor(SCR2[:, :], CUM[:, :], Q1[:, :],
                                                          OP.subtract))
                kb.op("vector", lambda e: e.tensor_copy(Q2[:, :], SCR2[:, :]))
                kb.op("vector", lambda e: e.tensor_tensor(SCR2[:, :], SCR2[:, :], Q2[:, :],
                                                          OP.subtract))
                v_q_done = kb.op("vector", lambda e: e.tensor_copy(R2[:, :], SCR2[:, :]))
                v_dt_pipe = v_q_done
            elif fc >= 64:
                cc = fc - 64
                cb = idx % 2
                for tt in range(2):
                    v_cc = kb.op("scalar", lambda e, tt=tt, pb=pb, cb=cb:
                                 e.copy(CIN[cb][:, 3 + tt * 512: 3 + (tt + 1) * 512],
                                        banks[pb + tt][:, :]),
                                 waits=([(s_pe, v_mm), (s_dve, cin_free[cb])]
                                        if tt == 0 else ()))
                p3bank_free[pb] = p3bank_free[pb + 1] = (s_act, v_cc)
                v_t0 = kb.op("vector", lambda e, cb=cb, cc=cc:
                             e.tensor_scalar_mul(ACC[:, :], CIN[cb][:, 0:T],
                                                 CONVW[:, cc * KC:cc * KC + 1]),
                             waits=[(s_act, v_cc)])
                for k in range(1, 4):
                    v_tk = kb.op("vector", lambda e, cb=cb, cc=cc, k=k:
                                 e.scalar_tensor_tensor(ACC[:, :], CIN[cb][:, k:k + T],
                                                        CONVW[:, cc * KC + k:cc * KC + k + 1],
                                                        ACC[:, :], OP.mult, OP.add))
                cin_free[cb] = v_tk
                kb.op("vector", lambda e, cc=cc:
                      e.tensor_copy(CR3[:, cc * 3:(cc + 1) * 3], ACC[:, 0:3]))
                v_halo = kb.op("vector", lambda e, cc=cc, cb=cb:
                               e.tensor_copy(HALOS[:, cc * 3:(cc + 1) * 3],
                                             CIN[cb][:, 1024:1027]))
                if cc == NXBC - 1:
                    v_halos = v_halo
                sbuf_i = idx % 2
                v_silu = kb.op("scalar", lambda e, cc=cc, sbuf_i=sbuf_i:
                               e.activation(SPST[sbuf_i][:, :], ACC[:, :], AF.Silu,
                                            bias=CONVB[:, cc:cc + 1]),
                               waits=[(s_dve, v_halo),
                                      (s_spst2[sbuf_i], spst_free[sbuf_i])])
                spst_free[sbuf_i] = kb.dma("sync",
                                           _ap(xbcsp.tensor, cc * 128 * T,
                                               [[T, 128], [1, T]]),
                                           SPST[sbuf_i][:, :], s_spst2[sbuf_i],
                                           waits=[(s_act, v_silu)])
                v_xbc_spill = spst_free[sbuf_i]
            else:
                sbuf_i = idx % 2
                for tt in range(2):
                    v_gn = kb.op("scalar", lambda e, tt=tt, pb=pb, sbuf_i=sbuf_i:
                                 e.copy(SPST[sbuf_i][:, tt * 512:(tt + 1) * 512],
                                        banks[pb + tt][:, :]),
                                 waits=([(s_pe, v_mm),
                                         (s_spst2[sbuf_i], spst_free[sbuf_i])]
                                        if tt == 0 else ()))
                p3bank_free[pb] = p3bank_free[pb + 1] = (s_act, v_gn)
                spst_free[sbuf_i] = kb.dma("sync",
                                           _ap(gatesp.tensor, fc * 128 * T,
                                               [[T, 128], [1, T]]),
                                           SPST[sbuf_i][:, :], s_spst2[sbuf_i],
                                           waits=[(s_act, v_gn)])
                v_gate_spill = spst_free[sbuf_i]

        tp_xbc = dtap("xbcsp", [NXBC * 128 * T], BF16)
        if tp_xbc is not None:
            kb.dma("sync", tp_xbc,
                   _ap(xbcsp.tensor, 0, [[NXBC * 128 * T, 1], [1, NXBC * 128 * T]]),
                   s_sp, waits=[(s_spst2[0], spst_free[0]), (s_spst2[1], spst_free[1])])
        tp_dt = dtap("dt", [128, T])
        if tp_dt is not None:
            kb.dma("sync", tp_dt, DT[:, :], s_sp, waits=[(s_dve, v_dt_pipe)])
        tp_cum = dtap("cum", [128, T])
        if tp_cum is not None:
            kb.dma("sync", tp_cum, CUM[:, :], s_sp, waits=[(s_dve, v_dt_pipe)])

        v_hs = kb.op("vector", lambda e: e.tensor_scalar_mul(HALOS[:, :], HALOS[:, :],
                                                             ISH0[:, 0:1]),
                     waits=[(s_dve, v_halos), (s_ld, v_consts)])
        v_c2aw = kb.dma("gpsimd", cc2a_in, HALOS[:, :], s_gd, waits=[(s_dve, v_hs)])
        kb.op("gpsimd", lambda e: e.collective_compute(
            "AllReduce", OP.add, replica_groups=[[0, 1], [2, 3], [4, 5], [6, 7]],
            ins=[cc2a_in.opt()], outs=[cc2a_out.opt()]),
            waits=[(s_gd, v_c2aw)], inc=s_cc2a, chain=False)
        kb.wait("gpsimd", s_cc2a, 1)
        v_hb = kb.dma("gpsimd", HALOS[:, :], cc2a_out, s_gd)
        v_f = kb.op("vector", lambda e: e.tensor_scalar_mul(HALOS[:, :], HALOS[:, :],
                                                            ISH1[:, 0:1]),
                    waits=[(s_gd, v_hb)])
        for t in range(3):
            for k in range(0, 3 - t):
                kb.op("vector", lambda e, t=t, k=k: e.tensor_tensor(
                    _ap(SCR2, 0, [[T, 128], [3, NXBC]]),
                    _ap(HALOS, t + k, [[NXBC * 3, 128], [3, NXBC]]),
                    _ap(CONVW, k, [[NXBC * KC, 128], [KC, NXBC]]),
                    OP.mult))
                kb.op("vector", lambda e, t=t: e.tensor_tensor(
                    _ap(CR3, t, [[NXBC * 3, 128], [3, NXBC]]),
                    _ap(CR3, t, [[NXBC * 3, 128], [3, NXBC]]),
                    _ap(SCR2, 0, [[T, 128], [3, NXBC]]),
                    OP.add))
        v_fb = kb.op("vector", lambda e: e.tensor_tensor(
            CR3[:, :], CR3[:, :],
            _ap(CONVB, 0, [[NXBC, 128], [1, NXBC], [0, 3]]), OP.add))
        v_fix = kb.op("scalar", lambda e: e.activation(XBCFIX[:, :], CR3[:, :], AF.Silu),
                      waits=[(s_dve, v_fb)])

        tp_fix = dtap("xbcfix", [128, NXBC * 3], BF16)
        if tp_fix is not None:
            kb.dma("sync", tp_fix, XBCFIX[:, :], s_sp, waits=[(s_act, v_fix)])

        scope_p3a.close()
        scope_xn.close()
        kb.scope_barrier([s_dve, s_act, s_pe, s_gd, s_sp, s_cc])

        slsp = kb.dint("slsp", [NCHUNK, 128, INTER], BF16)
        scope_p3b = contextlib.ExitStack()
        _pb = lambda name, p, f, dt: scope_p3b.enter_context(
            nc.sbuf_tensor(name, [p, f], dt))
        S = _pb("S", 128, INTER, F32)
        HSB = [_pb(f"HSB{i}", 128, 2048, BF16) for i in range(2)]
        X_TOK = [_pb(f"XTOK{i}", 128, INTER, BF16) for i in range(2)]
        XW = [_pb(f"XW{i}", 128, INTER, BF16) for i in range(2)]
        BFM3 = _pb("BFM3", 128, 2048, BF16)
        B_TOK = [_pb(f"BTOK{i}", 128, 1024, BF16) for i in range(2)]
        SB16 = _pb("SB16", 128, INTER, BF16)
        CDF = _pb("CDF", 1, 4 * 128, F32)
        CDECB = _pb("CDECB", 128, 128, F32)

        v_szero = kb.op("vector", lambda e: e.memset(S[:, :], 0.0))
        hsb_free = [(None, 0), (None, 0)]
        state_bank_free = [None] * 8
        v_cdf = 0
        v_xtok_spill = {}
        v_sl_spill = {}
        v_states_last = 0
        v_xtspill_last = 0
        v_slspill_last = 0

        for c in range(NCHUNK):
            v_g = kb.dma("sync", _ap(CDF, c * 128, [[512, 1], [1, 128]]),
                         _ap(CUM, c * CS + (CS - 1), [[T, 128], [1, 1]]), s_cdf,
                         waits=[(s_dve, v_dt_pipe)])
            v_cdf = kb.op("scalar", lambda e, c=c: e.activation(
                _ap(CDF, c * 128, [[512, 1], [1, 128]]),
                _ap(CDF, c * 128, [[512, 1], [1, 128]]), AF.Exp),
                waits=[(s_cdf, v_g)])

            v_dl = kb.op("scalar", lambda e, c=c: e.activation(
                SCR2[:, 0:CS], CUM[:, c * CS:(c + 1) * CS], AF.Exp, scale=-1.0,
                bias=CUM[:, c * CS + CS - 1:c * CS + CS]),
                waits=[(s_dve, v_dt_pipe)])
            v_w = kb.op("vector", lambda e, c=c: e.tensor_mul(
                SCR2[:, CS:2 * CS], SCR2[:, 0:CS], DT[:, c * CS:(c + 1) * CS]),
                waits=[(s_act, v_dl)])
            for jt in range(2):
                tb = 6 + jt
                v_tp = kb.op("tensor", lambda e, jt=jt, tb=tb:
                             e.matmul(banks[tb][:, 0:128],
                                      SCR2[:, CS + jt * 128:CS + (jt + 1) * 128],
                                      IDN[:, :], is_transpose=True, start=True,
                                      stop=True),
                             waits=[(s_dve, max(v_w, bank_free[tb]))], inc=s_pe)
                bank_free[tb] = kb.op("vector", lambda e, c=c, jt=jt, tb=tb:
                                      e.tensor_copy(
                                          WDTT[:, (2 * c + jt) * 128:(2 * c + jt + 1) * 128],
                                          banks[tb][:, 0:128]),
                                      waits=[(s_pe, v_tp)])

            for blk in range(8):
                hb = blk % 2
                v_h = kb.dma("sync", HSB[hb][:, :],
                             _ap(xbcsp.tensor, blk * 8 * 128 * T + c * CS,
                                 [[T, 128], [128 * T, 8], [1, CS]]), s_hsb2[hb],
                             waits=[hsb_free[hb]] if hsb_free[hb][0] else [])
                v_patch = 0
                if c == 0:
                    v_patch = kb.op("vector", lambda e, blk=blk:
                                    e.tensor_copy(
                                        _ap(HSB[blk % 2], 0, [[2048, 128], [CS, 8], [1, 3]]),
                                        _ap(XBCFIX, blk * 24, [[240, 128], [3, 8], [1, 3]])),
                                    waits=[(s_hsb2[hb], v_h), (s_act, v_fix)])
                for sub in range(8):
                    cc = blk * 8 + sub
                    for jt in range(2):
                        tb = 6 + jt
                        w = [(s_hsb2[hb], v_h), (s_dve, max(bank_free[tb], v_patch))] \
                            if sub == 0 else [(s_dve, bank_free[tb])]
                        v_tp = kb.op("tensor", lambda e, hb=hb, sub=sub, jt=jt, tb=tb:
                                     e.matmul(banksb[tb][:, 0:128],
                                              HSB[hb][:, sub * CS + jt * 128:
                                                      sub * CS + (jt + 1) * 128],
                                              IDNB[:, :], is_transpose=True,
                                              start=True, stop=True),
                                     waits=w, inc=s_pe)
                        w_ev = [(s_pe, v_tp)]
                        if blk == 0 and sub == 0 and v_xtspill_last:
                            w_ev.append((s_xtsp, v_xtspill_last))
                        bank_free[tb] = kb.op(
                            "vector", lambda e, cc=cc, jt=jt, tb=tb:
                            e.tensor_copy(X_TOK[jt][:, cc * 128:(cc + 1) * 128],
                                          banksb[tb][:, 0:128]),
                            waits=w_ev)
                hsb_free[hb] = (s_dve, bank_free[7])
            v_xtok = bank_free[7]
            v_xw = 0
            for jt in range(2):
                v_xw = kb.op("vector", lambda e, c=c, jt=jt: e.tensor_tensor(
                    XW[jt][:, :], X_TOK[jt][:, :],
                    _ap(WDTT, (2 * c + jt) * 128, [[1024, 128], [1, 128], [0, PD]]),
                    OP.mult),
                    waits=([(s_pe, v_states_last)] if jt == 0 else ()))
            for jt in range(2):
                v_xtok_spill[(c, jt)] = kb.dma(
                    "sync", _ap(xtoksp.tensor, (c * 2 + jt) * 128 * INTER,
                                [[INTER, 128], [1, INTER]]),
                    X_TOK[jt][:, :], s_xtsp, waits=[(s_dve, v_xw)])
            v_xtspill_last = v_xtok_spill[(c, 1)]

            if c > 0:
                v_sb = kb.op("vector", lambda e: e.tensor_copy(SB16[:, :], S[:, :]),
                             waits=([(s_slsp, v_slspill_last)] if v_slspill_last else ()))
                v_sl_spill[c] = kb.dma("sync",
                                       _ap(slsp.tensor, c * 128 * INTER,
                                           [[INTER, 128], [1, INTER]]),
                                       SB16[:, :], s_slsp, waits=[(s_dve, v_sb)])
                v_slspill_last = v_sl_spill[c]

            v_b = kb.dma("sync", BFM3[:, :],
                         _ap(xbcsp.tensor, 64 * 128 * T + c * CS,
                             [[T, 128], [128 * T, 8], [1, CS]]), s_bfm,
                         waits=[(s_dve, v_xw)])
            v_bpatch = 0
            if c == 0:
                v_bpatch = kb.op("vector", lambda e: e.tensor_copy(
                    _ap(BFM3, 0, [[2048, 128], [CS, 8], [1, 3]]),
                    _ap(XBCFIX, 64 * 3, [[240, 128], [3, 8], [1, 3]])),
                    waits=[(s_bfm, v_b), (s_act, v_fix)])
            for g in range(G):
                for jt in range(2):
                    tb = 6 + jt
                    w = [(s_bfm, v_b), (s_dve, max(bank_free[tb], v_bpatch))] \
                        if g == 0 else [(s_dve, bank_free[tb])]
                    v_tp = kb.op("tensor", lambda e, g=g, jt=jt, tb=tb:
                                 e.matmul(banksb[tb][:, 0:128],
                                          BFM3[:, g * CS + jt * 128:g * CS + (jt + 1) * 128],
                                          IDNB[:, :], is_transpose=True,
                                          start=True, stop=True),
                                 waits=w, inc=s_pe)
                    bank_free[tb] = kb.op(
                        "vector", lambda e, g=g, jt=jt, tb=tb:
                        e.tensor_copy(B_TOK[jt][:, g * 128:(g + 1) * 128],
                                      banksb[tb][:, 0:128]),
                        waits=[(s_pe, v_tp)])
            v_btok = bank_free[7]

            v_cb = kb.op("tensor", lambda e, c=c: e.matmul(
                banks[5][:, 0:128], ONESF[0:1, :],
                _ap(CDF, c * 128, [[512, 1], [1, 128]]),
                start=True, stop=True),
                waits=[(s_act, v_cdf), (s_dve, max(v_ones, state_bank_free[5][1]
                                                   if state_bank_free[5] else 0))],
                inc=s_pe)
            v_cdecb = kb.op("vector", lambda e: e.tensor_copy(CDECB[:, :],
                                                              banks[5][:, 0:128]),
                            waits=[(s_pe, v_cb)])
            state_bank_free[5] = (s_dve, v_cdecb)

            for g in range(G):
                bka, bkb = (0, 1) if g % 2 == 0 else (2, 3)
                for jt in range(2):
                    for hh in range(16):
                        h = g * 16 + hh
                        bk = bka if hh < 8 else bkb
                        reg = (hh % 8) * 64
                        w = []
                        if jt == 0 and hh == 0:
                            w = [(s_dve, v_btok)]
                            for bkx in (bka, bkb):
                                if state_bank_free[bkx]:
                                    w.append(state_bank_free[bkx])
                        v_smm = kb.op("tensor", lambda e, g=g, jt=jt, h=h, bk=bk, reg=reg,
                                      hh=hh:
                                      e.matmul(banks[bk][:, reg:reg + PD],
                                               B_TOK[jt][:, g * 128:(g + 1) * 128],
                                               XW[jt][:, h * PD:(h + 1) * PD],
                                               start=(jt == 0 and hh % 8 == 0),
                                               stop=(jt == 1 and hh % 8 == 7)),
                                      waits=w,
                                      inc=(s_pe if jt == 1 and hh == 15 else None))
                for hh in range(16):
                    h = g * 16 + hh
                    bk = bka if hh < 8 else bkb
                    reg = (hh % 8) * 64
                    w = [(s_pe, v_smm), (s_dve, v_cdecb)] if hh == 0 else []
                    v_su = kb.op("vector", lambda e, h=h, bk=bk, reg=reg:
                                 e.scalar_tensor_tensor(S[:, h * PD:(h + 1) * PD],
                                                        S[:, h * PD:(h + 1) * PD],
                                                        CDECB[:, h:h + 1],
                                                        banks[bk][:, reg:reg + PD],
                                                        OP.mult, OP.add),
                                 waits=w)
                state_bank_free[bka] = state_bank_free[bkb] = (s_dve, v_su)
                v_states_last = v_smm

        v_ss = kb.op("vector", lambda e: e.tensor_scalar_mul(S[:, :], S[:, :],
                                                             ISH0[:, 0:1]))
        v_c2bw = kb.dma("gpsimd", cc2b_in, S[:, :], s_gd, waits=[(s_dve, v_ss)])
        kb.op("gpsimd", lambda e: e.collective_compute(
            "AllReduce", OP.add, replica_groups=[[0, 1], [2, 3], [4, 5], [6, 7]],
            ins=[cc2b_in.opt()], outs=[cc2b_out.opt()]),
            waits=[(s_gd, v_c2bw)], inc=s_cc2b, chain=False)
        v_c2b = 1

        kb.op("vector", lambda e: e.memset(PBF[0:1, 0:128], 1.0))
        for c in range(1, NCHUNK):
            kb.op("vector", lambda e, c=c: e.tensor_mul(
                PBF[0:1, c * 128:(c + 1) * 128],
                PBF[0:1, (c - 1) * 128:c * 128],
                CDF[0:1, (c - 1) * 128:c * 128]),
                waits=([(s_act, v_cdf)] if c == 1 else []))
        v_pbf = kb.op("vector", lambda e: e.tensor_scalar_mul(PBF[0:1, :], PBF[0:1, :],
                                                              ISH1[0:1, 0:1]))
        scope_p3b.close()
        kb.scope_barrier([s_dve, s_act, s_pe, s_gd, s_sp, s_ld, s_cc])

        scope_p4 = contextlib.ExitStack()
        _p4 = lambda name, p, f, dt: scope_p4.enter_context(
            nc.sbuf_tensor(name, [p, f], dt))
        XT4 = [_p4(f"XT4_{i}", 128, INTER, BF16) for i in range(2)]
        CB4 = _p4("CB4", 128, 2048, BF16)
        BB4 = _p4("BB4", 128, 2048, BF16)
        SLB = _p4("SLB", 128, INTER, BF16)
        S4B = _p4("S4B", 128, INTER, BF16)
        S4S = _p4("S4S", 128, INTER, BF16)
        PBBE = _p4("PBBE", 128, 128, F32)
        GM = _p4("GM", 128, G * 512, BF16)
        FCB3 = [_p4(f"FCB3_{i}", 3, 2048, BF16) for i in range(2)]
        EROWS = [_p4(f"EROWS{i}", 1, 2048, BF16) for i in range(2)]
        EB = [_p4(f"EB{i}", 128, 512, BF16) for i in range(3)]
        SC4 = [_p4(f"SC4_{i}", 128, 512, BF16) for i in range(4)]
        CX4 = [_p4(f"CX4_{i}", 128, 256, BF16) for i in range(4)]
        HS4 = _p4("HS4", 128, 2048, BF16)
        GT4 = _p4("GT4", 128, 2048, BF16)
        SILG = _p4("SILG", 128, 2048, BF16)
        GG4 = _p4("GG4", 128, 2048, F32)
        GSQ4 = [_p4(f"GSQ4_{i}", 128, 256, BF16) for i in range(2)]
        RSR = _p4("RSR", 1, 256, F32)
        RSBE = _p4("RSBE", 128, 256, F32)
        YGST = [_p4(f"YGST{i}", 128, 2048, BF16) for i in range(2)]
        ONECOL = _p4("ONECOL", 128, 1, BF16)
        CROW = _p4("CROW", 1, 4, F32)

        v_onecol = kb.op("vector", lambda e: e.memset(ONECOL[:, :], 1.0))
        v_crow = kb.op("vector", lambda e: e.memset(CROW[0:1, 0:1], 1e-5))
        kb.wait("gpsimd", s_cc2b, 1)
        v_s4 = kb.dma("gpsimd", S4B[:, :], cc2b_out, s_gd)
        tp_s4 = dtap("s4", [128, INTER])
        if tp_s4 is not None:
            kb.dma("sync", tp_s4, cc2b_out, s_sp, waits=[(s_gd, v_s4)])

        if "sc0" in debug:
            dbg["sc0"] = kb.dout("dbg_sc0", [128, 512], BF16)
            dbg["e0"] = kb.dout("dbg_e0", [128, 512], BF16)
            dbg["g0"] = kb.dout("dbg_g0", [128, 512], BF16)
        yraw_tap_on = "yraw" in debug
        yraw_guard = 0
        djbank_exp = [(None, 0), (None, 0)]
        eb_free = [(None, 0)] * 3
        sc_free = [(None, 0)] * 4
        cx_free = [(None, 0)] * 4
        ybank_evac = [(None, 0), (None, 0)]
        ybank_ready = [0, 0]
        fcb_guard = [(None, 0), (None, 0)]
        ebcast_guard = (None, 0)
        gsq_guard = [(None, 0), (None, 0)]
        gm_dve = 0
        pbb_guard = 0
        silg_guard = 0
        hs4_guard = 0
        pool_gg_last = 0
        var_guard = 0
        rsb_guard = 0
        ygst_guard = [(None, 0), (None, 0)]
        v_ebc = 0
        v_yg_spill = 0
        re_guard = []

        for c in range(NCHUNK):
            w_re = re_guard
            v_x0 = kb.dma("sync", XT4[0][:, :],
                          _ap(xtoksp.tensor, (c * 2 + 0) * 128 * INTER,
                              [[INTER, 128], [1, INTER]]), s_x42[0], waits=w_re)
            v_x1 = kb.dma("sync", XT4[1][:, :],
                          _ap(xtoksp.tensor, (c * 2 + 1) * 128 * INTER,
                              [[INTER, 128], [1, INTER]]), s_x42[1], waits=w_re)
            v_bb = kb.dma("sync", BB4[:, :],
                          _ap(xbcsp.tensor, 64 * 128 * T + c * CS,
                              [[T, 128], [128 * T, 8], [1, CS]]), s_bc, waits=w_re)
            v_cb4 = kb.dma("sync", CB4[:, :],
                           _ap(xbcsp.tensor, 72 * 128 * T + c * CS,
                               [[T, 128], [128 * T, 8], [1, CS]]), s_bc, waits=w_re)
            v_slb = 0
            if c > 0:
                v_slb = kb.dma("sync", SLB[:, :],
                               _ap(slsp.tensor, c * 128 * INTER,
                                   [[INTER, 128], [1, INTER]]), s_slbr, waits=w_re)
            v_bpat = v_cpat = 0
            if c == 0:
                v_bpat = kb.op("vector", lambda e: e.tensor_copy(
                    _ap(BB4, 0, [[2048, 128], [CS, 8], [1, 3]]),
                    _ap(XBCFIX, 64 * 3, [[240, 128], [3, 8], [1, 3]])),
                    waits=[(s_bc, v_cb4), (s_act, v_fix)])
                v_cpat = kb.op("vector", lambda e: e.tensor_copy(
                    _ap(CB4, 0, [[2048, 128], [CS, 8], [1, 3]]),
                    _ap(XBCFIX, 72 * 3, [[240, 128], [3, 8], [1, 3]])),
                    waits=[(s_bc, v_cb4), (s_act, v_fix)])

            for g in range(G):
                gb = 6 + (g % 2)
                for jt in range(2):
                    w = []
                    if jt == 0:
                        w = [(s_bc, v_cb4), (s_dve, max(v_bpat, v_cpat))]
                        if g >= 2:
                            w.append((s_dve, gm_dve))
                        if c > 0 and g < 2:
                            w = w + re_guard
                    v_gmm = kb.op("tensor", lambda e, g=g, jt=jt, gb=gb:
                                  e.matmul(banks[gb][:, jt * CS:(jt + 1) * CS],
                                           BB4[:, g * CS + jt * 128:g * CS + (jt + 1) * 128],
                                           CB4[:, g * CS:(g + 1) * CS],
                                           start=(jt == 0), stop=(jt == 1)),
                                  waits=w, inc=(s_pe if jt == 1 else None))
                gm_dve = kb.op("vector", lambda e, g=g, gb=gb:
                               e.tensor_copy(GM[:, g * 512:(g + 1) * 512],
                                             banks[gb][:, :]),
                               waits=[(s_pe, v_gmm)])
            v_gm = gm_dve

            v_pbb = kb.op("tensor", lambda e, c=c: e.matmul(
                banks[5][:, 0:128], ONESF[0:1, :],
                _ap(PBF, c * 128, [[512, 1], [1, 128]]),
                start=True, stop=True),
                waits=[(s_dve, max(v_pbf, pbb_guard))], inc=s_pe)
            pbb_guard = kb.op("vector", lambda e: e.tensor_copy(PBBE[:, :],
                                                                banks[5][:, 0:128]),
                              waits=[(s_pe, v_pbb)])
            v_s4s = kb.op("vector", lambda e: e.tensor_tensor(
                S4S[:, :], S4B[:, :],
                _ap(PBBE, 0, [[128, 128], [1, 128], [0, PD]]), OP.mult),
                waits=[(s_dve, pbb_guard), (s_gd, v_s4)] + re_guard)

            for g in range(G):
                v_gt = kb.dma("scalar", GT4[:, :],
                              _ap(gatesp.tensor, g * 8 * 128 * T + c * CS,
                                  [[T, 128], [128 * T, 8], [1, CS]]), s_gt4r,
                              waits=[(s_act, silg_guard)] if c + g > 0 else [])
                v_hs4 = kb.dma("scalar", HS4[:, :],
                               _ap(xbcsp.tensor, g * 8 * 128 * T + c * CS,
                                   [[T, 128], [128 * T, 8], [1, CS]]), s_hs4r,
                               waits=[(s_dve, hs4_guard)] if c + g > 0 else [])
                v_hpat = 0
                if c == 0:
                    v_hpat = kb.op("vector", lambda e, g=g: e.tensor_copy(
                        _ap(HS4, 0, [[2048, 128], [CS, 8], [1, 3]]),
                        _ap(XBCFIX, g * 24, [[240, 128], [3, 8], [1, 3]])),
                        waits=[(s_hs4r, v_hs4), (s_act, v_fix)])
                silg_guard = kb.op("scalar", lambda e: e.activation(
                    SILG[:, :], GT4[:, :], AF.Silu),
                    waits=[(s_gt4r, v_gt), (s_pool, pool_gg_last)])

                for hb2 in range(2):
                    hb = g * 2 + hb2
                    fb = hb % 2
                    wfl = [fcb_guard[fb]] if fcb_guard[fb][0] else []
                    v_f1 = kb.dma("scalar", _ap(FCB3[fb], 0, [[2048, 1], [1, 2048]]),
                                  _ap(Q1, hb * 8 * T + c * CS, [[T, 8], [1, CS]]),
                                  s_fcb2[fb], waits=wfl)
                    v_f2 = kb.dma("scalar", _ap(FCB3[fb], 2048, [[2048, 1], [1, 2048]]),
                                  _ap(Q2, hb * 8 * T + c * CS, [[T, 8], [1, CS]]),
                                  s_fcb2[fb], waits=wfl)
                    v_f3 = kb.dma("scalar", _ap(FCB3[fb], 4096, [[2048, 1], [1, 2048]]),
                                  _ap(R2, hb * 8 * T + c * CS, [[T, 8], [1, CS]]),
                                  s_fcb2[fb], waits=wfl)
                    v_f4 = kb.dma("scalar", EROWS[fb][0:1, :],
                                  _ap(ECUM, hb * 8 * T + c * CS, [[T, 8], [1, CS]]),
                                  s_fcb2[fb], waits=wfl)
                    v_fl = v_f4
                    last_dmm = 0
                    for hh in range(8):
                        h = hb * 8 + hh
                        dbk = h % 2
                        ebuf = h % 3
                        sbuf4 = h % 4
                        cbuf = h % 4
                        w = [(s_fcb2[fb], v_fl)]
                        if djbank_exp[dbk][0]:
                            w.append(djbank_exp[dbk])
                        kb.op("tensor", lambda e, dbk=dbk:
                              e.matmul(banks[dbk][:, :], IDNB[:, :], MASKP[:, :],
                                       start=True, stop=False),
                              waits=w)
                        last_dmm = kb.op("tensor", lambda e, dbk=dbk, hh=hh, fb=fb:
                                         e.matmul(banks[dbk][:, :], ONESB[0:3, :],
                                                  _ap(FCB3[fb], hh * CS,
                                                      [[2048, 3], [0, 2], [1, CS]]),
                                                  start=False, stop=True),
                                         waits=[], inc=s_pe)
                        w_e = [(s_pe, last_dmm)]
                        if eb_free[ebuf][0]:
                            w_e.append(eb_free[ebuf])
                        for jt in range(2):
                            v_exp = kb.op("scalar", lambda e, dbk=dbk, ebuf=ebuf, jt=jt,
                                          c=c, h=h:
                                          e.activation(EB[ebuf][:, jt * CS:(jt + 1) * CS],
                                                       banks[dbk][:, jt * CS:(jt + 1) * CS],
                                                       AF.Exp,
                                                       bias=LCT[:, (2 * c + jt) * 128 + h:
                                                                (2 * c + jt) * 128 + h + 1]),
                                          waits=(w_e if jt == 0 else []))
                        djbank_exp[dbk] = (s_act, v_exp)
                        w_s = [(s_act, v_exp), (s_dve, v_gm)]
                        if sc_free[sbuf4][0]:
                            w_s.append(sc_free[sbuf4])
                        eng_s = "vector" if h % 2 == 0 else "gpsimd"
                        v_sc = kb.op(eng_s, lambda e, ebuf=ebuf, sbuf4=sbuf4, g=g:
                                     e.tensor_tensor(SC4[sbuf4][:, :], EB[ebuf][:, :],
                                                     GM[:, g * 512:(g + 1) * 512],
                                                     OP.mult),
                                     waits=w_s)
                        sc_ready = (s_dve if h % 2 == 0 else s_pool, v_sc)
                        if c == 0 and h == 0 and "sc0" in debug:
                            kb.dma("sync", dbg["sc0"], SC4[sbuf4][:, :], s_sp,
                                   waits=[sc_ready])
                            kb.dma("sync", dbg["e0"], EB[ebuf][:, :], s_sp,
                                   waits=[sc_ready])
                            kb.dma("sync", dbg["g0"], GM[:, 0:512], s_sp,
                                   waits=[(s_dve, v_gm)])
                        eb_free[ebuf] = sc_ready
                        if hh % 2 == 0:
                            w_eb = [(s_fcb2[fb], v_fl)]
                            if ebcast_guard[0]:
                                w_eb.append(ebcast_guard)
                            v_ebc = kb.op("tensor", lambda e, hh=hh, fb=fb:
                                          e.matmul(banks[2][:, :], ONESB[0:1, :],
                                                   EROWS[fb][0:1, hh * CS:(hh + 2) * CS],
                                                   start=True, stop=True),
                                          waits=w_eb, inc=s_pe)
                        w_c = [(s_pe, v_ebc), (s_bc, v_cb4)]
                        if cx_free[cbuf][0]:
                            w_c.append(cx_free[cbuf])
                        if c == 0:
                            w_c.append((s_dve, v_cpat))
                        v_cx = kb.op("vector", lambda e, cbuf=cbuf, g=g, hh=hh:
                                     e.tensor_tensor(CX4[cbuf][:, :],
                                                     CB4[:, g * CS:(g + 1) * CS],
                                                     banks[2][:, (hh % 2) * CS:
                                                              (hh % 2 + 1) * CS],
                                                     OP.mult),
                                     waits=w_c)
                        if hh % 2 == 1:
                            ebcast_guard = (s_dve, v_cx)
                        ybk = 3 + (h // 4) % 2
                        preg = (h % 2) * 64
                        creg = ((h % 4) // 2) * CS
                        w_y = [sc_ready, (s_x42[0], v_x0), (s_x42[1], v_x1)]
                        if h % 4 < 2 and ybank_evac[ybk - 3][0]:
                            w_y.append(ybank_evac[ybk - 3])
                        kb.op("tensor", lambda e, h=h, ybk=ybk, preg=preg, creg=creg,
                              sbuf4=sbuf4:
                              e.matmul(banks[ybk][preg:preg + PD, creg:creg + CS],
                                       XT4[0][:, h * PD:(h + 1) * PD],
                                       SC4[sbuf4][:, 0:CS],
                                       start=(h % 4 < 2), stop=False,
                                       skip_group_check=True),
                              waits=w_y)
                        kb.op("tensor", lambda e, h=h, ybk=ybk, preg=preg, creg=creg,
                              sbuf4=sbuf4:
                              e.matmul(banks[ybk][preg:preg + PD, creg:creg + CS],
                                       XT4[1][:, h * PD:(h + 1) * PD],
                                       SC4[sbuf4][:, CS:2 * CS],
                                       start=False, stop=False,
                                       skip_group_check=True))
                        w_yi = [(s_dve, v_cx)]
                        if c > 0:
                            kb.op("tensor", lambda e, h=h, ybk=ybk, preg=preg,
                                  creg=creg, cbuf=cbuf:
                                  e.matmul(banks[ybk][preg:preg + PD, creg:creg + CS],
                                           SLB[:, h * PD:(h + 1) * PD],
                                           CX4[cbuf][:, :],
                                           start=False, stop=False,
                                           skip_group_check=True),
                                  waits=[(s_slbr, v_slb), (s_dve, v_cx)])
                        v_ymm = kb.op("tensor", lambda e, h=h, ybk=ybk, preg=preg,
                                      creg=creg, cbuf=cbuf:
                                      e.matmul(banks[ybk][preg:preg + PD, creg:creg + CS],
                                               S4S[:, h * PD:(h + 1) * PD],
                                               CX4[cbuf][:, :],
                                               start=False, stop=(h % 4 >= 2),
                                               skip_group_check=True),
                                      waits=w_yi + [(s_dve, v_s4s)],
                                      inc=(s_pe if h % 4 == 3 else None))
                        if h % 4 == 3:
                            for _i in range(4):
                                sc_free[_i] = (s_pe, v_ymm)
                                cx_free[_i] = (s_pe, v_ymm)
                            ybank_ready[ybk - 3] = v_ymm
                    fcb_guard[fb] = (s_pe, s_pe.n)
                    for cc8 in range(hb2 * 4, hb2 * 4 + 4):
                        cc = g * 8 + cc8
                        ybk = 3 + ((cc % 4) // 2)
                        creg = (cc % 2) * CS
                        w_e2 = [(s_pe, ybank_ready[ybk - 3]), (s_hs4r, v_hs4),
                                (s_pool, pool_gg_last)]
                        if c == 0:
                            w_e2.append((s_dve, v_hpat))
                        if yraw_tap_on and yraw_guard:
                            w_e2 = w_e2 + [(s_sp, yraw_guard)]
                        v_gpre = kb.op("vector",
                                       lambda e, cc=cc, cc8=cc8, ybk=ybk, creg=creg:
                                       e.scalar_tensor_tensor(
                                           SCR2[:, 0:CS],
                                           HS4[:, cc8 * CS:(cc8 + 1) * CS],
                                           DGRP[:, cc:cc + 1],
                                           banks[ybk][:, creg:creg + CS],
                                           OP.mult, OP.add),
                                       waits=w_e2)
                        if yraw_tap_on:
                            yraw_guard = kb.dma("sync",
                                                _ap(yrawsp.tensor,
                                                    cc * 128 * T + c * CS,
                                                    [[T, 128], [1, CS]]),
                                                SCR2[:, 0:CS], s_sp,
                                                waits=[(s_dve, v_gpre)])
                        v_gg = kb.op("gpsimd", lambda e, cc8=cc8:
                                     e.tensor_tensor(GG4[:, cc8 * CS:(cc8 + 1) * CS],
                                                     SCR2[:, 0:CS],
                                                     SILG[:, cc8 * CS:(cc8 + 1) * CS],
                                                     OP.mult),
                                     waits=[(s_dve, v_gpre), (s_act, silg_guard)])
                        pool_gg_last = v_gg
                        gq = cc8 % 2
                        w_sq = [(s_pool, v_gg)]
                        if gsq_guard[gq][0]:
                            w_sq.append(gsq_guard[gq])
                        v_gsq = kb.op("scalar", lambda e, cc8=cc8, gq=gq:
                                      e.activation(GSQ4[gq][:, :],
                                                   GG4[:, cc8 * CS:(cc8 + 1) * CS],
                                                   AF.Square),
                                      waits=w_sq)
                        w_v = [(s_act, v_gsq)]
                        if cc8 == 0:
                            w_v.append((s_dve, max(v_onecol, var_guard, v_gm)))
                        v_var = kb.op("tensor", lambda e, cc8=cc8, gq=gq:
                                      e.matmul(banks[6][0:1, 0:CS], ONECOL[:, :],
                                               GSQ4[gq][:, :],
                                               start=(cc8 == 0), stop=(cc8 == 7)),
                                      waits=w_v, inc=s_pe)
                        gsq_guard[gq] = (s_pe, v_var)
                    ybank_evac[0] = ybank_evac[1] = (s_dve, v_gpre)
                v_rs1 = kb.op("scalar", lambda e: e.activation(
                    RSR[0:1, :], banks[6][0:1, 0:CS], AF.Sqrt, scale=1.0 / GS,
                    bias=CROW[0:1, 0:1]),
                    waits=[(s_pe, v_var), (s_dve, v_crow)])
                var_guard = kb.op("vector", lambda e: e.reciprocal(RSR[0:1, :],
                                                                   RSR[0:1, :]),
                                  waits=[(s_act, v_rs1)])
                hs4_guard = var_guard
                v_rsb = kb.op("tensor", lambda e: e.matmul(
                    banks[7][:, 0:CS], ONESF[0:1, :], RSR[0:1, :],
                    start=True, stop=True),
                    waits=[(s_dve, max(var_guard, rsb_guard))], inc=s_pe)
                v_rsbe = kb.op("vector", lambda e: e.tensor_copy(RSBE[:, :],
                                                                 banks[7][:, 0:CS]),
                               waits=[(s_pe, v_rsb)])
                rsb_guard = v_rsbe
                rsbe_guard = v_rsbe
                yb2 = g % 2
                w_y7 = [(s_dve, v_rsbe)]
                if ygst_guard[yb2][0]:
                    w_y7.append(ygst_guard[yb2])
                for cc8 in range(8):
                    cc = g * 8 + cc8
                    v_yg = kb.op("vector", lambda e, cc=cc, cc8=cc8, yb2=yb2:
                                 e.scalar_tensor_tensor(YGST[yb2][:, cc8 * CS:(cc8 + 1) * CS],
                                                        GG4[:, cc8 * CS:(cc8 + 1) * CS],
                                                        NWGRP[:, cc:cc + 1],
                                                        RSBE[:, :], OP.mult, OP.mult),
                                 waits=(w_y7 if cc8 == 0 else []))
                v_yg_spill = kb.dma("sync",
                                    _ap(ygsp.tensor, g * 8 * 128 * T + c * CS,
                                        [[T, 128], [128 * T, 8], [1, CS]]),
                                    YGST[yb2][:, :], s_ygsp, waits=[(s_dve, v_yg)])
                ygst_guard[yb2] = (s_ygsp, v_yg_spill)
            re_guard = [(s_pe, s_pe.n), (s_dve, s_dve.n), (s_pool, s_pool.n)]

        tp_yraw = dtap("yraw", [NGATE * 128 * T])
        if tp_yraw is not None:
            kb.dma("sync", tp_yraw,
                   _ap(yrawsp.tensor, 0, [[NGATE * 128 * T, 1], [1, NGATE * 128 * T]]),
                   s_sp, waits=[(s_sp, yraw_guard)])
        tp_sl = dtap("sl", [3 * 128 * INTER], BF16)
        if tp_sl is not None:
            kb.dma("sync", tp_sl,
                   _ap(slsp.tensor, 128 * INTER, [[3 * 128 * INTER, 1], [1, 3 * 128 * INTER]]),
                   s_sp, waits=[(s_sp, s_sp.n)])
        tp_yg = dtap("yg", [NGATE * 128 * T], BF16)
        if tp_yg is not None:
            kb.dma("sync", tp_yg,
                   _ap(ygsp.tensor, 0, [[NGATE * 128 * T, 1], [1, NGATE * 128 * T]]),
                   s_sp, waits=[(s_sp, v_yg_spill)])

        scope_p4.close()
        scope_dt.close()
        kb.scope_barrier([s_dve, s_act, s_pe, s_gd, s_sp, s_ld, s_pool])

        scope_p5 = contextlib.ExitStack()
        _p5 = lambda name, p, f, dt: scope_p5.enter_context(
            nc.sbuf_tensor(name, [p, f], dt))
        YG5 = _p5("YG5", 128, NGATE * T, BF16)
        OW5 = [_p5(f"OW5_{i}", 128, NGATE * 128, BF16) for i in range(2)]
        HDF = [_p5(f"HDF{i}", 128, T, BF16) for i in range(2)]
        G2 = _p5("G2", 128, T, F32)
        XRW = [_p5("XRW0", 128, T, BF16)] * 2
        OST = [_p5("OST0", 128, T, F32)] * 2

        v_yg5 = 0
        for gg in range(8):
            v_yg5 = kb.dma("sync", YG5[:, gg * 8 * T:(gg + 1) * 8 * T],
                           _ap(ygsp.tensor, gg * 8 * 128 * T,
                               [[T, 128], [128 * T, 8], [1, T]]), s_yg5)
        ow_free = [0, 0]
        hdf_free = [0, 0]
        g2_guard = 0
        xrw_free = [0, 0]
        ost_free = [0, 0]
        p5bank_free = [0, 0, 0, 0]
        v_out_last = 0
        for dmc in range(32):
            wb = dmc % 2
            v_w = kb.dma("gpsimd", OW5[wb][:, :],
                         _ap(how.tensor, dmc * 128 * NGATE * 128,
                             [[NGATE * 128, 128], [1, NGATE * 128]]), s_w2[wb],
                         waits=[(s_pe, ow_free[wb])])
            v_hdf = kb.dma("sync", HDF[wb][:, :],
                           _ap(hdsp.tensor, dmc * 128 * 128,
                               [[128, 128], [D * 128, 8], [1, 128]]), s_hdsp2[wb],
                           waits=[(s_act, hdf_free[wb])])
            v_xrw = kb.dma("sync", XRW[wb][:, :],
                           _ap(xrawsp.tensor, dmc * 128 * T, [[T, 128], [1, T]]),
                           s_xrw, waits=[(s_dve, xrw_free[wb])])
            pb = (dmc % 2) * 2
            for fc in range(NGATE):
                w = []
                if fc == 0:
                    w = [(s_w2[wb], v_w), (s_yg5, v_yg5),
                         (s_dve, max(p5bank_free[pb], p5bank_free[pb + 1]))]
                for tt in range(2):
                    v_mm = kb.op("tensor",
                                 lambda e, wb=wb, fc=fc, tt=tt, pb=pb:
                                 e.matmul(banks[pb + tt][:, :],
                                          OW5[wb][:, fc * 128:(fc + 1) * 128],
                                          YG5[:, fc * T + tt * 512: fc * T + (tt + 1) * 512],
                                          start=(fc == 0), stop=(fc == NGATE - 1)),
                                 waits=(w if tt == 0 else ()),
                                 inc=(s_pe if fc == NGATE - 1 and tt == 1 else None))
            ow_free[wb] = v_mm
            hdf_free[wb] = kb.op("scalar", lambda e, wb=wb:
                                 e.activation(G2[:, :], HDF[wb][:, :], AF.Silu),
                                 waits=[(s_hdsp2[wb], v_hdf), (s_dve, g2_guard)])
            ob = dmc % 2
            for tt in range(2):
                kb.op("vector", lambda e, tt=tt, pb=pb, ob=ob:
                      e.tensor_tensor(OST[ob][:, tt * 512:(tt + 1) * 512],
                                      banks[pb + tt][:, :], G2[:, tt * 512:(tt + 1) * 512],
                                      OP.mult),
                      waits=([(s_pe, v_mm), (s_act, hdf_free[wb]),
                              (s_out, ost_free[ob])] if tt == 0 else ()))
            p5bank_free[pb] = p5bank_free[pb + 1] = kb.op(
                "vector", lambda e, ob=ob, wb=wb: e.tensor_tensor(
                    OST[ob][:, :], OST[ob][:, :], XRW[wb][:, :], OP.add),
                waits=[(s_xrw, v_xrw)])
            g2_guard = p5bank_free[pb]
            xrw_free[wb] = p5bank_free[pb]
            ost_free[ob] = kb.dma("sync", out_d[dmc * 128:(dmc + 1) * 128, :],
                                  OST[ob][:, :], s_out,
                                  waits=[(s_dve, p5bank_free[pb])])
            v_out_last = ost_free[ob]
        kb.wait("sync", s_out, v_out_last)
        scope_p5.close()

        block = es.enter_context(nc.Block())
        for e in ENGINES:
            fns = kb.ops[e]
            if fns:
                getattr(block, e)(lambda eng, fns=fns: [f(eng) for f in fns])

    return nc, dbg


def _pack_inputs(inputs):
    f32 = np.float32
    ipw = np.ascontiguousarray(inputs["L0_attn_in_proj_w"], dtype=f32)
    ffnw = np.ascontiguousarray(inputs["L0_ffn_W"], dtype=f32)
    outw = np.ascontiguousarray(inputs["L0_attn_out_w"], dtype=f32)
    shared = {
        "w1": np.ascontiguousarray(
            ipw.reshape(FCH, 128, 32, 128).transpose(0, 3, 2, 1)).reshape(FCH, 128, 32 * 128),
        "hw2": np.ascontiguousarray(
            ffnw.reshape(32, 128, 32, 128).transpose(0, 3, 2, 1)).reshape(32, 128, 32 * 128),
        "how": np.ascontiguousarray(
            outw.T.reshape(NGATE, 128, 32, 128).transpose(2, 1, 0, 3)).reshape(
                32, 128, NGATE * 128),
        "affn": np.ascontiguousarray(inputs["L0_ffn_A"], dtype=f32).reshape(32, 128, RANK),
        "w8": np.ascontiguousarray(ffnw[:NEXP].T).reshape(32, 128, NEXP),
        "bmat": np.ascontiguousarray(inputs["L0_ffn_Bmat"], dtype=f32),
        "limes": np.ascontiguousarray(inputs["L0_ffn_LiMEs"], dtype=f32),
        "bffn": np.ascontiguousarray(
            np.asarray(inputs["L0_ffn_b"], dtype=f32).reshape(32, 128).T),
        "convw": np.ascontiguousarray(
            np.asarray(inputs["L0_attn_conv_w"], dtype=f32).reshape(NXBC, 128, KC)
            .transpose(1, 0, 2)).reshape(128, NXBC * KC),
        "convb": np.ascontiguousarray(
            np.asarray(inputs["L0_attn_conv_b"], dtype=f32).reshape(NXBC, 128).T),
        "dtb": np.ascontiguousarray(
            np.asarray(inputs["L0_attn_dt_bias"], dtype=f32).reshape(128, 1)),
        "alog": np.ascontiguousarray(
            np.asarray(inputs["L0_attn_A_log"], dtype=f32).reshape(128, 1)),
        "dgrp": np.ascontiguousarray(
            np.repeat(np.asarray(inputs["L0_attn_D"], dtype=f32).reshape(64, 2).T, 64, axis=0)),
        "nwgrp": np.ascontiguousarray(
            np.asarray(inputs["L0_attn_norm_w"], dtype=f32).reshape(NGATE, 128).T),
        "idn": np.eye(128, dtype=f32),
    }
    j = np.arange(128)[:, None]
    i = np.arange(CS)[None, :]
    m0 = np.where(i >= j, 0.0, -1e6).astype(f32)
    m1 = np.where(i >= j + 128, 0.0, -1e6).astype(f32)
    shared["maskp"] = np.ascontiguousarray(np.concatenate([m0, m1], axis=1))

    x_full = np.asarray(inputs["L0_attn_input_states"], dtype=f32)
    maps = []
    for core in range(8):
        b, h = core // 2, core % 2
        m = dict(shared)
        m["x"] = np.ascontiguousarray(x_full[b, h * T:(h + 1) * T, :])
        m["ish0"] = np.full((128, 1), 1.0 - h, dtype=f32)
        m["ish1"] = np.full((128, 1), float(h), dtype=f32)
        maps.append(m)
    return maps


_NC_CACHE = {}


def _get_nc(debug=()):
    key = tuple(sorted(debug))
    if key not in _NC_CACHE:
        _NC_CACHE[key] = build_program(debug)
    return _NC_CACHE[key]


def kernel(**inputs):
    nc, _ = _get_nc()
    maps = _pack_inputs(inputs)
    res = run_bass_kernel_spmd(nc, maps, core_ids=list(range(8)))
    out = np.empty((B_, SEQ, D), dtype=np.float32)
    for core in range(8):
        b, h = core // 2, core % 2
        out[b, h * T:(h + 1) * T, :] = np.asarray(res.results[core]["out"]).T
    return out
